# version 03
# baseline (speedup 1.0000x reference)
import contextlib
import math
import numpy as np
import ml_dtypes
import concourse.bass as bass
import concourse.mybir as mybir
from concourse.bass_utils import run_bass_kernel_spmd

F32 = mybir.dt.float32
BF16 = mybir.dt.bfloat16
U8 = mybir.dt.uint8
AF = mybir.ActivationFunctionType
ALU = mybir.AluOpType

D = 4096
SB = 4096
TOK = 1024
NCH = 32
NKT = 32
NBLK = 8
NSLOT = 8
NQ = 2
EPS = 1e-5
CSHIFT = 8.0

DEBUG = False
TRUNC = 5


class Op:
    __slots__ = ("stream", "idx", "fn", "deps", "inc", "num", "grp", "gseq")

    def __init__(self, stream, idx, fn):
        self.stream = stream
        self.idx = idx
        self.fn = fn
        self.deps = []
        self.inc = False
        self.num = None
        self.grp = None
        self.gseq = 0


class Builder:
    STREAMS = ("pe", "act", "dve", "pool", "syn")

    def __init__(self):
        self.ops = {s: [] for s in self.STREAMS}
        self.last_write = {}
        self.readers = {}
        self.grps = {}
        self.gseq = 0

    def add(self, stream, fn, reads=(), writes=(), self_dep=False):
        op = Op(stream, len(self.ops[stream]), fn)
        self.gseq += 1
        op.gseq = self.gseq
        deps = []
        for b in reads:
            w = self.last_write.get(b)
            if w is not None:
                deps.append(w)
        for b in writes:
            w = self.last_write.get(b)
            if w is not None:
                deps.append(w)
            deps.extend(self.readers.get(b, ()))
        if stream == "pe" and not self_dep:
            op.deps = [d for d in deps if d.stream != stream]
        else:
            op.deps = deps
        self.ops[stream].append(op)
        for b in writes:
            self.last_write[b] = op
            self.readers[b] = []
        for b in reads:
            self.readers.setdefault(b, []).append(op)
        return op

    def pe(self, fn, reads=(), writes=()):
        return self.add("pe", fn, reads, writes)

    def act(self, fn, reads=(), writes=(), self_dep=False):
        return self.add("act", fn, reads, writes, self_dep=self_dep)

    def dve(self, fn, reads=(), writes=(), self_dep=False):
        return self.add("dve", fn, reads, writes, self_dep=self_dep)

    def dma(self, q, out_ap, in_ap, grp=None, reads=(), writes=(), **kw):
        assert grp is not None, "dma needs a sem group"
        def fn(eng):
            return eng.dma_start(out_ap, in_ap, **kw)
        op = self.add(q, fn, reads, writes)
        op.grp = grp
        self.grps.setdefault(grp, []).append(op)
        return op

    def finalize(self):
        for g, ops in self.grps.items():
            for i, op in enumerate(ops):
                op.num = 16 * (i + 1)
                op.inc = True
        for s in self.STREAMS:
            for op in self.ops[s]:
                merged = {}
                for d in op.deps:
                    key = d.grp if d.grp is not None else d.stream
                    cur = merged.get(key)
                    if cur is None:
                        merged[key] = d
                    elif d.grp is not None:
                        if d.num > cur.num:
                            merged[key] = d
                    elif d.idx > cur.idx:
                        merged[key] = d
                op.deps = list(merged.values())
                for d in op.deps:
                    d.inc = True
        for s in ("pe", "act", "dve"):
            n = 0
            for op in self.ops[s]:
                if op.inc:
                    n += 1
                op.num = n

    def emit_stream(self, s, eng, sems):
        import bisect
        gseqs = {g: [o.gseq for o in ops] for g, ops in self.grps.items()}
        wm = {}
        for op in self.ops[s]:
            for d in op.deps:
                key = d.grp if d.grp is not None else d.stream
                if d.grp is not None:
                    k = bisect.bisect_left(gseqs[d.grp], op.gseq)
                    val = 16 * max(k, (d.num + 15) // 16)
                else:
                    val = d.num
                if val > wm.get(key, 0):
                    eng.wait_ge(sems[key], val)
                    wm[key] = val
            inst = op.fn(eng)
            if op.inc:
                key = op.grp if op.grp is not None else s
                inst.then_inc(sems[key], 16 if op.grp is not None else 1)


def build_program():
    nc = bass.Bass("TRN2", target_bir_lowering=False)
    B = Builder()
    stack = contextlib.ExitStack()

    def din(name, shape, dtype=F32):
        return nc.dram_tensor(name, list(shape), dtype, kind="ExternalInput").ap()

    xT = din("xT", (D, SB))
    wcat = din("wcat", (D, 17))
    b17_d = din("b17_d", (17, 1))
    wqk_ext = din("wqk_ext", (5, 16 * 32))
    wqk_aug = din("wqk_aug", (13, 128))
    e4_d = din("e4_d", (128, 32))
    e4t_d = din("e4t_d", (32, 128))
    ones_dr = din("ones_dr", (1, NSLOT * TOK))
    ones_col = din("ones_col", (128, 1))
    ident_d = din("ident_d", (16, 16))
    wvuT = din("wvuT", (D, 4))
    bvu_d = din("bvu_d", (128, 32))
    wo_d = din("wo_d", (D, D))
    bo_d = din("bo_d", (128, 32))
    wbT_d = din("wbT_d", (D, D))
    encT_d = din("encT_d", (D, 40))
    bb_d = din("bb_d", (128, 32))
    co0 = din("co0", (3, D, 32))
    co1 = din("co1", (3, D, 8))
    mk0 = din("mk0", (3, D, 32), U8)
    mk1 = din("mk1", (3, D, 8), U8)
    mmat = din("mmat", (D, D))

    dbg = "ExternalOutput" if DEBUG else "Internal"
    out = nc.dram_tensor("out", [D, TOK], F32, kind="ExternalOutput").ap()
    c_stage = nc.dram_tensor("c_stage", [32, TOK], F32, kind=dbg).ap()
    ctx_stage = nc.dram_tensor("ctx_stage", [NQ * NSLOT * 20, 512], F32, kind=dbg).ap()
    x1_spill = nc.dram_tensor("x1_spill", [D, TOK], F32, kind=dbg).ap()
    ffn_spill = nc.dram_tensor("ffn_spill", [D, TOK], BF16, kind=dbg).ap()
    attn_dbg = nc.dram_tensor("attn_dbg", [D, TOK], F32, kind=dbg).ap()
    attn_sp = nc.dram_tensor("attn_sp", [D, TOK], BF16, kind="Internal").ap()
    cqkv_dbg = nc.dram_tensor("cqkv_dbg", [17, SB], F32, kind=dbg).ap()
    q_dbg = nc.dram_tensor("q_dbg", [128, TOK], F32, kind=dbg).ap()
    a_dbg = nc.dram_tensor("a_dbg", [128, D], F32, kind=dbg).ap()
    bc_dbg = nc.dram_tensor("bc_dbg", [128, 32], F32, kind=dbg).ap()
    ctxn_dbg = nc.dram_tensor("ctxn_dbg", [128, TOK], F32, kind=dbg).ap()
    h_dbg = nc.dram_tensor("h_dbg", [40, D], F32, kind=dbg).ap()
    u_dbg = nc.dram_tensor("u_dbg", [40, TOK], F32, kind=dbg).ap()
    g_dbg = nc.dram_tensor("g_dbg", [128, NCH * 40], F32, kind=dbg).ap()

    def sbuf(name, shape, dtype):
        return stack.enter_context(nc.sbuf_tensor(name, list(shape), dtype))

    def psum(name, shape, dtype=F32):
        return stack.enter_context(nc.psum_tensor(name, list(shape), dtype))

    proB = sbuf("proB", (128, 4096), F32)
    proB_bf = proB.bitcast(BF16)
    xblk = [proB_bf[:, 0:2048], proB_bf[:, 2048:4096]]
    xsq_t = sbuf("xsq", (128, 2048), BF16)
    xsq = [xsq_t, xsq_t]
    wcat_sb = sbuf("wcat_sb", (128, NCH * 17), BF16)
    b17_sb = sbuf("b17_sb", (17, 1), F32)
    sneg = sbuf("sneg", (17, 1), F32)
    blk_f = sbuf("blk_f", (17, 512), F32)
    blk_fix = sbuf("blk_fix", (17, 512), F32)
    rowpool = sbuf("rowpool", (1, 4608), F32)
    ones17 = sbuf("ones17", (1, 32), F32)
    cqb = proB_bf[:, 4096:5120]
    qsb = proB_bf[:, 5120:6144]
    c_sb = proB[:, 3072:4096]
    phA = sbuf("phA", (128, NKT * 128 + 3 * 2048 + 2 * TOK), BF16)
    kh2 = phA[:, 0:NKT * 128]
    cqm2 = [phA[:, NKT * 128 + 3 * 2048:NKT * 128 + 3 * 2048 + TOK],
            phA[:, NKT * 128 + 3 * 2048 + TOK:]]
    ckvlat_t = sbuf("ckvlat", (128, SB), BF16)
    ckvlat = ckvlat_t
    ckv32 = ckvlat_t.bitcast(F32)
    ckv1 = sbuf("ckv1", (128, NKT * 8), BF16)
    expsc = phA[:, NKT * 128:NKT * 128 + 3 * 2048]
    ctx_sp = sbuf("ctx_sp", (128, 512), F32)
    ctx_u = sbuf("ctx_u", (128, 512), F32)
    dnm = sbuf("dnm", (32, 512), F32)
    dtmp = sbuf("dtmp", (32, 512), F32)
    dtmp2 = sbuf("dtmp2", (32, 512), F32)
    ctx_n = sbuf("ctx_n", (128, 512), BF16)
    a_bf = sbuf("a_bf", (128, D), BF16)
    bc_sb = sbuf("bc_sb", (128, 32), F32)
    bo_sb = sbuf("bo_sb", (128, 32), F32)
    bb_sb = sbuf("bb_sb", (128, 32), F32)
    attn_c = sbuf("attn_c", (128, 512), BF16)
    attn_r = sbuf("attn_r", (128, 512), BF16)
    sq_c = sbuf("sq_c", (128, 512), BF16)
    x1_bf = sbuf("x1_bf", (128, NCH * TOK), BF16)
    xf_c = [sbuf(f"xf_c{i}", (128, 512), F32) for i in range(2)]
    x1f_c = sbuf("x1f_c", (128, 512), F32)
    hcat = sbuf("hcat", (40, D), BF16)
    g0all = sbuf("g0all", (128, NCH * 32), BF16)
    g1all = sbuf("g1all", (128, NCH * 8), BF16)
    gacc = sbuf("gacc", (128, 32), F32)
    gacc1 = sbuf("gacc1", (128, 8), F32)
    gtmp = sbuf("gtmp", (128, 32), F32)
    cfe = sbuf("cfe", (128, 32), F32)
    cfe1 = sbuf("cfe1", (128, 8), F32)
    mke = sbuf("mke", (128, 32), BF16)
    mke1 = sbuf("mke1", (128, 8), BF16)
    mk_u8 = sbuf("mk_u8", (128, 32), U8)
    mk1_u8 = sbuf("mk1_u8", (128, 8), U8)
    u_sb = sbuf("u_sb", (40, TOK), BF16)
    wstr = [sbuf(f"wstr{i}", (128, 4096), BF16) for i in range(2)]
    wvall = wstr[0]
    wos = [sbuf(f"wos{i}", (128, 512), BF16) for i in range(2)]
    ms = [sbuf(f"ms{i}", (128, 512), BF16) for i in range(2)]
    enc_sb = proB_bf[:, 4096:5376]
    wqk_sb = sbuf("wqk_sb", (5, 16 * 32), BF16)
    wqka_sb = sbuf("wqka_sb", (13, 128), BF16)
    e4_sb = sbuf("e4_sb", (128, 32), BF16)
    e4t_sb = sbuf("e4t_sb", (32, 128), F32)
    ident_sb = sbuf("ident_sb", (16, 16), F32)
    ones_bf = sbuf("ones_bf", (128, 1), BF16)
    ones_f32 = sbuf("ones_f32", (128, 1), F32)
    onesr_f32 = sbuf("onesr_f32", (1, 128), F32)
    bvu_sb = sbuf("bvu_sb", (128, NCH), BF16)
    rb_sb = proB[:, 3072:3584]
    mb_sb = proB[:, 3584:4096]
    rb3 = proB[:, 0:1024]
    mb3 = proB[:, 1024:2048]
    a32 = a_bf.bitcast(F32)
    rb4 = a32[:, 0:512]
    mb4 = a32[:, 512:1024]
    senc_sb = sbuf("senc_sb", (128, 1), F32)
    ffn_c = ckvlat_t[:, 0:512]
    ffn_r = ckvlat_t[:, 512:1024]
    out_c = ckv32[:, 512:1024]
    tt1 = ckv32[:, 1024:1536]
    tt2 = ckv32[:, 1536:2048]

    def slot(i, j=0):
        return rowpool[0:1, 512 * i:512 * (i + 1)]

    ring = psum("ring", (128, 3072))
    pctx = psum("pctx", (128, 512))
    paux = psum("paux", (128, 512))
    rbank = [ring[:, 512 * i:512 * (i + 1)] for i in range(6)]
    RB = [("rng", i) for i in range(6)]
    PCTX = ("pctx",)
    PAUX = ("paux",)



    for c in range(NCH):
        B.dma("pool", wcat_sb[:, 17 * c:17 * (c + 1)], wcat[128 * c:128 * (c + 1), :],
              grp="cpool", writes=[("wcat_sb",)])
    B.dma("pool", wqk_sb[:], wqk_ext[:], grp="cpool", writes=[("wqk_sb",)])
    B.dma("pool", wqka_sb[:], wqk_aug[:], grp="cpool", writes=[("wqka_sb",)])
    B.dma("pool", e4_sb[:], e4_d[:], grp="cpool", writes=[("e4_sb",)])
    B.dma("syn", e4t_sb[:], e4t_d[:], grp="csyn", writes=[("e4t_sb",)])
    B.dma("syn", b17_sb[:], b17_d[:], grp="csyn", writes=[("b17_sb",)])
    B.dma("syn", ident_sb[:], ident_d[:], grp="csyn", writes=[("ident_sb",)])
    B.dma("syn", ones17[:], ones_dr[0:1, 0:32], grp="csyn", writes=[("ones17",)])
    B.dma("pool", ones_bf[:], ones_col[:], grp="cpool", writes=[("ones_bf",)])
    B.dma("syn", ones_f32[:], ones_col[:], grp="csyn", writes=[("ones_f32",)])
    B.dma("syn", onesr_f32[:], ones_dr[0:1, 0:128], grp="csyn", writes=[("onesr_f32",)])
    B.dma("syn", bo_sb[:], bo_d[:], grp="csyn", writes=[("bo_sb",)])
    B.dma("syn", bb_sb[:], bb_d[:], grp="csyn", writes=[("bb_sb",)])
    B.dma("pool", bvu_sb[:], bvu_d[:], grp="cpool", writes=[("bvu_sb",)])
    B.dve(lambda v: v.memset(wvall[:], 0.0), writes=[("wstr", 0)])
    for c in range(NCH):
        B.dma("pool", wvall[:, 128 * c + 4 * c:128 * c + 4 * c + 4],
              wvuT[128 * c:128 * (c + 1), :], grp="cpool", writes=[("wstr", 0)])
    B.dve(lambda v: v.memset(ckvlat[:], 1.0), writes=[("ckvlat",)])
    B.dve(lambda v: v.memset(ckv1[:], 1.0), writes=[("ckv1",)])
    B.dve(lambda v: v.memset(cqm2[0][:], 0.0), writes=[("cqm", 0)])
    B.dve(lambda v: v.memset(cqm2[1][:], 0.0), writes=[("cqm", 1)])
    B.dve(lambda v: v.memset(cqb[:], 0.0), writes=[("cqb",)])
    B.dma("pool", cqb[12:13, :], ones_dr[0:1, 0:TOK], grp="cpool", writes=[("cqb",)])
    for bf_ in range(2):
        for g in range(4):
            B.dma("pool", cqm2[bf_][32 * g + 12:32 * g + 13, :],
                  ones_dr[0:1, 0:TOK], grp="cpool", writes=[("cqm", bf_)])

    for half, (co_, mk_, cf_, mkf, mku, ga, gall, wdt) in enumerate(
            [(co0, mk0, cfe, mke, mk_u8, gacc, g0all, 32),
             (co1, mk1, cfe1, mke1, mk1_u8, gacc1, g1all, 8)]):
        for c in range(NCH):
            for e in range(3):
                B.dma("syn", cf_[:, :], co_[e, 128 * c:128 * (c + 1), :],
                      grp=f"gld{half}", writes=[("cfe", half)])
                B.dma("syn", mku[:, :], mk_[e, 128 * c:128 * (c + 1), :],
                      grp=f"gld{half}", writes=[("mku", half)])
                B.dve(lambda v, a=mkf, bb_=mku: v.tensor_copy(a[:], bb_[:]),
                      reads=[("mku", half)], writes=[("mkf", half)], self_dep=True)
                if e == 0:
                    B.dve(lambda v, a=ga, cc=cf_, m_=mkf: v.tensor_mul(a[:], cc[:], m_[:]),
                          reads=[("cfe", half), ("mkf", half)], writes=[("ga", half)],
                          self_dep=True)
                else:
                    B.dve(lambda v, cc=cf_, m_=mkf, w=wdt:
                          v.tensor_mul(gtmp[:, :w], cc[:], m_[:]),
                          reads=[("cfe", half), ("mkf", half)], writes=[("gtmp",)],
                          self_dep=True)
                    B.dve(lambda v, a=ga, w=wdt: v.tensor_add(a[:], a[:], gtmp[:, :w]),
                          reads=[("gtmp",)], writes=[("ga", half)], self_dep=True)
            B.dve(lambda v, a=ga, gl=gall, c=c, w=wdt:
                  v.tensor_copy(gl[:, w * c:w * (c + 1)], a[:]),
                  reads=[("ga", half)], writes=[("gall", half)], self_dep=True)

    for c in range(NCH):
        B.pe(lambda t, c=c: t.matmul(paux[0:17, 0:1], wcat_sb[:, 17 * c:17 * (c + 1)],
                                     ones_bf[:], start=(c == 0), stop=(c == NCH - 1)),
             reads=[("wcat_sb",), ("ones_bf",)], writes=[PAUX])
    B.dve(lambda v: v.tensor_scalar_mul(sneg[:], paux[0:17, 0:1], -1.0),
          reads=[PAUX], writes=[("sneg",)], self_dep=True)

    for blk in range(NBLK):
        for q4 in range(8):
            par = q4 % 2
            xb, xq = xblk[par], xsq[par]
            B.dma("pool", xb[:, :],
                  xT.rearrange("(cc p) t -> p cc t", p=128)
                  [:, 4 * q4:4 * q4 + 4, 512 * blk:512 * (blk + 1)],
                  grp=f"xb{par}", writes=[("xblk", par)])
            B.dve(lambda v, xb=xb, xq=xq: v.tensor_mul(xq[:], xb[:], xb[:]),
                  reads=[("xblk", par)], writes=[("xsq",)])
            for i in range(4):
                c = 4 * q4 + i
                B.pe(lambda t, xb=xb, i=i, c=c:
                     t.matmul(paux[0:17, :], wcat_sb[:, 17 * c:17 * (c + 1)],
                              xb[:, 512 * i:512 * (i + 1)],
                              start=(c == 0), stop=(c == NCH - 1)),
                     reads=[("xblk", par), ("wcat_sb",)], writes=[PAUX])
                B.pe(lambda t, xq=xq, i=i, c=c:
                     t.matmul(pctx[0:1, :], ones_bf[:],
                              xq[:, 512 * i:512 * (i + 1)],
                              start=(c == 0), stop=(c == NCH - 1),
                              skip_group_check=True),
                     reads=[("xsq",), ("ones_bf",)], writes=[PCTX])
                B.pe(lambda t, xb=xb, i=i, c=c:
                     t.matmul(pctx[32:33, :], ones_bf[:],
                              xb[:, 512 * i:512 * (i + 1)],
                              start=(c == 0), stop=(c == NCH - 1),
                              tile_position=(0, 32), skip_group_check=True),
                     reads=[("xblk", par), ("ones_bf",)], writes=[PCTX])
        B.dve(lambda v: v.tensor_copy(blk_f[0:17, :], paux[0:17, :]),
              reads=[PAUX], writes=[("blk_f",)])
        B.dve(lambda v: v.tensor_scalar_mul(slot(0), pctx[32:33, :], 1.0 / D),
              reads=[PCTX], writes=[("rp", 0)])
        B.dve(lambda v: v.tensor_scalar_mul(slot(1), pctx[0:1, :], 1.0 / D),
              reads=[PCTX], writes=[("rp", 1)])
        B.dve(lambda v: v.tensor_mul(slot(2), slot(0), slot(0)),
              reads=[("rp", 0)], writes=[("rp", 2)], self_dep=True)
        B.dve(lambda v: v.tensor_sub(slot(1), slot(1), slot(2)),
              reads=[("rp", 2), ("rp", 1)], writes=[("rp", 1)], self_dep=True)
        B.dve(lambda v: v.tensor_scalar_add(slot(1), slot(1), EPS),
              reads=[("rp", 1)], writes=[("rp", 1)], self_dep=True)
        B.act(lambda s: s.activation(slot(3), slot(1), AF.Ln),
              reads=[("rp", 1)], writes=[("rp", 3)])
        B.act(lambda s: s.activation(slot(3), slot(3), AF.Exp, scale=-0.5),
              reads=[("rp", 3)], writes=[("rp", 3)], self_dep=True)
        B.dve(lambda v: v.tensor_mul(slot(4), slot(0), slot(3)),
              reads=[("rp", 0), ("rp", 3)], writes=[("rp", 4)], self_dep=True)
        B.pe(lambda t: t.matmul(rbank[0][0:17, :], ones17[0:1, 0:17], slot(3),
                                start=True, stop=True),
             reads=[("rp", 3), ("ones17",)], writes=[RB[0]])
        B.pe(lambda t: t.matmul(rbank[1][0:17, :], ones17[0:1, 0:17], slot(4),
                                start=True, stop=True),
             reads=[("rp", 4), ("ones17",)], writes=[RB[1]])
        B.dve(lambda v: v.tensor_mul(blk_fix[:], blk_f[0:17, :], rbank[0][0:17, :]),
              reads=[("blk_f",), RB[0]], writes=[("blk_fix",)])
        B.dve(lambda v: v.scalar_tensor_tensor(blk_fix[:], rbank[1][0:17, :], sneg[:],
                                               blk_fix[:], ALU.mult, ALU.add),
              reads=[RB[1], ("sneg",), ("blk_fix",)], writes=[("blk_fix",)],
              self_dep=True)
        B.dve(lambda v: v.tensor_scalar_add(blk_fix[:], blk_fix[:], b17_sb[:]),
              reads=[("b17_sb",), ("blk_fix",)], writes=[("blk_fix",)])
        B.dma("pool", ckvlat[0:4, 512 * blk:512 * (blk + 1)], blk_fix[12:16, :],
              grp="ckv", reads=[("blk_fix",)], writes=[("ckvlat", blk)])
        for i in range(4):
            kt = 4 * blk + i
            B.pe(lambda t, i=i: t.transpose(rbank[2][:, 0:16],
                                            blk_fix[0:16, 128 * i:128 * (i + 1)],
                                            ident_sb[:]),
                 reads=[("blk_fix",), ("ident_sb",)], writes=[RB[2]])
            B.dve(lambda v, kt=kt: v.tensor_copy(ckv1[:, 8 * kt:8 * kt + 4],
                                                 rbank[2][:, 12:16]),
                  reads=[RB[2]], writes=[("ckv1", kt)])
        if DEBUG:
            B.dma("syn", cqkv_dbg[:, 512 * blk:512 * (blk + 1)], blk_fix[:],
                  grp="dbg", reads=[("blk_fix",)], writes=[("cqkv_dbg",)])
        if blk < 2:
            B.dve(lambda v, blk=blk:
                  v.tensor_copy(cqb[0:12, 512 * blk:512 * (blk + 1)], blk_fix[0:12, :]),
                  reads=[("blk_fix",)], writes=[("cqb", blk)])
            for g in range(4):
                for bf_ in range(2):
                    B.dma("pool", cqm2[bf_][32 * g:32 * g + 12,
                                            512 * blk:512 * (blk + 1)],
                          blk_fix[0:12, :],
                          grp="cqc", reads=[("blk_fix",)], writes=[("cqm", bf_)])

    for h2 in range(2):
        B.pe(lambda t, h2=h2: t.matmul(rbank[0][:, :], wqka_sb[:],
                                       cqb[0:13, 512 * h2:512 * (h2 + 1)],
                                       start=True, stop=True),
             reads=[("cqb", 0), ("cqb", 1), ("wqka_sb",)], writes=[RB[0]])
        B.dve(lambda v, h2=h2: v.tensor_copy(qsb[:, 512 * h2:512 * (h2 + 1)],
                                             rbank[0][:, :]),
              reads=[RB[0]], writes=[("qsb", h2)])
        if DEBUG:
            B.dma("pool", q_dbg[:, 512 * h2:512 * (h2 + 1)],
                  qsb[:, 512 * h2:512 * (h2 + 1)],
                  grp="dbg", reads=[("qsb", h2)], writes=[("q_dbg",)])
        B.dve(lambda v, h2=h2: v.tensor_mul(sq_c[:],
                                            qsb[:, 512 * h2:512 * (h2 + 1)],
                                            qsb[:, 512 * h2:512 * (h2 + 1)]),
              reads=[("qsb", h2)], writes=[("sq_c",)])
        B.pe(lambda t, h2=h2: t.matmul(rbank[1][0:32, :], e4_sb[:], sq_c[:],
                                       start=True, stop=True),
             reads=[("sq_c",), ("e4_sb",)], writes=[RB[1]])
        B.dve(lambda v, h2=h2: v.tensor_scalar(c_sb[0:32, 512 * h2:512 * (h2 + 1)],
                                               rbank[1][0:32, :], 0.5, CSHIFT,
                                               ALU.mult, ALU.add),
              reads=[RB[1]], writes=[("c_sb", h2)])
    B.dma("syn", c_stage[:], c_sb[0:32, :], grp="cst",
          reads=[("c_sb", 0), ("c_sb", 1)], writes=[("c_stage",)])


    if TRUNC < 2:
        return finish(nc, B, stack)
    B.dve(lambda v: v.memset(bc_sb[:], 0.0), writes=[("bc_sb",)])
    for og in range(8):
        for c in range(NCH):
            par = c % 2
            B.dma("pool", wos[par][:, :],
                  wo_d[128 * c:128 * (c + 1), 512 * og:512 * (og + 1)],
                  grp=f"wos{par}", writes=[("wos", par)])
            B.pe(lambda t, c=c, par=par: t.matmul(
                rbank[3][:, :], wvall[:, 128 * c:128 * (c + 1)], wos[par][:, :],
                start=(c == 0), stop=(c == NCH - 1)),
                reads=[("wos", par), ("wstr", 0)], writes=[RB[3]])
            for sub in range(4):
                B.pe(lambda t, c=c, par=par, sub=sub: t.matmul(
                    paux[:, sub:sub + 1],
                    wos[par][:, 128 * sub:128 * (sub + 1)], bvu_sb[:, c:c + 1],
                    start=True, stop=True),
                    reads=[("wos", par), ("bvu_sb",)], writes=[PAUX])
            B.dve(lambda v, og=og: v.tensor_add(bc_sb[:, 4 * og:4 * (og + 1)],
                                                bc_sb[:, 4 * og:4 * (og + 1)],
                                                paux[:, 0:4]),
                  reads=[PAUX, ("bc_sb",)], writes=[("bc_sb",)], self_dep=True)
        B.dve(lambda v, og=og: v.tensor_copy(a_bf[:, 512 * og:512 * (og + 1)],
                                             rbank[3][:, :]),
              reads=[RB[3]], writes=[("a_bf", og)])
    B.dve(lambda v: v.tensor_add(bc_sb[:], bc_sb[:], bo_sb[:]),
          reads=[("bo_sb",), ("bc_sb",)], writes=[("bc_sb",)], self_dep=True)

    if DEBUG:
        B.dma("pool", a_dbg[:], a_bf[:], grp="dbg", reads=[("a_bf", og) for og in range(8)],
              writes=[("a_dbg",)])
        B.dma("syn", bc_dbg[:], bc_sb[:], grp="dbg", reads=[("bc_sb",)], writes=[("bc_dbg",)])
    for jg in range(8):
        for c in range(NCH):
            par = c % 2
            B.dma("pool", ms[par][:, :],
                  mmat[128 * c:128 * (c + 1), 512 * jg:512 * (jg + 1)],
                  grp=f"ms{par}", writes=[("ms", par), ("xsq",)])
            B.pe(lambda t, c=c, par=par: t.matmul(
                rbank[5][0:32, :], g0all[:, 32 * c:32 * (c + 1)], ms[par][:, :],
                start=(c == 0), stop=(c == NCH - 1), skip_group_check=True),
                reads=[("ms", par), ("gall", 0)], writes=[RB[5]])
            B.pe(lambda t, c=c, par=par: t.matmul(
                rbank[5][32:40, :], g1all[:, 8 * c:8 * (c + 1)], ms[par][:, :],
                start=(c == 0), stop=(c == NCH - 1), tile_position=(0, 32),
                skip_group_check=True),
                reads=[("ms", par), ("gall", 1)], writes=[RB[5]])
        B.dve(lambda v, jg=jg: v.tensor_scalar_mul(hcat[0:32, 512 * jg:512 * (jg + 1)],
                                                   rbank[5][0:32, :], 0.5),
              reads=[RB[5]], writes=[("hcat",)])
        B.dve(lambda v, jg=jg: v.tensor_scalar_mul(hcat[32:40, 512 * jg:512 * (jg + 1)],
                                                   rbank[5][32:40, :], 0.5),
              reads=[RB[5]], writes=[("hcat",)])

    if DEBUG:
        B.dma("pool", h_dbg[:], hcat[0:40, :], grp="dbg", reads=[("hcat",)], writes=[("h_dbg",)])
        B.dma("pool", g_dbg[:, 0:NCH * 32], g0all[:], grp="dbg", reads=[("gall", 0)],
              writes=[("g_dbg",)])
        B.dma("pool", g_dbg[:, NCH * 32:NCH * 32 + NCH * 8], g1all[:],
              grp="dbg", reads=[("gall", 1)], writes=[("g_dbg",)])
    if TRUNC < 3:
        return finish(nc, B, stack)
    B.dve(lambda v: v.memset(paux[:, :], 0.0), writes=[PAUX])
    B.dve(lambda v: v.memset(pctx[:, :], 0.0), writes=[PCTX])
    for s_ in range(NSLOT):
        cb = s_ % 2
        for g in range(4):
            B.dma("pool", cqm2[cb][32 * g + 13:32 * g + 14, :],
                  c_stage[4 * s_ + g:4 * s_ + g + 1, :],
                  grp=f"cq{cb}", reads=[("c_stage",)], writes=[("cqm", cb)])
        for ktg in range(8):
            for g in range(4):
                B.pe(lambda t, g=g, ktg=ktg, s_=s_: t.matmul(
                    paux[32 * g:32 * g + 16, :],
                    wqk_sb[0:5, 16 * (4 * s_ + g):16 * (4 * s_ + g) + 16],
                    ckvlat[0:5, 512 * ktg:512 * (ktg + 1)],
                    start=True, stop=True, tile_position=(0, 32 * g)),
                    reads=[("wqk_sb",), ("ckvlat", ktg)],
                    writes=[PAUX])
            B.dve(lambda v, ktg=ktg: v.tensor_scalar_mul(
                kh2[:, 512 * ktg:512 * (ktg + 1)], paux[:, :], 0.5),
                reads=[PAUX], writes=[("kh2", ktg)])
        for q in range(NQ):
            prev = None
            for kt in range(NKT):
                pa = (2 * kt) % 3
                pb = (2 * kt + 1) % 3
                slot3 = kt % 3
                for g in range(4):
                    pair = pa if g < 2 else pb
                    bank = 2 * pair + (g & 1)
                    B.pe(lambda t, g=g, kt=kt, cb=cb, q=q, bank=bank: t.matmul(
                        rbank[bank][:, :],
                        kh2[32 * g:32 * g + 16, 128 * kt:128 * (kt + 1)],
                        cqm2[cb][32 * g:32 * g + 16, 512 * q:512 * (q + 1)],
                        start=True, stop=True, tile_position=(32 * g, 0)),
                        reads=[("kh2", kt // 4), ("cqm", cb)], writes=[RB[bank]])
                rv = ring.rearrange("p (c n) -> p c n", n=1024)
                dst = expsc[:, 2048 * slot3:2048 * (slot3 + 1)]
                if pb == pa + 1:
                    src = ring[:, 1024 * pa:1024 * (pb + 1)]
                    dst2 = dst
                else:
                    src = rv[:, ::-2, :]
                    dst2 = dst.rearrange("p (c n) -> p c n", n=1024)
                B.act(lambda s, src=src, dst2=dst2: s.activation(dst2, src, AF.Exp),
                      reads=[RB[2 * pa], RB[2 * pa + 1], RB[2 * pb], RB[2 * pb + 1]],
                      writes=[("expsc", slot3)])
                if prev is not None:
                    pkt, pslot3 = prev
                    for g in range(4):
                        pair_off = 0 if g < 2 else 1024
                        off = 2048 * pslot3 + pair_off + 512 * (g & 1)
                        B.pe(lambda t, g=g, pkt=pkt, off=off: t.matmul(
                            pctx[32 * g:32 * g + 5, :],
                            ckv1[:, 8 * pkt:8 * pkt + 5],
                            expsc[:, off:off + 512],
                            start=(pkt == 0), stop=(pkt == NKT - 1),
                            tile_position=(0, 32 * g), skip_group_check=True),
                            reads=[("expsc", pslot3), ("ckv1", pkt)],
                            writes=[PCTX])
                prev = (kt, slot3)
            pkt, pslot3 = prev
            for g in range(4):
                pair_off = 0 if g < 2 else 1024
                off = 2048 * pslot3 + pair_off + 512 * (g & 1)
                B.pe(lambda t, g=g, pkt=pkt, off=off: t.matmul(
                    pctx[32 * g:32 * g + 5, :],
                    ckv1[:, 8 * pkt:8 * pkt + 5],
                    expsc[:, off:off + 512],
                    start=(pkt == 0), stop=(pkt == NKT - 1),
                    tile_position=(0, 32 * g), skip_group_check=True),
                    reads=[("expsc", pslot3), ("ckv1", pkt)], writes=[PCTX])
            B.dve(lambda v: v.tensor_copy(ctx_sp[:], pctx[:]),
                  reads=[PCTX], writes=[("ctx_sp",)])
            for g in range(4):
                B.dma("syn", ctx_stage[(q * NSLOT + s_) * 20 + 5 * g:
                                       (q * NSLOT + s_) * 20 + 5 * (g + 1), :],
                      ctx_sp[32 * g:32 * g + 5, :],
                      grp="ctw", reads=[("ctx_sp",)], writes=[("ctx_stage", q, s_)])

    def stat_finish(pbank, pkey, ri, mi):
        B.dve(lambda v: v.tensor_scalar_mul(slot(0), pbank[0:1, :], 1.0 / D),
              reads=[pkey], writes=[("rp", 0)])
        B.dve(lambda v: v.tensor_scalar_mul(slot(1), pbank[32:33, :], 1.0 / D),
              reads=[pkey], writes=[("rp", 1)])
        B.dve(lambda v: v.tensor_mul(slot(2), slot(0), slot(0)),
              reads=[("rp", 0)], writes=[("rp", 2)], self_dep=True)
        B.dve(lambda v: v.tensor_sub(slot(1), slot(1), slot(2)),
              reads=[("rp", 2), ("rp", 1)], writes=[("rp", 1)], self_dep=True)
        B.dve(lambda v: v.tensor_scalar_add(slot(1), slot(1), EPS),
              reads=[("rp", 1)], writes=[("rp", 1)], self_dep=True)
        B.act(lambda s: s.activation(slot(ri), slot(1), AF.Ln),
              reads=[("rp", 1)], writes=[("rp", ri)])
        B.act(lambda s: s.activation(slot(ri), slot(ri), AF.Exp, scale=-0.5),
              reads=[("rp", ri)], writes=[("rp", ri)], self_dep=True)
        B.dve(lambda v: v.tensor_mul(slot(mi), slot(0), slot(ri)),
              reads=[("rp", 0), ("rp", ri)], writes=[("rp", mi)], self_dep=True)

    if TRUNC < 4:
        return finish(nc, B, stack)
    B.dve(lambda v: v.engine_nop(),
          writes=[("ckvlat", b) for b in range(NBLK)] +
                 [("c_sb", 0), ("c_sb", 1), ("ckvlat",)])
    for q in range(NQ):
        for s_ in range(NSLOT):
            for g in range(4):
                base = (q * NSLOT + s_) * 20 + 5 * g
                B.dma("syn", ctx_u[16 * s_ + 4 * g:16 * s_ + 4 * (g + 1), :],
                      ctx_stage[base:base + 4, :],
                      grp="cxr", reads=[("ctx_stage", q, s_)], writes=[("ctx_u", q)])
                B.dma("syn", dnm[4 * s_ + g:4 * s_ + g + 1, :],
                      ctx_stage[base + 4:base + 5, :],
                      grp="cxr", reads=[("ctx_stage", q, s_)], writes=[("dnm", q)])
        B.dve(lambda v: v.reciprocal(dtmp[:], dnm[:]),
              reads=[("dnm", q)], writes=[("dtmp",)])
        B.dve(lambda v: v.tensor_mul(dtmp2[:], dnm[:], dtmp[:]),
              reads=[("dnm", q), ("dtmp",)], writes=[("dtmp2",)], self_dep=True)
        B.dve(lambda v: v.tensor_scalar(dtmp2[:], dtmp2[:], -1.0, 2.0, ALU.mult, ALU.add),
              reads=[("dtmp2",)], writes=[("dtmp2",)], self_dep=True)
        B.dve(lambda v: v.tensor_mul(dtmp[:], dtmp[:], dtmp2[:]),
              reads=[("dtmp",), ("dtmp2",)], writes=[("dtmp",)], self_dep=True)
        B.pe(lambda t: t.matmul(rbank[2][:, :], e4t_sb[:], dtmp[:],
                                start=True, stop=True),
             reads=[("dtmp",), ("e4t_sb",)], writes=[RB[2]])
        B.dve(lambda v: v.tensor_mul(ctx_n[:], ctx_u[:], rbank[2][:, :]),
              reads=[("ctx_u", q), RB[2]], writes=[("ctx_n", q)])
        if DEBUG:
            B.dma("pool", ctxn_dbg[:, 512 * q:512 * (q + 1)], ctx_n[:],
                  grp="dbg", reads=[("ctx_n", q)], writes=[("ctxn_dbg",)])
        for ot in range(NCH):
            pbk = ot % 2
            B.pe(lambda t, ot=ot, pbk=pbk: t.matmul(
                rbank[pbk][:, :], a_bf[:, 128 * ot:128 * (ot + 1)], ctx_n[:],
                start=True, stop=True),
                reads=[("ctx_n", q), ("a_bf", ot // 4)], writes=[RB[pbk]])
            B.dve(lambda v, ot=ot, pbk=pbk: v.tensor_scalar(
                attn_c[:], rbank[pbk][:, :],
                bc_sb[:, ot:ot + 1], None, ALU.add),
                reads=[RB[pbk], ("bc_sb",)], writes=[("attn_c",)])
            B.dve(lambda v: v.tensor_mul(sq_c[:], attn_c[:], attn_c[:]),
                  reads=[("attn_c",)], writes=[("sq_c",)])
            B.pe(lambda t, ot=ot: t.matmul(rbank[2][0:1, :], ones_bf[:], attn_c[:],
                                           start=(ot == 0), stop=(ot == NCH - 1),
                                           skip_group_check=True),
                 reads=[("attn_c",), ("ones_bf",)], writes=[RB[2]])
            B.pe(lambda t, ot=ot: t.matmul(rbank[2][32:33, :], ones_bf[:], sq_c[:],
                                           start=(ot == 0), stop=(ot == NCH - 1),
                                           tile_position=(0, 32),
                                           skip_group_check=True),
                 reads=[("sq_c",), ("ones_bf",)], writes=[RB[2]])
            B.dma("syn", attn_sp[128 * ot:128 * (ot + 1), 512 * q:512 * (q + 1)],
                  attn_c[:], grp="aw", reads=[("attn_c",)], writes=[("attn_spl", ot, q)])
            if DEBUG:
                B.dma("pool", attn_dbg[128 * ot:128 * (ot + 1), 512 * q:512 * (q + 1)],
                      attn_c[:], grp="dbg", reads=[("attn_c",)], writes=[("attn_dbg",)])
        stat_finish(rbank[2], RB[2], 3, 4)
        B.pe(lambda t: t.matmul(rbank[3][:, :], onesr_f32[:], slot(3),
                                start=True, stop=True),
             reads=[("rp", 3), ("onesr_f32",)], writes=[RB[3]])
        B.pe(lambda t: t.matmul(rbank[4][:, :], onesr_f32[:], slot(4),
                                start=True, stop=True),
             reads=[("rp", 4), ("onesr_f32",)], writes=[RB[4]])
        B.dve(lambda v: v.tensor_copy(rb_sb[:], rbank[3][:, :]),
              reads=[RB[3]], writes=[("rb_sb",)])
        B.dve(lambda v: v.tensor_copy(mb_sb[:], rbank[4][:, :]),
              reads=[RB[4]], writes=[("mb_sb",)])
        for c in range(NCH):
            par = c % 2
            B.dma("syn", xf_c[par][:, :],
                  xT[128 * c:128 * (c + 1), 512 * q:512 * (q + 1)],
                  grp=f"xf{par}", writes=[("xf_c", par)])
            B.dma("pool", attn_r[:, :],
                  attn_sp[128 * c:128 * (c + 1), 512 * q:512 * (q + 1)],
                  grp="ar", reads=[("attn_spl", c, q)], writes=[("attn_r",)])
            B.dve(lambda v: v.tensor_mul(tt1[:], attn_r[:], rb_sb[:]),
                  reads=[("attn_r",), ("rb_sb",)], writes=[("tt1",)])
            B.dve(lambda v: v.tensor_sub(tt2[:], tt1[:], mb_sb[:]),
                  reads=[("tt1",), ("mb_sb",)], writes=[("tt2",)])
            B.dve(lambda v, par=par: v.tensor_add(x1f_c[:], tt2[:], xf_c[par][:, :]),
                  reads=[("tt2",), ("xf_c", par)], writes=[("x1f_c",)])
            B.dve(lambda v, c=c, q=q: v.tensor_copy(
                x1_bf[:, TOK * c + 512 * q:TOK * c + 512 * (q + 1)], x1f_c[:]),
                reads=[("x1f_c",)], writes=[("x1_bf", c, q)])
            B.dma("syn", x1_spill[128 * c:128 * (c + 1), 512 * q:512 * (q + 1)],
                  x1f_c[:], grp="x1w", reads=[("x1f_c",)], writes=[("x1_spill", c, q)])

    if TRUNC < 5:
        return finish(nc, B, stack)
    B.dve(lambda v: v.engine_nop(),
          writes=[("xblk", 0, qq) for qq in range(8)] +
                 [("xblk", 1, qq) for qq in range(8)] +
                 [("cqb", 0), ("cqb", 1), ("qsb", 0), ("qsb", 1),
                  ("c_sb", 0), ("c_sb", 1)])
    for c in range(NCH):
        B.dma("pool", enc_sb[:, 40 * c:40 * (c + 1)], encT_d[128 * c:128 * (c + 1), :],
              grp="enc", writes=[("enc_sb",), ("cqb", 0), ("cqb", 1),
                                 ("qsb", 0), ("qsb", 1)])
    for q in range(NQ):
        for c in range(NCH):
            B.dve(lambda v, c=c, q=q: v.tensor_mul(
                sq_c[:], x1_bf[:, TOK * c + 512 * q:TOK * c + 512 * (q + 1)],
                x1_bf[:, TOK * c + 512 * q:TOK * c + 512 * (q + 1)]),
                reads=[("x1_bf", c, q)], writes=[("sq_c",)])
            B.pe(lambda t, c=c, q=q: t.matmul(
                rbank[2][0:1, :], ones_bf[:],
                x1_bf[:, TOK * c + 512 * q:TOK * c + 512 * (q + 1)],
                start=(c == 0), stop=(c == NCH - 1), skip_group_check=True),
                reads=[("x1_bf", c, q), ("ones_bf",)], writes=[RB[2]])
            B.pe(lambda t, c=c: t.matmul(rbank[2][32:33, :], ones_bf[:], sq_c[:],
                                         start=(c == 0), stop=(c == NCH - 1),
                                         tile_position=(0, 32),
                                         skip_group_check=True),
                 reads=[("sq_c",), ("ones_bf",)], writes=[RB[2]])
        stat_finish(rbank[2], RB[2], 3, 4)
        B.pe(lambda t: t.matmul(rbank[3][:, :], onesr_f32[:], slot(4),
                                start=True, stop=True),
             reads=[("rp", 4), ("onesr_f32",)], writes=[RB[3]])
        B.dve(lambda v, q=q: v.tensor_copy(mb3[:, 512 * q:512 * (q + 1)],
                                           rbank[3][:, :]),
              reads=[RB[3]], writes=[("mb3", q)])
        B.pe(lambda t: t.matmul(rbank[3][:, :], onesr_f32[:], slot(3),
                                start=True, stop=True),
             reads=[("rp", 3), ("onesr_f32",)], writes=[RB[3]])
        B.dve(lambda v, q=q: v.tensor_scalar_mul(rb3[:, 512 * q:512 * (q + 1)],
                                                 rbank[3][:, :], -1.0),
              reads=[RB[3]], writes=[("rb3", q)])

    for ot in range(NCH + 1):
        par = ot % 2
        if ot > 0:
            for sub in range(4):
                B.dma("pool", wstr[par][:, 1024 * sub:1024 * (sub + 1)],
                      wbT_d.rearrange("(cc p) o -> p cc o", p=128)
                      [:, 8 * sub:8 * sub + 8, 128 * (ot - 1):128 * ot],
                      grp=f"wb{par}", writes=[("wstr", par)])
        for q in range(NQ):
            ww = 40 if ot == 0 else 128
            wkey = ("enc_sb",) if ot == 0 else ("wstr", par)
            for c in range(NCH):
                if ot == 0:
                    lt = enc_sb[:, 40 * c:40 * (c + 1)]
                else:
                    lt = wstr[par].rearrange("p (cc o) -> p cc o", o=128)[:, c, :]
                if q == 0:
                    B.pe(lambda t, c=c, lt=lt, ww=ww: t.matmul(
                        paux[0:ww, 0:1], lt, ones_bf[:],
                        start=(c == 0), stop=(c == NCH - 1)),
                        reads=[wkey, ("ones_bf",)], writes=[PAUX])
                B.pe(lambda t, c=c, q=q, lt=lt, ww=ww, par=par: t.matmul(
                    rbank[par][0:ww, :], lt,
                    x1_bf[:, TOK * c + 512 * q:TOK * c + 512 * (q + 1)],
                    start=(c == 0), stop=(c == NCH - 1)),
                    reads=[wkey, ("x1_bf", c, q)], writes=[RB[par]])
            if q == 0:
                B.dve(lambda v, ww=ww: v.tensor_copy(senc_sb[0:ww, :], paux[0:ww, 0:1]),
                      reads=[PAUX], writes=[("senc_sb",)], self_dep=True)
            if ot == 0:
                B.dve(lambda v, q=q, par=par: v.scalar_tensor_tensor(
                    tt1[0:40, :], mb3[0:40, 512 * q:512 * (q + 1)], senc_sb[0:40, :],
                    rbank[par][0:40, :], ALU.mult, ALU.subtract),
                    reads=[("mb3", q), ("senc_sb",), RB[par]], writes=[("tt1",)],
                    self_dep=True)
                B.dve(lambda v, q=q: v.tensor_mul(tt2[0:40, :], tt1[0:40, :],
                                                  rb3[0:40, 512 * q:512 * (q + 1)]),
                      reads=[("tt1",), ("rb3", q)], writes=[("tt2",)])
                B.dve(lambda v, q=q: v.tensor_copy(u_sb[0:40, 512 * q:512 * (q + 1)],
                                                   tt2[0:40, :]),
                      reads=[("tt2",)], writes=[("u_sb", q)])
                if DEBUG:
                    B.dma("pool", u_dbg[:, 512 * q:512 * (q + 1)],
                          u_sb[0:40, 512 * q:512 * (q + 1)],
                          grp="dbg", reads=[("u_sb", q)], writes=[("u_dbg",)])
            else:
                ub = 4 + (1 - par)
                B.pe(lambda t, ot=ot, q=q, ub=ub: t.matmul(
                    rbank[ub][:, :], hcat[0:40, 128 * (ot - 1):128 * ot],
                    u_sb[0:40, 512 * q:512 * (q + 1)], start=True, stop=True),
                    reads=[("u_sb", q), ("hcat",)], writes=[RB[ub]])
                B.dve(lambda v, q=q, par=par: v.scalar_tensor_tensor(
                    tt1[:], mb3[:, 512 * q:512 * (q + 1)], senc_sb[:],
                    rbank[par][:, :], ALU.mult, ALU.subtract),
                    reads=[("mb3", q), ("senc_sb",), RB[par]], writes=[("tt1",)],
                    self_dep=True)
                B.dve(lambda v, q=q: v.tensor_mul(tt2[:], tt1[:],
                                                  rb3[:, 512 * q:512 * (q + 1)]),
                      reads=[("tt1",), ("rb3", q)], writes=[("tt2",)])
                B.dve(lambda v, ot=ot, ub=ub: v.scalar_tensor_tensor(
                    ffn_c[:], tt2[:], bb_sb[:, ot - 1:ot], rbank[ub][:, :],
                    ALU.add, ALU.add),
                    reads=[("tt2",), ("bb_sb",), RB[ub]], writes=[("ffn_c",)])
                B.dve(lambda v: v.tensor_mul(sq_c[:], ffn_c[:], ffn_c[:]),
                      reads=[("ffn_c",)], writes=[("sq_c",)])
                sbk, skey = (rbank[2], RB[2]) if q == 0 else (pctx, PCTX)
                B.pe(lambda t, ot=ot, sbk=sbk: t.matmul(
                    sbk[0:1, :], ones_bf[:], ffn_c[:],
                    start=(ot == 1), stop=(ot == NCH), skip_group_check=True),
                    reads=[("ffn_c",), ("ones_bf",)], writes=[skey])
                B.pe(lambda t, ot=ot, sbk=sbk: t.matmul(
                    sbk[32:33, :], ones_bf[:], sq_c[:],
                    start=(ot == 1), stop=(ot == NCH), tile_position=(0, 32),
                    skip_group_check=True),
                    reads=[("sq_c",), ("ones_bf",)], writes=[skey])
                B.dma("syn", ffn_spill[128 * (ot - 1):128 * ot, 512 * q:512 * (q + 1)],
                      ffn_c[:], grp="fw", reads=[("ffn_c",)], writes=[("ffn_sp", ot, q)])

    B.dve(lambda v: v.engine_nop(),
          writes=[("a_bf", og) for og in range(8)])
    st4slots = [(5, 6), (7, 8)]
    for q in range(NQ):
        sbk, skey = (rbank[2], RB[2]) if q == 0 else (pctx, PCTX)
        stat_finish(sbk, skey, st4slots[q][0], st4slots[q][1])

    for q in range(NQ):
        ri, mi = st4slots[q]
        B.pe(lambda t, ri=ri: t.matmul(rbank[3][:, :], onesr_f32[:],
                                       slot(ri), start=True, stop=True),
             reads=[("rp", ri), ("onesr_f32",)], writes=[RB[3]])
        B.dve(lambda v: v.tensor_copy(rb4[:], rbank[3][:, :]),
              reads=[RB[3]], writes=[("rb4", q)])
        B.pe(lambda t, mi=mi: t.matmul(rbank[3][:, :], onesr_f32[:],
                                       slot(mi), start=True, stop=True),
             reads=[("rp", mi), ("onesr_f32",)], writes=[RB[3]])
        B.dve(lambda v: v.tensor_copy(mb4[:], rbank[3][:, :]),
              reads=[RB[3]], writes=[("mb4", q)])
        for c in range(NCH):
            par = c % 2
            B.dma("syn", xf_c[par][:, :],
                  x1_spill[128 * c:128 * (c + 1), 512 * q:512 * (q + 1)],
                  grp=f"xf{par}", reads=[("x1_spill", c, q)], writes=[("xf_c", par)])
            B.dma("pool", ffn_r[:, :],
                  ffn_spill[128 * c:128 * (c + 1), 512 * q:512 * (q + 1)],
                  grp="fr", reads=[("ffn_sp", c + 1, q)], writes=[("ffn_r",)])
            B.dve(lambda v: v.tensor_mul(tt1[:], ffn_r[:, :], rb4[:]),
                  reads=[("ffn_r",), ("rb4", q)], writes=[("tt1",)])
            B.dve(lambda v: v.tensor_sub(tt2[:], tt1[:], mb4[:]),
                  reads=[("tt1",), ("mb4", q)], writes=[("tt2",)])
            B.dve(lambda v, par=par: v.tensor_add(out_c[:], tt2[:], xf_c[par][:, :]),
                  reads=[("tt2",), ("xf_c", par)], writes=[("out_c",)])
            B.dma("syn", out[128 * c:128 * (c + 1), 512 * q:512 * (q + 1)],
                  out_c[:], grp="ow", reads=[("out_c",)], writes=[("out", c, q)])

    return finish(nc, B, stack)


def finish(nc, B, stack):
    B.finalize()
    sems = {s: stack.enter_context(nc.semaphore(s)) for s in Builder.STREAMS}
    for g in B.grps:
        sems[g] = stack.enter_context(nc.semaphore(g))
    grp_totals = {g: 16 * len(ops) for g, ops in B.grps.items()}
    with nc.Block() as block:
        @block.tensor
        def _(t):
            B.emit_stream("pe", t, sems)

        @block.scalar
        def _(s):
            B.emit_stream("act", s, sems)

        @block.vector
        def _(v):
            B.emit_stream("dve", v, sems)

        @block.gpsimd
        def _(g):
            B.emit_stream("pool", g, sems)

        @block.sync
        def _(sy):
            B.emit_stream("syn", sy, sems)
            for g, tot in grp_totals.items():
                sy.wait_ge(sems[g], tot)

    stack.close()
    return nc


def host_constants(inputs):
    wq = np.asarray(inputs["L0_attn_Wq_d"], np.float32)
    bq = np.asarray(inputs["L0_attn_bq_d"], np.float32)
    wqk = np.asarray(inputs["L0_attn_W_qk"], np.float32)
    bqk = np.asarray(inputs["L0_attn_b_qk"], np.float32)
    wkv = np.asarray(inputs["L0_attn_Wkv_d"], np.float32)
    bkv = np.asarray(inputs["L0_attn_bkv_d"], np.float32)
    wvu = np.asarray(inputs["L0_attn_Wv_u"], np.float32)
    bvu = np.asarray(inputs["L0_attn_bv_u"], np.float32)
    wo = np.ascontiguousarray(np.asarray(inputs["L0_attn_Wo"], np.float32))
    bo = np.asarray(inputs["L0_attn_bo"], np.float32)
    wb = np.asarray(inputs["L0_ffn_W_base"], np.float32)
    bb = np.asarray(inputs["L0_ffn_b_base"], np.float32)
    enc0 = np.asarray(inputs["L0_ffn_enc0"], np.float32)
    enc1 = np.asarray(inputs["L0_ffn_enc1"], np.float32)

    d = {}
    d["wcat"] = np.ascontiguousarray(
        np.concatenate([wq, wkv, np.ones((D, 1), np.float32)], axis=1))
    d["b17_d"] = np.concatenate([bq, bkv, np.zeros(1, np.float32)])[:, None].copy()
    wqe = np.zeros((5, 16 * 32), np.float32)
    for h in range(32):
        for l in range(4):
            wqe[l, 16 * h:16 * h + 12] = wqk[:, 4 * h + l]
            wqe[l, 16 * h + 12] = bqk[4 * h + l]
        wqe[4, 16 * h + 13] = -2.0
    d["wqk_ext"] = wqe
    d["wqk_aug"] = np.ascontiguousarray(np.concatenate([wqk, bqk[None, :]], axis=0))
    e4 = np.zeros((128, 32), np.float32)
    e4t = np.zeros((32, 128), np.float32)
    for h in range(32):
        for l in range(4):
            e4[4 * h + l, h] = 1.0
            e4t[h, 4 * h + l] = 1.0
    d["e4_d"] = e4
    d["e4t_d"] = e4t
    d["ones_dr"] = np.ones((1, NSLOT * TOK), np.float32)
    d["ones_col"] = np.ones((128, 1), np.float32)
    d["ident_d"] = np.eye(16, dtype=np.float32)
    d["wvuT"] = np.ascontiguousarray(wvu.T)
    d["bvu_d"] = np.ascontiguousarray(bvu.reshape(32, 128).T)
    d["wo_d"] = wo
    d["bo_d"] = np.ascontiguousarray(bo.reshape(32, 128).T)
    d["wbT_d"] = np.ascontiguousarray(wb.T)
    d["encT_d"] = np.ascontiguousarray(np.concatenate([enc0.T, enc1.T], axis=1))
    d["bb_d"] = np.ascontiguousarray(bb.reshape(32, 128).T)
    d["co0"] = np.ascontiguousarray(np.asarray(inputs["L0_ffn_coefs0"], np.float32))
    d["co1"] = np.ascontiguousarray(np.asarray(inputs["L0_ffn_coefs1"], np.float32))
    d["mk0"] = np.ascontiguousarray(np.asarray(inputs["L0_ffn_mask0"]).astype(np.uint8))
    d["mk1"] = np.ascontiguousarray(np.asarray(inputs["L0_ffn_mask1"]).astype(np.uint8))

    n = D
    j = np.arange(n, dtype=np.float64)
    M = np.empty((n, n), np.float32)
    M[0] = np.float32(1.0 / math.sqrt(n))
    f = np.arange(1, n // 2, dtype=np.float64)
    ang = (2.0 * np.pi / n) * np.outer(f, j)
    M[1:n - 1:2] = (math.sqrt(2.0 / n) * np.cos(ang)).astype(np.float32)
    M[2:n - 1:2] = (math.sqrt(2.0 / n) * np.sin(ang)).astype(np.float32)
    M[n - 1] = (((-1.0) ** j) / math.sqrt(n)).astype(np.float32)
    d["mmat"] = M
    return d


def kernel(**inputs):
    x = np.asarray(inputs["L0_attn_x"], np.float32)
    const = host_constants(inputs)
    nc = build_program()
    in_maps = []
    for core in range(8):
        b, r = core // 4, core % 4
        order = np.concatenate([np.arange(r * TOK, (r + 1) * TOK),
                                np.arange(0, r * TOK),
                                np.arange((r + 1) * TOK, SB)])
        xTc = np.ascontiguousarray(x[b][order].T)
        m = dict(const)
        m["xT"] = xTc
        in_maps.append(m)
    res = run_bass_kernel_spmd(nc, in_maps, core_ids=list(range(8)))
    outf = np.empty((2, SB, D), np.float32)
    for core in range(8):
        b, r = core // 4, core % 4
        outf[b, r * TOK:(r + 1) * TOK, :] = res.results[core]["out"].T
    return outf


# version 04
# speedup vs baseline: 1.4069x; 1.4069x over previous
import contextlib
import math
import numpy as np
import ml_dtypes
import concourse.bass as bass
import concourse.mybir as mybir
from concourse.bass_utils import run_bass_kernel_spmd

F32 = mybir.dt.float32
BF16 = mybir.dt.bfloat16
U8 = mybir.dt.uint8
AF = mybir.ActivationFunctionType
ALU = mybir.AluOpType

D = 4096
SB = 4096
TOK = 1024
NCH = 32
NKT = 32
NBLK = 8
NSLOT = 8
NQ = 2
EPS = 1e-5
CSHIFT = 8.0

DEBUG = False
TRUNC = 5


class Op:
    __slots__ = ("stream", "idx", "fn", "deps", "inc", "num", "grp", "gseq")

    def __init__(self, stream, idx, fn):
        self.stream = stream
        self.idx = idx
        self.fn = fn
        self.deps = []
        self.inc = False
        self.num = None
        self.grp = None
        self.gseq = 0


class Builder:
    STREAMS = ("pe", "act", "dve", "pool", "syn")

    def __init__(self):
        self.ops = {s: [] for s in self.STREAMS}
        self.last_write = {}
        self.readers = {}
        self.grps = {}
        self.gseq = 0

    def add(self, stream, fn, reads=(), writes=(), self_dep=False):
        op = Op(stream, len(self.ops[stream]), fn)
        self.gseq += 1
        op.gseq = self.gseq
        deps = []
        for b in reads:
            w = self.last_write.get(b)
            if w is not None:
                deps.append(w)
        for b in writes:
            w = self.last_write.get(b)
            if w is not None:
                deps.append(w)
            deps.extend(self.readers.get(b, ()))
        if stream == "pe" and not self_dep:
            op.deps = [d for d in deps if d.stream != stream]
        else:
            op.deps = deps
        self.ops[stream].append(op)
        for b in writes:
            self.last_write[b] = op
            self.readers[b] = []
        for b in reads:
            self.readers.setdefault(b, []).append(op)
        return op

    def pe(self, fn, reads=(), writes=()):
        return self.add("pe", fn, reads, writes)

    def act(self, fn, reads=(), writes=(), self_dep=False):
        return self.add("act", fn, reads, writes, self_dep=self_dep)

    def dve(self, fn, reads=(), writes=(), self_dep=False):
        return self.add("dve", fn, reads, writes, self_dep=self_dep)

    def dma(self, q, out_ap, in_ap, grp=None, reads=(), writes=(), **kw):
        assert grp is not None, "dma needs a sem group"
        def fn(eng):
            return eng.dma_start(out_ap, in_ap, **kw)
        op = self.add(q, fn, reads, writes)
        op.grp = grp
        self.grps.setdefault(grp, []).append(op)
        return op

    def finalize(self):
        for g, ops in self.grps.items():
            for i, op in enumerate(ops):
                op.num = 16 * (i + 1)
                op.inc = True
        for s in self.STREAMS:
            for op in self.ops[s]:
                merged = {}
                for d in op.deps:
                    key = d.grp if d.grp is not None else d.stream
                    cur = merged.get(key)
                    if cur is None:
                        merged[key] = d
                    elif d.grp is not None:
                        if d.num > cur.num:
                            merged[key] = d
                    elif d.idx > cur.idx:
                        merged[key] = d
                op.deps = list(merged.values())
                for d in op.deps:
                    d.inc = True
        for s in ("pe", "act", "dve"):
            n = 0
            for op in self.ops[s]:
                if op.inc:
                    n += 1
                op.num = n

    def emit_stream(self, s, eng, sems):
        import bisect
        gseqs = {g: [o.gseq for o in ops] for g, ops in self.grps.items()}
        wm = {}
        for op in self.ops[s]:
            for d in op.deps:
                key = d.grp if d.grp is not None else d.stream
                if d.grp is not None:
                    k = bisect.bisect_left(gseqs[d.grp], op.gseq)
                    val = 16 * max(k, (d.num + 15) // 16)
                else:
                    val = d.num
                if val > wm.get(key, 0):
                    eng.wait_ge(sems[key], val)
                    wm[key] = val
            inst = op.fn(eng)
            if op.inc:
                key = op.grp if op.grp is not None else s
                inst.then_inc(sems[key], 16 if op.grp is not None else 1)


def build_program():
    nc = bass.Bass("TRN2", target_bir_lowering=False)
    B = Builder()
    stack = contextlib.ExitStack()

    def din(name, shape, dtype=F32):
        return nc.dram_tensor(name, list(shape), dtype, kind="ExternalInput").ap()

    xT = din("xT", (D, SB))
    wcat = din("wcat", (D, 17))
    b17_d = din("b17_d", (17, 1))
    wqk_ext = din("wqk_ext", (5, 16 * 32))
    wqk_aug = din("wqk_aug", (13, 128))
    e4_d = din("e4_d", (128, 32))
    e4t_d = din("e4t_d", (32, 128))
    ones_dr = din("ones_dr", (1, NSLOT * TOK))
    ones_col = din("ones_col", (128, 1))
    ident_d = din("ident_d", (16, 16))
    wvuT = din("wvuT", (D, 4))
    bvu_d = din("bvu_d", (128, 32))
    wo_d = din("wo_d", (D, D))
    bo_d = din("bo_d", (128, 32))
    wbT_d = din("wbT_d", (D, D))
    encT_d = din("encT_d", (D, 40))
    bb_d = din("bb_d", (128, 32))
    co0 = din("co0", (3, D, 32))
    co1 = din("co1", (3, D, 8))
    mk0 = din("mk0", (3, D, 32), U8)
    mk1 = din("mk1", (3, D, 8), U8)
    mmat = din("mmat", (D, D), BF16)

    dbg = "ExternalOutput" if DEBUG else "Internal"
    out = nc.dram_tensor("out", [D, TOK], F32, kind="ExternalOutput").ap()
    c_stage = nc.dram_tensor("c_stage", [32, TOK], F32, kind=dbg).ap()
    ctx_stage = nc.dram_tensor("ctx_stage", [NQ * NSLOT * 20, 512], F32, kind=dbg).ap()
    x1_spill = nc.dram_tensor("x1_spill", [D, TOK], F32, kind=dbg).ap()
    ffn_spill = nc.dram_tensor("ffn_spill", [D, TOK], BF16, kind=dbg).ap()
    attn_dbg = nc.dram_tensor("attn_dbg", [D, TOK], F32, kind=dbg).ap()
    attn_sp = nc.dram_tensor("attn_sp", [D, TOK], BF16, kind="Internal").ap()
    cqkv_dbg = nc.dram_tensor("cqkv_dbg", [17, SB], F32, kind=dbg).ap()
    q_dbg = nc.dram_tensor("q_dbg", [128, TOK], F32, kind=dbg).ap()
    a_dbg = nc.dram_tensor("a_dbg", [128, D], F32, kind=dbg).ap()
    bc_dbg = nc.dram_tensor("bc_dbg", [128, 32], F32, kind=dbg).ap()
    ctxn_dbg = nc.dram_tensor("ctxn_dbg", [128, TOK], F32, kind=dbg).ap()
    h_dbg = nc.dram_tensor("h_dbg", [40, D], F32, kind=dbg).ap()
    u_dbg = nc.dram_tensor("u_dbg", [40, TOK], F32, kind=dbg).ap()
    g_dbg = nc.dram_tensor("g_dbg", [128, NCH * 40], F32, kind=dbg).ap()

    def sbuf(name, shape, dtype):
        return stack.enter_context(nc.sbuf_tensor(name, list(shape), dtype))

    def psum(name, shape, dtype=F32):
        return stack.enter_context(nc.psum_tensor(name, list(shape), dtype))

    proB = sbuf("proB", (128, 4096), F32)
    proB_bf = proB.bitcast(BF16)
    xblk = [proB_bf[:, 0:2048], proB_bf[:, 2048:4096]]
    xsq_t = sbuf("xsq", (128, 2048), BF16)
    xsq = [xsq_t, xsq_t]
    wcat_sb = sbuf("wcat_sb", (128, NCH * 17), BF16)
    b17_sb = sbuf("b17_sb", (17, 1), F32)
    sneg = sbuf("sneg", (17, 1), F32)
    blk_f = sbuf("blk_f", (17, 512), F32)
    blk_fix = sbuf("blk_fix", (17, 512), F32)
    rowpool = sbuf("rowpool", (1, 4608), F32)
    ones17 = sbuf("ones17", (1, 32), F32)
    cqb = proB_bf[:, 4096:5120]
    qsb = proB_bf[:, 5120:6144]
    c_sb = proB[:, 3072:4096]
    phA = sbuf("phA", (128, NKT * 128 + 3 * 2048 + 2 * TOK), BF16)
    kh2 = phA[:, 0:NKT * 128]
    cqm2 = [phA[:, NKT * 128 + 3 * 2048:NKT * 128 + 3 * 2048 + TOK],
            phA[:, NKT * 128 + 3 * 2048 + TOK:]]
    ckvlat_t = sbuf("ckvlat", (128, SB), BF16)
    ckvlat = ckvlat_t
    ckv32 = ckvlat_t.bitcast(F32)
    ckv1 = sbuf("ckv1", (128, NKT * 8), BF16)
    expsc = phA[:, NKT * 128:NKT * 128 + 3 * 2048]
    ctx_sp = sbuf("ctx_sp", (128, 512), F32)
    ctx_u = sbuf("ctx_u", (128, 512), F32)
    dnm = sbuf("dnm", (32, 512), F32)
    dtmp = sbuf("dtmp", (32, 512), F32)
    dtmp2 = sbuf("dtmp2", (32, 512), F32)
    ctx_n = sbuf("ctx_n", (128, 512), BF16)
    a_bf = sbuf("a_bf", (128, D), BF16)
    bc_sb = sbuf("bc_sb", (128, 32), F32)
    bo_sb = sbuf("bo_sb", (128, 32), F32)
    bb_sb = sbuf("bb_sb", (128, 32), F32)
    attn_c = sbuf("attn_c", (128, 512), BF16)
    attn_r = sbuf("attn_r", (128, 512), BF16)
    sq_c = sbuf("sq_c", (128, 512), BF16)
    x1_bf = sbuf("x1_bf", (128, NCH * TOK), BF16)
    xf_c = [sbuf(f"xf_c{i}", (128, 512), F32) for i in range(2)]
    x1f_c = sbuf("x1f_c", (128, 512), F32)
    hcat = sbuf("hcat", (40, D), BF16)
    g0all = sbuf("g0all", (128, NCH * 32), BF16)
    g1all = sbuf("g1all", (128, NCH * 8), BF16)
    u_sb = sbuf("u_sb", (40, TOK), BF16)
    wstr = [sbuf(f"wstr{i}", (128, 4096), BF16) for i in range(2)]
    wvall = wstr[0]
    x1f = x1_bf.bitcast(F32)
    x1u8 = x1_bf.bitcast(U8)
    cfw = x1f[:, 0:1024]
    acc0 = x1f[:, 1024:2048]
    gtmpw = x1f[:, 2048:3072]
    mkwf = x1_bf[:, 6144:7168]
    mkw8 = x1u8[:, 14336:15360]
    bcrow = x1f[0:1, 4096:8192]
    wos2 = [phA[:, 0:2048], phA[:, 2048:4096]]
    ms2 = [phA[:, 4096:6144], phA[:, 6144:8192]]
    enc_sb = proB_bf[:, 4096:5376]
    wqk_sb = sbuf("wqk_sb", (5, 16 * 32), BF16)
    wqka_sb = sbuf("wqka_sb", (13, 128), BF16)
    e4_sb = sbuf("e4_sb", (128, 32), BF16)
    e4t_sb = sbuf("e4t_sb", (32, 128), F32)
    ident_sb = sbuf("ident_sb", (16, 16), F32)
    ones_bf = sbuf("ones_bf", (128, 1), BF16)
    ones_f32 = sbuf("ones_f32", (128, 1), F32)
    onesr_f32 = sbuf("onesr_f32", (1, 128), F32)
    bvu_sb = sbuf("bvu_sb", (128, NCH), BF16)
    rb_sb = proB[:, 3072:3584]
    mb_sb = proB[:, 3584:4096]
    rb3 = proB[:, 0:1024]
    mb3 = proB[:, 1024:2048]
    a32 = a_bf.bitcast(F32)
    rb4 = a32[:, 0:512]
    mb4 = a32[:, 512:1024]
    senc_sb = sbuf("senc_sb", (128, 1), F32)
    ffn_c = ckvlat_t[:, 0:512]
    ffn_r = ckvlat_t[:, 512:1024]
    out_c = ckv32[:, 512:1024]
    tt1 = ckv32[:, 1024:1536]
    tt2 = ckv32[:, 1536:2048]

    def slot(i, j=0):
        return rowpool[0:1, 512 * i:512 * (i + 1)]

    ring = psum("ring", (128, 3072))
    pctx = psum("pctx", (128, 512))
    paux = psum("paux", (128, 512))
    rbank = [ring[:, 512 * i:512 * (i + 1)] for i in range(6)]
    RB = [("rng", i) for i in range(6)]
    PCTX = ("pctx",)
    PAUX = ("paux",)



    for c in range(NCH):
        B.dma("pool", wcat_sb[:, 17 * c:17 * (c + 1)], wcat[128 * c:128 * (c + 1), :],
              grp="cpool", writes=[("wcat_sb",)])
    B.dma("pool", wqk_sb[:], wqk_ext[:], grp="cpool", writes=[("wqk_sb",)])
    B.dma("pool", wqka_sb[:], wqk_aug[:], grp="cpool", writes=[("wqka_sb",)])
    B.dma("pool", e4_sb[:], e4_d[:], grp="cpool", writes=[("e4_sb",)])
    B.dma("syn", e4t_sb[:], e4t_d[:], grp="csyn", writes=[("e4t_sb",)])
    B.dma("syn", b17_sb[:], b17_d[:], grp="csyn", writes=[("b17_sb",)])
    B.dma("syn", ident_sb[:], ident_d[:], grp="csyn", writes=[("ident_sb",)])
    B.dma("syn", ones17[:], ones_dr[0:1, 0:32], grp="csyn", writes=[("ones17",)])
    B.dma("pool", ones_bf[:], ones_col[:], grp="cpool", writes=[("ones_bf",)])
    B.dma("syn", ones_f32[:], ones_col[:], grp="csyn", writes=[("ones_f32",)])
    B.dma("syn", onesr_f32[:], ones_dr[0:1, 0:128], grp="csyn", writes=[("onesr_f32",)])
    B.dma("syn", bo_sb[:], bo_d[:], grp="csyn", writes=[("bo_sb",)])
    B.dma("syn", bb_sb[:], bb_d[:], grp="csyn", writes=[("bb_sb",)])
    B.dma("pool", bvu_sb[:], bvu_d[:], grp="cpool", writes=[("bvu_sb",)])
    B.dve(lambda v: v.memset(wvall[:], 0.0), writes=[("wstr", 0)])
    for c in range(NCH):
        B.dma("pool", wvall[:, 128 * c + 4 * c:128 * c + 4 * c + 4],
              wvuT[128 * c:128 * (c + 1), :], grp="cpool", writes=[("wstr", 0)])
    B.dve(lambda v: v.memset(ckvlat[:], 1.0), writes=[("ckvlat",)])
    B.dve(lambda v: v.memset(ckv1[:], 1.0), writes=[("ckv1",)])
    B.dve(lambda v: v.memset(cqm2[0][:], 0.0), writes=[("cqm", 0)])
    B.dve(lambda v: v.memset(cqm2[1][:], 0.0), writes=[("cqm", 1)])
    B.dve(lambda v: v.memset(cqb[:], 0.0), writes=[("cqb",)])
    B.dma("pool", cqb[12:13, :], ones_dr[0:1, 0:TOK], grp="cpool", writes=[("cqb",)])
    for bf_ in range(2):
        for g in range(4):
            B.dma("pool", cqm2[bf_][32 * g + 12:32 * g + 13, :],
                  ones_dr[0:1, 0:TOK], grp="cpool", writes=[("cqm", bf_)])

    for half, (co_, mk_, gall, wdt) in enumerate(
            [(co0, mk0, g0all, 32), (co1, mk1, g1all, 8)]):
        W = NCH * wdt
        for e in range(3):
            B.dma("syn", cfw[:, 0:W].rearrange("p (cc w) -> p cc w", w=wdt),
                  co_[e, :, :].rearrange("(cc p) w -> p cc w", p=128),
                  grp=f"gld{half}", writes=[("cfe",)])
            B.dma("syn", mkw8[:, 0:W].rearrange("p (cc w) -> p cc w", w=wdt),
                  mk_[e, :, :].rearrange("(cc p) w -> p cc w", p=128),
                  grp=f"gld{half}", writes=[("mku",)])
            B.dve(lambda v, W=W: v.tensor_copy(mkwf[:, 0:W], mkw8[:, 0:W]),
                  reads=[("mku",)], writes=[("mkf",)], self_dep=True)
            if e == 0:
                B.dve(lambda v, W=W: v.tensor_mul(acc0[:, 0:W], cfw[:, 0:W],
                                                  mkwf[:, 0:W]),
                      reads=[("cfe",), ("mkf",)], writes=[("ga",)], self_dep=True)
            else:
                B.dve(lambda v, W=W: v.tensor_mul(gtmpw[:, 0:W], cfw[:, 0:W],
                                                  mkwf[:, 0:W]),
                      reads=[("cfe",), ("mkf",)], writes=[("gtmp",)], self_dep=True)
                B.dve(lambda v, W=W: v.tensor_add(acc0[:, 0:W], acc0[:, 0:W],
                                                  gtmpw[:, 0:W]),
                      reads=[("gtmp",)], writes=[("ga",)], self_dep=True)
        B.dve(lambda v, gl=gall, W=W: v.tensor_copy(gl[:, :], acc0[:, 0:W]),
              reads=[("ga",)], writes=[("gall", half)], self_dep=True)

    for c in range(NCH):
        B.pe(lambda t, c=c: t.matmul(paux[0:17, 0:1], wcat_sb[:, 17 * c:17 * (c + 1)],
                                     ones_bf[:], start=(c == 0), stop=(c == NCH - 1)),
             reads=[("wcat_sb",), ("ones_bf",)], writes=[PAUX])
    B.dve(lambda v: v.tensor_scalar_mul(sneg[:], paux[0:17, 0:1], -1.0),
          reads=[PAUX], writes=[("sneg",)], self_dep=True)

    for blk in range(NBLK):
        for q4 in range(8):
            par = q4 % 2
            xb, xq = xblk[par], xsq[par]
            B.dma("pool", xb[:, :],
                  xT.rearrange("(cc p) t -> p cc t", p=128)
                  [:, 4 * q4:4 * q4 + 4, 512 * blk:512 * (blk + 1)],
                  grp=f"xb{par}", writes=[("xblk", par)])
            B.dve(lambda v, xb=xb, xq=xq: v.tensor_mul(xq[:], xb[:], xb[:]),
                  reads=[("xblk", par)], writes=[("xsq",)])
            for i in range(4):
                c = 4 * q4 + i
                B.pe(lambda t, xb=xb, i=i, c=c:
                     t.matmul(paux[0:17, :], wcat_sb[:, 17 * c:17 * (c + 1)],
                              xb[:, 512 * i:512 * (i + 1)],
                              start=(c == 0), stop=(c == NCH - 1)),
                     reads=[("xblk", par), ("wcat_sb",)], writes=[PAUX])
                B.pe(lambda t, xq=xq, i=i, c=c:
                     t.matmul(pctx[0:1, :], ones_bf[:],
                              xq[:, 512 * i:512 * (i + 1)],
                              start=(c == 0), stop=(c == NCH - 1),
                              skip_group_check=True),
                     reads=[("xsq",), ("ones_bf",)], writes=[PCTX])
                B.pe(lambda t, xb=xb, i=i, c=c:
                     t.matmul(pctx[32:33, :], ones_bf[:],
                              xb[:, 512 * i:512 * (i + 1)],
                              start=(c == 0), stop=(c == NCH - 1),
                              tile_position=(0, 32), skip_group_check=True),
                     reads=[("xblk", par), ("ones_bf",)], writes=[PCTX])
        B.dve(lambda v: v.tensor_copy(blk_f[0:17, :], paux[0:17, :]),
              reads=[PAUX], writes=[("blk_f",)])
        B.dve(lambda v: v.tensor_scalar_mul(slot(0), pctx[32:33, :], 1.0 / D),
              reads=[PCTX], writes=[("rp", 0)])
        B.dve(lambda v: v.tensor_scalar_mul(slot(1), pctx[0:1, :], 1.0 / D),
              reads=[PCTX], writes=[("rp", 1)])
        B.dve(lambda v: v.tensor_mul(slot(2), slot(0), slot(0)),
              reads=[("rp", 0)], writes=[("rp", 2)], self_dep=True)
        B.dve(lambda v: v.tensor_sub(slot(1), slot(1), slot(2)),
              reads=[("rp", 2), ("rp", 1)], writes=[("rp", 1)], self_dep=True)
        B.dve(lambda v: v.tensor_scalar_add(slot(1), slot(1), EPS),
              reads=[("rp", 1)], writes=[("rp", 1)], self_dep=True)
        B.act(lambda s: s.activation(slot(3), slot(1), AF.Ln),
              reads=[("rp", 1)], writes=[("rp", 3)])
        B.act(lambda s: s.activation(slot(3), slot(3), AF.Exp, scale=-0.5),
              reads=[("rp", 3)], writes=[("rp", 3)], self_dep=True)
        B.dve(lambda v: v.tensor_mul(slot(4), slot(0), slot(3)),
              reads=[("rp", 0), ("rp", 3)], writes=[("rp", 4)], self_dep=True)
        B.pe(lambda t: t.matmul(rbank[0][0:17, :], ones17[0:1, 0:17], slot(3),
                                start=True, stop=True),
             reads=[("rp", 3), ("ones17",)], writes=[RB[0]])
        B.pe(lambda t: t.matmul(rbank[1][0:17, :], ones17[0:1, 0:17], slot(4),
                                start=True, stop=True),
             reads=[("rp", 4), ("ones17",)], writes=[RB[1]])
        B.dve(lambda v: v.tensor_mul(blk_fix[:], blk_f[0:17, :], rbank[0][0:17, :]),
              reads=[("blk_f",), RB[0]], writes=[("blk_fix",)])
        B.dve(lambda v: v.scalar_tensor_tensor(blk_fix[:], rbank[1][0:17, :], sneg[:],
                                               blk_fix[:], ALU.mult, ALU.add),
              reads=[RB[1], ("sneg",), ("blk_fix",)], writes=[("blk_fix",)],
              self_dep=True)
        B.dve(lambda v: v.tensor_scalar_add(blk_fix[:], blk_fix[:], b17_sb[:]),
              reads=[("b17_sb",), ("blk_fix",)], writes=[("blk_fix",)])
        B.dma("pool", ckvlat[0:4, 512 * blk:512 * (blk + 1)], blk_fix[12:16, :],
              grp="ckv", reads=[("blk_fix",)], writes=[("ckvlat", blk)])
        for i in range(4):
            kt = 4 * blk + i
            B.pe(lambda t, i=i: t.transpose(rbank[2][:, 0:16],
                                            blk_fix[0:16, 128 * i:128 * (i + 1)],
                                            ident_sb[:]),
                 reads=[("blk_fix",), ("ident_sb",)], writes=[RB[2]])
            B.dve(lambda v, kt=kt: v.tensor_copy(ckv1[:, 8 * kt:8 * kt + 4],
                                                 rbank[2][:, 12:16]),
                  reads=[RB[2]], writes=[("ckv1", kt)])
        if DEBUG:
            B.dma("syn", cqkv_dbg[:, 512 * blk:512 * (blk + 1)], blk_fix[:],
                  grp="dbg", reads=[("blk_fix",)], writes=[("cqkv_dbg",)])
        if blk < 2:
            B.dve(lambda v, blk=blk:
                  v.tensor_copy(cqb[0:12, 512 * blk:512 * (blk + 1)], blk_fix[0:12, :]),
                  reads=[("blk_fix",)], writes=[("cqb", blk)])
            for g in range(4):
                for bf_ in range(2):
                    B.dma("pool", cqm2[bf_][32 * g:32 * g + 12,
                                            512 * blk:512 * (blk + 1)],
                          blk_fix[0:12, :],
                          grp="cqc", reads=[("blk_fix",)], writes=[("cqm", bf_)])

    for h2 in range(2):
        B.pe(lambda t, h2=h2: t.matmul(rbank[0][:, :], wqka_sb[:],
                                       cqb[0:13, 512 * h2:512 * (h2 + 1)],
                                       start=True, stop=True),
             reads=[("cqb", 0), ("cqb", 1), ("wqka_sb",)], writes=[RB[0]])
        B.dve(lambda v, h2=h2: v.tensor_copy(qsb[:, 512 * h2:512 * (h2 + 1)],
                                             rbank[0][:, :]),
              reads=[RB[0]], writes=[("qsb", h2)])
        if DEBUG:
            B.dma("pool", q_dbg[:, 512 * h2:512 * (h2 + 1)],
                  qsb[:, 512 * h2:512 * (h2 + 1)],
                  grp="dbg", reads=[("qsb", h2)], writes=[("q_dbg",)])
        B.dve(lambda v, h2=h2: v.tensor_mul(sq_c[:],
                                            qsb[:, 512 * h2:512 * (h2 + 1)],
                                            qsb[:, 512 * h2:512 * (h2 + 1)]),
              reads=[("qsb", h2)], writes=[("sq_c",)])
        B.pe(lambda t, h2=h2: t.matmul(rbank[1][0:32, :], e4_sb[:], sq_c[:],
                                       start=True, stop=True),
             reads=[("sq_c",), ("e4_sb",)], writes=[RB[1]])
        B.dve(lambda v, h2=h2: v.tensor_scalar(c_sb[0:32, 512 * h2:512 * (h2 + 1)],
                                               rbank[1][0:32, :], 0.5, CSHIFT,
                                               ALU.mult, ALU.add),
              reads=[RB[1]], writes=[("c_sb", h2)])
    B.dma("syn", c_stage[:], c_sb[0:32, :], grp="cst",
          reads=[("c_sb", 0), ("c_sb", 1)], writes=[("c_stage",)])


    if TRUNC < 2:
        return finish(nc, B, stack)
    for seg in range(8):
        for cq in range(8):
            par = cq % 2
            B.dma("pool", wos2[par][:, :].rearrange("p (cc o) -> p cc o", o=512),
                  wo_d[512 * cq:512 * (cq + 1), 512 * seg:512 * (seg + 1)]
                  .rearrange("(cc p) o -> p cc o", p=128),
                  grp=f"wos{par}", writes=[("wos", par)])
            B.dma("syn", ms2[par][:, :].rearrange("p (cc o) -> p cc o", o=512),
                  mmat[512 * cq:512 * (cq + 1), 512 * seg:512 * (seg + 1)]
                  .rearrange("(cc p) o -> p cc o", p=128),
                  grp=f"ms{par}", writes=[("ms", par)])
            for i in range(4):
                c = 4 * cq + i
                B.pe(lambda t, c=c, par=par, i=i: t.matmul(
                    rbank[3][:, :], wvall[:, 128 * c:128 * (c + 1)],
                    wos2[par][:, 512 * i:512 * (i + 1)],
                    start=(c == 0), stop=(c == NCH - 1)),
                    reads=[("wos", par), ("wstr", 0)], writes=[RB[3]])
                B.pe(lambda t, c=c, par=par, i=i: t.matmul(
                    paux[0:1, :], bvu_sb[:, c:c + 1],
                    wos2[par][:, 512 * i:512 * (i + 1)],
                    start=(c == 0), stop=(c == NCH - 1), skip_group_check=True),
                    reads=[("wos", par), ("bvu_sb",)], writes=[PAUX])
            for i in range(4):
                c = 4 * cq + i
                B.pe(lambda t, c=c, par=par, i=i: t.matmul(
                    pctx[0:32, :], g0all[:, 32 * c:32 * (c + 1)],
                    ms2[par][:, 512 * i:512 * (i + 1)],
                    start=(c == 0), stop=(c == NCH - 1), skip_group_check=True),
                    reads=[("ms", par), ("gall", 0)], writes=[PCTX])
                B.pe(lambda t, c=c, par=par, i=i: t.matmul(
                    pctx[32:40, :], g1all[:, 8 * c:8 * (c + 1)],
                    ms2[par][:, 512 * i:512 * (i + 1)],
                    start=(c == 0), stop=(c == NCH - 1), tile_position=(0, 32),
                    skip_group_check=True),
                    reads=[("ms", par), ("gall", 1)], writes=[PCTX])
        B.dve(lambda v, og=seg: v.tensor_copy(a_bf[:, 512 * og:512 * (og + 1)],
                                              rbank[3][:, :]),
              reads=[RB[3]], writes=[("a_bf", seg)])
        B.dve(lambda v, og=seg: v.tensor_copy(bcrow[0:1, 512 * og:512 * (og + 1)],
                                              paux[0:1, :]),
              reads=[PAUX], writes=[("bcrow",)])
        B.dve(lambda v, jg=seg: v.tensor_scalar_mul(hcat[0:32, 512 * jg:512 * (jg + 1)],
                                                    pctx[0:32, :], 0.5),
              reads=[PCTX], writes=[("hcat",)])
        B.dve(lambda v, jg=seg: v.tensor_scalar_mul(hcat[32:40, 512 * jg:512 * (jg + 1)],
                                                    pctx[32:40, :], 0.5),
              reads=[PCTX], writes=[("hcat",)])
    for k in range(32):
        B.pe(lambda t, k=k: t.transpose(paux[:, k:k + 1],
                                        bcrow[0:1, 128 * k:128 * (k + 1)],
                                        ones_f32[0:1, 0:1]),
             reads=[("bcrow",), ("ones_f32",)], writes=[PAUX])
    B.dve(lambda v: v.tensor_copy(bc_sb[:, :], paux[:, 0:32]),
          reads=[PAUX], writes=[("bc_sb",)])
    B.dve(lambda v: v.tensor_add(bc_sb[:], bc_sb[:], bo_sb[:]),
          reads=[("bo_sb",), ("bc_sb",)], writes=[("bc_sb",)], self_dep=True)

    if DEBUG:
        B.dma("pool", a_dbg[:], a_bf[:], grp="dbg", reads=[("a_bf", og) for og in range(8)],
              writes=[("a_dbg",)])
        B.dma("syn", bc_dbg[:], bc_sb[:], grp="dbg", reads=[("bc_sb",)], writes=[("bc_dbg",)])

    if DEBUG:
        B.dma("pool", h_dbg[:], hcat[0:40, :], grp="dbg", reads=[("hcat",)], writes=[("h_dbg",)])
        B.dma("pool", g_dbg[:, 0:NCH * 32], g0all[:], grp="dbg", reads=[("gall", 0)],
              writes=[("g_dbg",)])
        B.dma("pool", g_dbg[:, NCH * 32:NCH * 32 + NCH * 8], g1all[:],
              grp="dbg", reads=[("gall", 1)], writes=[("g_dbg",)])
    if TRUNC < 3:
        return finish(nc, B, stack)
    B.dve(lambda v: v.memset(paux[:, :], 0.0), writes=[PAUX])
    B.dve(lambda v: v.memset(pctx[:, :], 0.0), writes=[PCTX])
    for s_ in range(NSLOT):
        cb = s_ % 2
        for g in range(4):
            B.dma("pool", cqm2[cb][32 * g + 13:32 * g + 14, :],
                  c_stage[4 * s_ + g:4 * s_ + g + 1, :],
                  grp=f"cq{cb}", reads=[("c_stage",)], writes=[("cqm", cb)])
        for ktg in range(8):
            for g in range(4):
                B.pe(lambda t, g=g, ktg=ktg, s_=s_: t.matmul(
                    paux[32 * g:32 * g + 16, :],
                    wqk_sb[0:5, 16 * (4 * s_ + g):16 * (4 * s_ + g) + 16],
                    ckvlat[0:5, 512 * ktg:512 * (ktg + 1)],
                    start=True, stop=True, tile_position=(0, 32 * g)),
                    reads=[("wqk_sb",), ("ckvlat", ktg)],
                    writes=[PAUX])
            B.dve(lambda v, ktg=ktg: v.tensor_scalar_mul(
                kh2[:, 512 * ktg:512 * (ktg + 1)], paux[:, :], 0.5),
                reads=[PAUX], writes=[("kh2", ktg)])
        for q in range(NQ):
            prev = None
            for kt in range(NKT):
                pa = (2 * kt) % 3
                pb = (2 * kt + 1) % 3
                slot3 = kt % 3
                for g in range(4):
                    pair = pa if g < 2 else pb
                    bank = 2 * pair + (g & 1)
                    B.pe(lambda t, g=g, kt=kt, cb=cb, q=q, bank=bank: t.matmul(
                        rbank[bank][:, :],
                        kh2[32 * g:32 * g + 16, 128 * kt:128 * (kt + 1)],
                        cqm2[cb][32 * g:32 * g + 16, 512 * q:512 * (q + 1)],
                        start=True, stop=True, tile_position=(32 * g, 0)),
                        reads=[("kh2", kt // 4), ("cqm", cb)], writes=[RB[bank]])
                rv = ring.rearrange("p (c n) -> p c n", n=1024)
                dst = expsc[:, 2048 * slot3:2048 * (slot3 + 1)]
                if pb == pa + 1:
                    src = ring[:, 1024 * pa:1024 * (pb + 1)]
                    dst2 = dst
                else:
                    src = rv[:, ::-2, :]
                    dst2 = dst.rearrange("p (c n) -> p c n", n=1024)
                B.act(lambda s, src=src, dst2=dst2: s.activation(dst2, src, AF.Exp),
                      reads=[RB[2 * pa], RB[2 * pa + 1], RB[2 * pb], RB[2 * pb + 1]],
                      writes=[("expsc", slot3)])
                if prev is not None:
                    pkt, pslot3 = prev
                    for g in range(4):
                        pair_off = 0 if g < 2 else 1024
                        off = 2048 * pslot3 + pair_off + 512 * (g & 1)
                        B.pe(lambda t, g=g, pkt=pkt, off=off: t.matmul(
                            pctx[32 * g:32 * g + 5, :],
                            ckv1[:, 8 * pkt:8 * pkt + 5],
                            expsc[:, off:off + 512],
                            start=(pkt == 0), stop=(pkt == NKT - 1),
                            tile_position=(0, 32 * g), skip_group_check=True),
                            reads=[("expsc", pslot3), ("ckv1", pkt)],
                            writes=[PCTX])
                prev = (kt, slot3)
            pkt, pslot3 = prev
            for g in range(4):
                pair_off = 0 if g < 2 else 1024
                off = 2048 * pslot3 + pair_off + 512 * (g & 1)
                B.pe(lambda t, g=g, pkt=pkt, off=off: t.matmul(
                    pctx[32 * g:32 * g + 5, :],
                    ckv1[:, 8 * pkt:8 * pkt + 5],
                    expsc[:, off:off + 512],
                    start=(pkt == 0), stop=(pkt == NKT - 1),
                    tile_position=(0, 32 * g), skip_group_check=True),
                    reads=[("expsc", pslot3), ("ckv1", pkt)], writes=[PCTX])
            B.dve(lambda v: v.tensor_copy(ctx_sp[:], pctx[:]),
                  reads=[PCTX], writes=[("ctx_sp",)])
            for g in range(4):
                B.dma("syn", ctx_stage[(q * NSLOT + s_) * 20 + 5 * g:
                                       (q * NSLOT + s_) * 20 + 5 * (g + 1), :],
                      ctx_sp[32 * g:32 * g + 5, :],
                      grp="ctw", reads=[("ctx_sp",)], writes=[("ctx_stage", q, s_)])

    def stat_finish(pbank, pkey, ri, mi):
        B.dve(lambda v: v.tensor_scalar_mul(slot(0), pbank[0:1, :], 1.0 / D),
              reads=[pkey], writes=[("rp", 0)])
        B.dve(lambda v: v.tensor_scalar_mul(slot(1), pbank[32:33, :], 1.0 / D),
              reads=[pkey], writes=[("rp", 1)])
        B.dve(lambda v: v.tensor_mul(slot(2), slot(0), slot(0)),
              reads=[("rp", 0)], writes=[("rp", 2)], self_dep=True)
        B.dve(lambda v: v.tensor_sub(slot(1), slot(1), slot(2)),
              reads=[("rp", 2), ("rp", 1)], writes=[("rp", 1)], self_dep=True)
        B.dve(lambda v: v.tensor_scalar_add(slot(1), slot(1), EPS),
              reads=[("rp", 1)], writes=[("rp", 1)], self_dep=True)
        B.act(lambda s: s.activation(slot(ri), slot(1), AF.Ln),
              reads=[("rp", 1)], writes=[("rp", ri)])
        B.act(lambda s: s.activation(slot(ri), slot(ri), AF.Exp, scale=-0.5),
              reads=[("rp", ri)], writes=[("rp", ri)], self_dep=True)
        B.dve(lambda v: v.tensor_mul(slot(mi), slot(0), slot(ri)),
              reads=[("rp", 0), ("rp", ri)], writes=[("rp", mi)], self_dep=True)

    if TRUNC < 4:
        return finish(nc, B, stack)
    B.dve(lambda v: v.engine_nop(),
          writes=[("ckvlat", b) for b in range(NBLK)] +
                 [("c_sb", 0), ("c_sb", 1), ("ckvlat",)])
    for q in range(NQ):
        for s_ in range(NSLOT):
            for g in range(4):
                base = (q * NSLOT + s_) * 20 + 5 * g
                B.dma("syn", ctx_u[16 * s_ + 4 * g:16 * s_ + 4 * (g + 1), :],
                      ctx_stage[base:base + 4, :],
                      grp="cxr", reads=[("ctx_stage", q, s_)], writes=[("ctx_u", q)])
                B.dma("syn", dnm[4 * s_ + g:4 * s_ + g + 1, :],
                      ctx_stage[base + 4:base + 5, :],
                      grp="cxr", reads=[("ctx_stage", q, s_)], writes=[("dnm", q)])
        B.dve(lambda v: v.reciprocal(dtmp[:], dnm[:]),
              reads=[("dnm", q)], writes=[("dtmp",)])
        B.dve(lambda v: v.tensor_mul(dtmp2[:], dnm[:], dtmp[:]),
              reads=[("dnm", q), ("dtmp",)], writes=[("dtmp2",)], self_dep=True)
        B.dve(lambda v: v.tensor_scalar(dtmp2[:], dtmp2[:], -1.0, 2.0, ALU.mult, ALU.add),
              reads=[("dtmp2",)], writes=[("dtmp2",)], self_dep=True)
        B.dve(lambda v: v.tensor_mul(dtmp[:], dtmp[:], dtmp2[:]),
              reads=[("dtmp",), ("dtmp2",)], writes=[("dtmp",)], self_dep=True)
        B.pe(lambda t: t.matmul(rbank[2][:, :], e4t_sb[:], dtmp[:],
                                start=True, stop=True),
             reads=[("dtmp",), ("e4t_sb",)], writes=[RB[2]])
        B.dve(lambda v: v.tensor_mul(ctx_n[:], ctx_u[:], rbank[2][:, :]),
              reads=[("ctx_u", q), RB[2]], writes=[("ctx_n", q)])
        if DEBUG:
            B.dma("pool", ctxn_dbg[:, 512 * q:512 * (q + 1)], ctx_n[:],
                  grp="dbg", reads=[("ctx_n", q)], writes=[("ctxn_dbg",)])
        for ot in range(NCH):
            pbk = ot % 2
            B.pe(lambda t, ot=ot, pbk=pbk: t.matmul(
                rbank[pbk][:, :], a_bf[:, 128 * ot:128 * (ot + 1)], ctx_n[:],
                start=True, stop=True),
                reads=[("ctx_n", q), ("a_bf", ot // 4)], writes=[RB[pbk]])
            B.dve(lambda v, ot=ot, pbk=pbk: v.tensor_scalar(
                attn_c[:], rbank[pbk][:, :],
                bc_sb[:, ot:ot + 1], None, ALU.add),
                reads=[RB[pbk], ("bc_sb",)], writes=[("attn_c",)])
            B.dve(lambda v: v.tensor_mul(sq_c[:], attn_c[:], attn_c[:]),
                  reads=[("attn_c",)], writes=[("sq_c",)])
            B.pe(lambda t, ot=ot: t.matmul(rbank[2][0:1, :], ones_bf[:], attn_c[:],
                                           start=(ot == 0), stop=(ot == NCH - 1),
                                           skip_group_check=True),
                 reads=[("attn_c",), ("ones_bf",)], writes=[RB[2]])
            B.pe(lambda t, ot=ot: t.matmul(rbank[2][32:33, :], ones_bf[:], sq_c[:],
                                           start=(ot == 0), stop=(ot == NCH - 1),
                                           tile_position=(0, 32),
                                           skip_group_check=True),
                 reads=[("sq_c",), ("ones_bf",)], writes=[RB[2]])
            B.dma("syn", attn_sp[128 * ot:128 * (ot + 1), 512 * q:512 * (q + 1)],
                  attn_c[:], grp="aw", reads=[("attn_c",)], writes=[("attn_spl", ot, q)])
            if DEBUG:
                B.dma("pool", attn_dbg[128 * ot:128 * (ot + 1), 512 * q:512 * (q + 1)],
                      attn_c[:], grp="dbg", reads=[("attn_c",)], writes=[("attn_dbg",)])
        stat_finish(rbank[2], RB[2], 3, 4)
        B.pe(lambda t: t.matmul(rbank[3][:, :], onesr_f32[:], slot(3),
                                start=True, stop=True),
             reads=[("rp", 3), ("onesr_f32",)], writes=[RB[3]])
        B.pe(lambda t: t.matmul(rbank[4][:, :], onesr_f32[:], slot(4),
                                start=True, stop=True),
             reads=[("rp", 4), ("onesr_f32",)], writes=[RB[4]])
        B.dve(lambda v: v.tensor_copy(rb_sb[:], rbank[3][:, :]),
              reads=[RB[3]], writes=[("rb_sb",)])
        B.dve(lambda v: v.tensor_copy(mb_sb[:], rbank[4][:, :]),
              reads=[RB[4]], writes=[("mb_sb",)])
        for c in range(NCH):
            par = c % 2
            B.dma("syn", xf_c[par][:, :],
                  xT[128 * c:128 * (c + 1), 512 * q:512 * (q + 1)],
                  grp=f"xf{par}", writes=[("xf_c", par)])
            B.dma("pool", attn_r[:, :],
                  attn_sp[128 * c:128 * (c + 1), 512 * q:512 * (q + 1)],
                  grp="ar", reads=[("attn_spl", c, q)], writes=[("attn_r",)])
            B.dve(lambda v: v.tensor_mul(tt1[:], attn_r[:], rb_sb[:]),
                  reads=[("attn_r",), ("rb_sb",)], writes=[("tt1",)])
            B.dve(lambda v: v.tensor_sub(tt2[:], tt1[:], mb_sb[:]),
                  reads=[("tt1",), ("mb_sb",)], writes=[("tt2",)])
            B.dve(lambda v, par=par: v.tensor_add(x1f_c[:], tt2[:], xf_c[par][:, :]),
                  reads=[("tt2",), ("xf_c", par)], writes=[("x1f_c",)])
            B.dve(lambda v, c=c, q=q: v.tensor_copy(
                x1_bf[:, TOK * c + 512 * q:TOK * c + 512 * (q + 1)], x1f_c[:]),
                reads=[("x1f_c",)], writes=[("x1_bf", c, q)])
            B.dma("syn", x1_spill[128 * c:128 * (c + 1), 512 * q:512 * (q + 1)],
                  x1f_c[:], grp="x1w", reads=[("x1f_c",)], writes=[("x1_spill", c, q)])

    if TRUNC < 5:
        return finish(nc, B, stack)
    B.dve(lambda v: v.engine_nop(),
          writes=[("xblk", 0, qq) for qq in range(8)] +
                 [("xblk", 1, qq) for qq in range(8)] +
                 [("cqb", 0), ("cqb", 1), ("qsb", 0), ("qsb", 1),
                  ("c_sb", 0), ("c_sb", 1)])
    for c in range(NCH):
        B.dma("pool", enc_sb[:, 40 * c:40 * (c + 1)], encT_d[128 * c:128 * (c + 1), :],
              grp="enc", writes=[("enc_sb",), ("cqb", 0), ("cqb", 1),
                                 ("qsb", 0), ("qsb", 1)])
    for q in range(NQ):
        for c in range(NCH):
            B.dve(lambda v, c=c, q=q: v.tensor_mul(
                sq_c[:], x1_bf[:, TOK * c + 512 * q:TOK * c + 512 * (q + 1)],
                x1_bf[:, TOK * c + 512 * q:TOK * c + 512 * (q + 1)]),
                reads=[("x1_bf", c, q)], writes=[("sq_c",)])
            B.pe(lambda t, c=c, q=q: t.matmul(
                rbank[2][0:1, :], ones_bf[:],
                x1_bf[:, TOK * c + 512 * q:TOK * c + 512 * (q + 1)],
                start=(c == 0), stop=(c == NCH - 1), skip_group_check=True),
                reads=[("x1_bf", c, q), ("ones_bf",)], writes=[RB[2]])
            B.pe(lambda t, c=c: t.matmul(rbank[2][32:33, :], ones_bf[:], sq_c[:],
                                         start=(c == 0), stop=(c == NCH - 1),
                                         tile_position=(0, 32),
                                         skip_group_check=True),
                 reads=[("sq_c",), ("ones_bf",)], writes=[RB[2]])
        stat_finish(rbank[2], RB[2], 3, 4)
        B.pe(lambda t: t.matmul(rbank[3][:, :], onesr_f32[:], slot(4),
                                start=True, stop=True),
             reads=[("rp", 4), ("onesr_f32",)], writes=[RB[3]])
        B.dve(lambda v, q=q: v.tensor_copy(mb3[:, 512 * q:512 * (q + 1)],
                                           rbank[3][:, :]),
              reads=[RB[3]], writes=[("mb3", q)])
        B.pe(lambda t: t.matmul(rbank[3][:, :], onesr_f32[:], slot(3),
                                start=True, stop=True),
             reads=[("rp", 3), ("onesr_f32",)], writes=[RB[3]])
        B.dve(lambda v, q=q: v.tensor_scalar_mul(rb3[:, 512 * q:512 * (q + 1)],
                                                 rbank[3][:, :], -1.0),
              reads=[RB[3]], writes=[("rb3", q)])

    for ot in range(NCH + 1):
        par = ot % 2
        if ot > 0:
            for sub in range(4):
                B.dma("pool", wstr[par][:, 1024 * sub:1024 * (sub + 1)],
                      wbT_d.rearrange("(cc p) o -> p cc o", p=128)
                      [:, 8 * sub:8 * sub + 8, 128 * (ot - 1):128 * ot],
                      grp=f"wb{par}", writes=[("wstr", par)])
        for q in range(NQ):
            ww = 40 if ot == 0 else 128
            wkey = ("enc_sb",) if ot == 0 else ("wstr", par)
            for c in range(NCH):
                if ot == 0:
                    lt = enc_sb[:, 40 * c:40 * (c + 1)]
                else:
                    lt = wstr[par].rearrange("p (cc o) -> p cc o", o=128)[:, c, :]
                if q == 0:
                    B.pe(lambda t, c=c, lt=lt, ww=ww: t.matmul(
                        paux[0:ww, 0:1], lt, ones_bf[:],
                        start=(c == 0), stop=(c == NCH - 1)),
                        reads=[wkey, ("ones_bf",)], writes=[PAUX])
                B.pe(lambda t, c=c, q=q, lt=lt, ww=ww, par=par: t.matmul(
                    rbank[par][0:ww, :], lt,
                    x1_bf[:, TOK * c + 512 * q:TOK * c + 512 * (q + 1)],
                    start=(c == 0), stop=(c == NCH - 1)),
                    reads=[wkey, ("x1_bf", c, q)], writes=[RB[par]])
            if q == 0:
                B.dve(lambda v, ww=ww: v.tensor_copy(senc_sb[0:ww, :], paux[0:ww, 0:1]),
                      reads=[PAUX], writes=[("senc_sb",)], self_dep=True)
            if ot == 0:
                B.dve(lambda v, q=q, par=par: v.scalar_tensor_tensor(
                    tt1[0:40, :], mb3[0:40, 512 * q:512 * (q + 1)], senc_sb[0:40, :],
                    rbank[par][0:40, :], ALU.mult, ALU.subtract),
                    reads=[("mb3", q), ("senc_sb",), RB[par]], writes=[("tt1",)],
                    self_dep=True)
                B.dve(lambda v, q=q: v.tensor_mul(tt2[0:40, :], tt1[0:40, :],
                                                  rb3[0:40, 512 * q:512 * (q + 1)]),
                      reads=[("tt1",), ("rb3", q)], writes=[("tt2",)])
                B.dve(lambda v, q=q: v.tensor_copy(u_sb[0:40, 512 * q:512 * (q + 1)],
                                                   tt2[0:40, :]),
                      reads=[("tt2",)], writes=[("u_sb", q)])
                if DEBUG:
                    B.dma("pool", u_dbg[:, 512 * q:512 * (q + 1)],
                          u_sb[0:40, 512 * q:512 * (q + 1)],
                          grp="dbg", reads=[("u_sb", q)], writes=[("u_dbg",)])
            else:
                ub = 4 + (1 - par)
                B.pe(lambda t, ot=ot, q=q, ub=ub: t.matmul(
                    rbank[ub][:, :], hcat[0:40, 128 * (ot - 1):128 * ot],
                    u_sb[0:40, 512 * q:512 * (q + 1)], start=True, stop=True),
                    reads=[("u_sb", q), ("hcat",)], writes=[RB[ub]])
                B.dve(lambda v, q=q, par=par: v.scalar_tensor_tensor(
                    tt1[:], mb3[:, 512 * q:512 * (q + 1)], senc_sb[:],
                    rbank[par][:, :], ALU.mult, ALU.subtract),
                    reads=[("mb3", q), ("senc_sb",), RB[par]], writes=[("tt1",)],
                    self_dep=True)
                B.dve(lambda v, q=q: v.tensor_mul(tt2[:], tt1[:],
                                                  rb3[:, 512 * q:512 * (q + 1)]),
                      reads=[("tt1",), ("rb3", q)], writes=[("tt2",)])
                B.dve(lambda v, ot=ot, ub=ub: v.scalar_tensor_tensor(
                    ffn_c[:], tt2[:], bb_sb[:, ot - 1:ot], rbank[ub][:, :],
                    ALU.add, ALU.add),
                    reads=[("tt2",), ("bb_sb",), RB[ub]], writes=[("ffn_c",)])
                B.dve(lambda v: v.tensor_mul(sq_c[:], ffn_c[:], ffn_c[:]),
                      reads=[("ffn_c",)], writes=[("sq_c",)])
                sbk, skey = (rbank[2], RB[2]) if q == 0 else (pctx, PCTX)
                B.pe(lambda t, ot=ot, sbk=sbk: t.matmul(
                    sbk[0:1, :], ones_bf[:], ffn_c[:],
                    start=(ot == 1), stop=(ot == NCH), skip_group_check=True),
                    reads=[("ffn_c",), ("ones_bf",)], writes=[skey])
                B.pe(lambda t, ot=ot, sbk=sbk: t.matmul(
                    sbk[32:33, :], ones_bf[:], sq_c[:],
                    start=(ot == 1), stop=(ot == NCH), tile_position=(0, 32),
                    skip_group_check=True),
                    reads=[("sq_c",), ("ones_bf",)], writes=[skey])
                B.dma("syn", ffn_spill[128 * (ot - 1):128 * ot, 512 * q:512 * (q + 1)],
                      ffn_c[:], grp="fw", reads=[("ffn_c",)], writes=[("ffn_sp", ot, q)])

    B.dve(lambda v: v.engine_nop(),
          writes=[("a_bf", og) for og in range(8)])
    st4slots = [(5, 6), (7, 8)]
    for q in range(NQ):
        sbk, skey = (rbank[2], RB[2]) if q == 0 else (pctx, PCTX)
        stat_finish(sbk, skey, st4slots[q][0], st4slots[q][1])

    for q in range(NQ):
        ri, mi = st4slots[q]
        B.pe(lambda t, ri=ri: t.matmul(rbank[3][:, :], onesr_f32[:],
                                       slot(ri), start=True, stop=True),
             reads=[("rp", ri), ("onesr_f32",)], writes=[RB[3]])
        B.dve(lambda v: v.tensor_copy(rb4[:], rbank[3][:, :]),
              reads=[RB[3]], writes=[("rb4", q)])
        B.pe(lambda t, mi=mi: t.matmul(rbank[3][:, :], onesr_f32[:],
                                       slot(mi), start=True, stop=True),
             reads=[("rp", mi), ("onesr_f32",)], writes=[RB[3]])
        B.dve(lambda v: v.tensor_copy(mb4[:], rbank[3][:, :]),
              reads=[RB[3]], writes=[("mb4", q)])
        for c in range(NCH):
            par = c % 2
            B.dma("syn", xf_c[par][:, :],
                  x1_spill[128 * c:128 * (c + 1), 512 * q:512 * (q + 1)],
                  grp=f"xf{par}", reads=[("x1_spill", c, q)], writes=[("xf_c", par)])
            B.dma("pool", ffn_r[:, :],
                  ffn_spill[128 * c:128 * (c + 1), 512 * q:512 * (q + 1)],
                  grp="fr", reads=[("ffn_sp", c + 1, q)], writes=[("ffn_r",)])
            B.dve(lambda v: v.tensor_mul(tt1[:], ffn_r[:, :], rb4[:]),
                  reads=[("ffn_r",), ("rb4", q)], writes=[("tt1",)])
            B.dve(lambda v: v.tensor_sub(tt2[:], tt1[:], mb4[:]),
                  reads=[("tt1",), ("mb4", q)], writes=[("tt2",)])
            B.dve(lambda v, par=par: v.tensor_add(out_c[:], tt2[:], xf_c[par][:, :]),
                  reads=[("tt2",), ("xf_c", par)], writes=[("out_c",)])
            B.dma("syn", out[128 * c:128 * (c + 1), 512 * q:512 * (q + 1)],
                  out_c[:], grp="ow", reads=[("out_c",)], writes=[("out", c, q)])

    return finish(nc, B, stack)


def finish(nc, B, stack):
    B.finalize()
    sems = {s: stack.enter_context(nc.semaphore(s)) for s in Builder.STREAMS}
    for g in B.grps:
        sems[g] = stack.enter_context(nc.semaphore(g))
    grp_totals = {g: 16 * len(ops) for g, ops in B.grps.items()}
    with nc.Block() as block:
        @block.tensor
        def _(t):
            B.emit_stream("pe", t, sems)

        @block.scalar
        def _(s):
            B.emit_stream("act", s, sems)

        @block.vector
        def _(v):
            B.emit_stream("dve", v, sems)

        @block.gpsimd
        def _(g):
            B.emit_stream("pool", g, sems)

        @block.sync
        def _(sy):
            B.emit_stream("syn", sy, sems)
            for g, tot in grp_totals.items():
                sy.wait_ge(sems[g], tot)

    stack.close()
    return nc


def host_constants(inputs):
    wq = np.asarray(inputs["L0_attn_Wq_d"], np.float32)
    bq = np.asarray(inputs["L0_attn_bq_d"], np.float32)
    wqk = np.asarray(inputs["L0_attn_W_qk"], np.float32)
    bqk = np.asarray(inputs["L0_attn_b_qk"], np.float32)
    wkv = np.asarray(inputs["L0_attn_Wkv_d"], np.float32)
    bkv = np.asarray(inputs["L0_attn_bkv_d"], np.float32)
    wvu = np.asarray(inputs["L0_attn_Wv_u"], np.float32)
    bvu = np.asarray(inputs["L0_attn_bv_u"], np.float32)
    wo = np.ascontiguousarray(np.asarray(inputs["L0_attn_Wo"], np.float32))
    bo = np.asarray(inputs["L0_attn_bo"], np.float32)
    wb = np.asarray(inputs["L0_ffn_W_base"], np.float32)
    bb = np.asarray(inputs["L0_ffn_b_base"], np.float32)
    enc0 = np.asarray(inputs["L0_ffn_enc0"], np.float32)
    enc1 = np.asarray(inputs["L0_ffn_enc1"], np.float32)

    d = {}
    d["wcat"] = np.ascontiguousarray(
        np.concatenate([wq, wkv, np.ones((D, 1), np.float32)], axis=1))
    d["b17_d"] = np.concatenate([bq, bkv, np.zeros(1, np.float32)])[:, None].copy()
    wqe = np.zeros((5, 16 * 32), np.float32)
    for h in range(32):
        for l in range(4):
            wqe[l, 16 * h:16 * h + 12] = wqk[:, 4 * h + l]
            wqe[l, 16 * h + 12] = bqk[4 * h + l]
        wqe[4, 16 * h + 13] = -2.0
    d["wqk_ext"] = wqe
    d["wqk_aug"] = np.ascontiguousarray(np.concatenate([wqk, bqk[None, :]], axis=0))
    e4 = np.zeros((128, 32), np.float32)
    e4t = np.zeros((32, 128), np.float32)
    for h in range(32):
        for l in range(4):
            e4[4 * h + l, h] = 1.0
            e4t[h, 4 * h + l] = 1.0
    d["e4_d"] = e4
    d["e4t_d"] = e4t
    d["ones_dr"] = np.ones((1, NSLOT * TOK), np.float32)
    d["ones_col"] = np.ones((128, 1), np.float32)
    d["ident_d"] = np.eye(16, dtype=np.float32)
    d["wvuT"] = np.ascontiguousarray(wvu.T)
    d["bvu_d"] = np.ascontiguousarray(bvu.reshape(32, 128).T)
    d["wo_d"] = wo
    d["bo_d"] = np.ascontiguousarray(bo.reshape(32, 128).T)
    d["wbT_d"] = np.ascontiguousarray(wb.T)
    d["encT_d"] = np.ascontiguousarray(np.concatenate([enc0.T, enc1.T], axis=1))
    d["bb_d"] = np.ascontiguousarray(bb.reshape(32, 128).T)
    d["co0"] = np.ascontiguousarray(np.asarray(inputs["L0_ffn_coefs0"], np.float32))
    d["co1"] = np.ascontiguousarray(np.asarray(inputs["L0_ffn_coefs1"], np.float32))
    d["mk0"] = np.ascontiguousarray(np.asarray(inputs["L0_ffn_mask0"]).astype(np.uint8))
    d["mk1"] = np.ascontiguousarray(np.asarray(inputs["L0_ffn_mask1"]).astype(np.uint8))

    n = D
    j = np.arange(n, dtype=np.float64)
    M = np.empty((n, n), np.float32)
    M[0] = np.float32(1.0 / math.sqrt(n))
    f = np.arange(1, n // 2, dtype=np.float64)
    ang = (2.0 * np.pi / n) * np.outer(f, j)
    M[1:n - 1:2] = (math.sqrt(2.0 / n) * np.cos(ang)).astype(np.float32)
    M[2:n - 1:2] = (math.sqrt(2.0 / n) * np.sin(ang)).astype(np.float32)
    M[n - 1] = (((-1.0) ** j) / math.sqrt(n)).astype(np.float32)
    d["mmat"] = M.astype(ml_dtypes.bfloat16)
    return d


def kernel(**inputs):
    x = np.asarray(inputs["L0_attn_x"], np.float32)
    const = host_constants(inputs)
    nc = build_program()
    in_maps = []
    for core in range(8):
        b, r = core // 4, core % 4
        order = np.concatenate([np.arange(r * TOK, (r + 1) * TOK),
                                np.arange(0, r * TOK),
                                np.arange((r + 1) * TOK, SB)])
        xTc = np.ascontiguousarray(x[b][order].T)
        m = dict(const)
        m["xT"] = xTc
        in_maps.append(m)
    res = run_bass_kernel_spmd(nc, in_maps, core_ids=list(range(8)))
    outf = np.empty((2, SB, D), np.float32)
    for core in range(8):
        b, r = core // 4, core % 4
        outf[b, r * TOK:(r + 1) * TOK, :] = res.results[core]["out"].T
    return outf


# version 05
# speedup vs baseline: 1.5012x; 1.0671x over previous
import contextlib
import math
import numpy as np
import ml_dtypes
import concourse.bass as bass
import concourse.mybir as mybir
from concourse.bass_utils import run_bass_kernel_spmd

F32 = mybir.dt.float32
BF16 = mybir.dt.bfloat16
U8 = mybir.dt.uint8
AF = mybir.ActivationFunctionType
ALU = mybir.AluOpType

D = 4096
SB = 4096
TOK = 1024
NCH = 32
NKT = 32
NBLK = 8
NSLOT = 8
NQ = 2
EPS = 1e-5
CSHIFT = 8.0

DEBUG = False
TRUNC = 5


class Op:
    __slots__ = ("stream", "idx", "fn", "deps", "inc", "num", "grp", "gseq")

    def __init__(self, stream, idx, fn):
        self.stream = stream
        self.idx = idx
        self.fn = fn
        self.deps = []
        self.inc = False
        self.num = None
        self.grp = None
        self.gseq = 0


class Builder:
    STREAMS = ("pe", "act", "dve", "pool", "syn")

    def __init__(self):
        self.ops = {s: [] for s in self.STREAMS}
        self.last_write = {}
        self.readers = {}
        self.grps = {}
        self.gseq = 0

    def add(self, stream, fn, reads=(), writes=(), self_dep=False):
        op = Op(stream, len(self.ops[stream]), fn)
        self.gseq += 1
        op.gseq = self.gseq
        deps = []
        for b in reads:
            w = self.last_write.get(b)
            if w is not None:
                deps.append(w)
        for b in writes:
            w = self.last_write.get(b)
            if w is not None:
                deps.append(w)
            deps.extend(self.readers.get(b, ()))
        if stream == "pe" and not self_dep:
            op.deps = [d for d in deps if d.stream != stream]
        else:
            op.deps = deps
        self.ops[stream].append(op)
        for b in writes:
            self.last_write[b] = op
            self.readers[b] = []
        for b in reads:
            self.readers.setdefault(b, []).append(op)
        return op

    def pe(self, fn, reads=(), writes=()):
        return self.add("pe", fn, reads, writes)

    def act(self, fn, reads=(), writes=(), self_dep=False):
        return self.add("act", fn, reads, writes, self_dep=self_dep)

    def dve(self, fn, reads=(), writes=(), self_dep=False):
        return self.add("dve", fn, reads, writes, self_dep=self_dep)

    def dma(self, q, out_ap, in_ap, grp=None, reads=(), writes=(), **kw):
        assert grp is not None, "dma needs a sem group"
        def fn(eng):
            return eng.dma_start(out_ap, in_ap, **kw)
        op = self.add(q, fn, reads, writes)
        op.grp = grp
        self.grps.setdefault(grp, []).append(op)
        return op

    def finalize(self):
        for g, ops in self.grps.items():
            for i, op in enumerate(ops):
                op.num = 16 * (i + 1)
                op.inc = True
        for s in self.STREAMS:
            for op in self.ops[s]:
                merged = {}
                for d in op.deps:
                    key = d.grp if d.grp is not None else d.stream
                    cur = merged.get(key)
                    if cur is None:
                        merged[key] = d
                    elif d.grp is not None:
                        if d.num > cur.num:
                            merged[key] = d
                    elif d.idx > cur.idx:
                        merged[key] = d
                op.deps = list(merged.values())
                for d in op.deps:
                    d.inc = True
        for s in ("pe", "act", "dve"):
            n = 0
            for op in self.ops[s]:
                if op.grp is not None:
                    continue
                if op.inc:
                    n += 1
                op.num = n

    def emit_stream(self, s, eng, sems):
        import bisect
        gseqs = {g: [o.gseq for o in ops] for g, ops in self.grps.items()}
        wm = {}
        for op in self.ops[s]:
            for d in op.deps:
                key = d.grp if d.grp is not None else d.stream
                if d.grp is not None:
                    k = bisect.bisect_left(gseqs[d.grp], op.gseq)
                    val = 16 * max(k, (d.num + 15) // 16)
                else:
                    val = d.num
                if val > wm.get(key, 0):
                    eng.wait_ge(sems[key], val)
                    wm[key] = val
            inst = op.fn(eng)
            if op.inc:
                key = op.grp if op.grp is not None else s
                inst.then_inc(sems[key], 16 if op.grp is not None else 1)


def build_program():
    nc = bass.Bass("TRN2", target_bir_lowering=False)
    B = Builder()
    stack = contextlib.ExitStack()

    def din(name, shape, dtype=F32):
        return nc.dram_tensor(name, list(shape), dtype, kind="ExternalInput").ap()

    xT = din("xT", (D, SB))
    wcat = din("wcat", (D, 17))
    b17_d = din("b17_d", (17, 1))
    wqk_ext = din("wqk_ext", (5, 16 * 32))
    wqk_aug = din("wqk_aug", (13, 128))
    e4_d = din("e4_d", (128, 32))
    e4t_d = din("e4t_d", (32, 128))
    ones_dr = din("ones_dr", (1, NSLOT * TOK))
    ones_col = din("ones_col", (128, 1))
    ident_d = din("ident_d", (16, 16))
    wvuT = din("wvuT", (D, 4))
    bvu_d = din("bvu_d", (128, 32))
    wo_d = din("wo_d", (D, D))
    bo_d = din("bo_d", (128, 32))
    wbT_d = din("wbT_d", (D, D))
    encT_d = din("encT_d", (D, 40))
    bb_d = din("bb_d", (128, 32))
    co0 = din("co0", (3, D, 32))
    co1 = din("co1", (3, D, 8))
    mk0 = din("mk0", (3, D, 32), U8)
    mk1 = din("mk1", (3, D, 8), U8)
    mmat = din("mmat", (D, D), BF16)

    dbg = "ExternalOutput" if DEBUG else "Internal"
    out = nc.dram_tensor("out", [D, TOK], F32, kind="ExternalOutput").ap()
    c_stage = nc.dram_tensor("c_stage", [32, TOK], F32, kind=dbg).ap()
    ctx_stage = nc.dram_tensor("ctx_stage", [NQ * NSLOT * 20, 512], F32, kind=dbg).ap()
    x1_spill = nc.dram_tensor("x1_spill", [D, TOK], F32, kind=dbg).ap()
    ffn_spill = nc.dram_tensor("ffn_spill", [D, TOK], BF16, kind=dbg).ap()
    attn_dbg = nc.dram_tensor("attn_dbg", [D, TOK], F32, kind=dbg).ap()
    attn_sp = nc.dram_tensor("attn_sp", [D, TOK], BF16, kind="Internal").ap()
    cqkv_dbg = nc.dram_tensor("cqkv_dbg", [17, SB], F32, kind=dbg).ap()
    q_dbg = nc.dram_tensor("q_dbg", [128, TOK], F32, kind=dbg).ap()
    a_dbg = nc.dram_tensor("a_dbg", [128, D], F32, kind=dbg).ap()
    bc_dbg = nc.dram_tensor("bc_dbg", [128, 32], F32, kind=dbg).ap()
    ctxn_dbg = nc.dram_tensor("ctxn_dbg", [128, TOK], F32, kind=dbg).ap()
    h_dbg = nc.dram_tensor("h_dbg", [40, D], F32, kind=dbg).ap()
    u_dbg = nc.dram_tensor("u_dbg", [40, TOK], F32, kind=dbg).ap()
    g_dbg = nc.dram_tensor("g_dbg", [128, NCH * 40], F32, kind=dbg).ap()

    def sbuf(name, shape, dtype):
        return stack.enter_context(nc.sbuf_tensor(name, list(shape), dtype))

    def psum(name, shape, dtype=F32):
        return stack.enter_context(nc.psum_tensor(name, list(shape), dtype))

    proB = sbuf("proB", (128, 4096), F32)
    proB_bf = proB.bitcast(BF16)
    xblk = [proB_bf[:, 0:2048], proB_bf[:, 2048:4096]]
    xsq_t = sbuf("xsq", (128, 2048), BF16)
    wcat_sb = sbuf("wcat_sb", (128, NCH * 17), BF16)
    b17_sb = sbuf("b17_sb", (17, 1), F32)
    sneg = sbuf("sneg", (17, 1), F32)
    blk_f = sbuf("blk_f", (17, 512), F32)
    blk_fix = sbuf("blk_fix", (17, 512), F32)
    rowpool = sbuf("rowpool", (1, 4608), F32)
    ones17 = sbuf("ones17", (1, 32), F32)
    cqb = proB_bf[:, 4096:5120]
    qsb = proB_bf[:, 5120:6144]
    c_sb = proB[:, 3072:4096]
    phA = sbuf("phA", (128, NKT * 128 + 3 * 2048 + 2 * TOK), BF16)
    kh2 = phA[:, 0:NKT * 128]
    cqm2 = [phA[:, NKT * 128 + 3 * 2048:NKT * 128 + 3 * 2048 + TOK],
            phA[:, NKT * 128 + 3 * 2048 + TOK:]]
    ckvlat_t = sbuf("ckvlat", (128, SB), BF16)
    ckvlat = ckvlat_t
    ckv32 = ckvlat_t.bitcast(F32)
    ckv1 = sbuf("ckv1", (128, NKT * 8), BF16)
    expsc = phA[:, NKT * 128:NKT * 128 + 3 * 2048]
    ctx_sp = sbuf("ctx_sp", (128, 512), F32)
    ctx_u = sbuf("ctx_u", (128, 512), F32)
    dnm = sbuf("dnm", (32, 512), F32)
    dtmp = sbuf("dtmp", (32, 512), F32)
    dtmp2 = sbuf("dtmp2", (32, 512), F32)
    ctx_n = sbuf("ctx_n", (128, 512), BF16)
    a_bf = sbuf("a_bf", (128, D), BF16)
    bc_sb = sbuf("bc_sb", (128, 32), F32)
    bo_sb = sbuf("bo_sb", (128, 32), F32)
    bb_sb = sbuf("bb_sb", (128, 32), F32)
    attn_c = sbuf("attn_c", (128, 512), BF16)
    attn_c2 = sbuf("attn_c2", (128, 512), BF16)
    attn_r = sbuf("attn_r", (128, 512), BF16)
    attn_rb = sbuf("attn_rb", (128, 512), BF16)
    sq_c = sbuf("sq_c", (128, 512), BF16)
    sq_c2 = sbuf("sq_c2", (128, 512), BF16)
    x1_bf = sbuf("x1_bf", (128, NCH * TOK), BF16)
    xf_c = [sbuf(f"xf_c{i}", (128, 512), F32) for i in range(2)]
    x1f_c = sbuf("x1f_c", (128, 512), F32)
    x1f_c2 = sbuf("x1f_c2", (128, 512), F32)
    hcat = sbuf("hcat", (40, D), BF16)
    g0all = sbuf("g0all", (128, NCH * 32), BF16)
    g1all = sbuf("g1all", (128, NCH * 8), BF16)
    u_sb = sbuf("u_sb", (40, TOK), BF16)
    wstr = [sbuf(f"wstr{i}", (128, 4096), BF16) for i in range(2)]
    wvall = wstr[0]
    x1f = x1_bf.bitcast(F32)
    x1u8 = x1_bf.bitcast(U8)
    cfw = x1f[:, 0:1024]
    acc0 = x1f[:, 1024:2048]
    gtmpw = x1f[:, 2048:3072]
    mkwf = x1_bf[:, 6144:7168]
    mkw8 = x1u8[:, 14336:15360]
    bcrow = x1f[0:1, 4096:8192]
    xraw = [x1f[:, 8192:10240], x1f[:, 10240:12288]]
    xsq = [xsq_t, x1_bf[:, 24576:26624]]
    wos2 = [phA[:, 0:2048], phA[:, 2048:4096]]
    ms2 = [phA[:, 4096:6144], phA[:, 6144:8192]]
    enc_sb = proB_bf[:, 4096:5376]
    wqk_sb = sbuf("wqk_sb", (5, 16 * 32), BF16)
    wqka_sb = sbuf("wqka_sb", (13, 128), BF16)
    e4_sb = sbuf("e4_sb", (128, 32), BF16)
    e4t_sb = sbuf("e4t_sb", (32, 128), F32)
    ident_sb = sbuf("ident_sb", (16, 16), F32)
    ones_bf = sbuf("ones_bf", (128, 1), BF16)
    ones_f32 = sbuf("ones_f32", (128, 1), F32)
    onesr_f32 = sbuf("onesr_f32", (1, 128), F32)
    bvu_sb = sbuf("bvu_sb", (128, NCH), BF16)
    rb_sb = proB[:, 3072:3584]
    mb_sb = proB[:, 3584:4096]
    rb3 = proB[:, 0:1024]
    mb3 = proB[:, 1024:2048]
    a32 = a_bf.bitcast(F32)
    rb4 = a32[:, 0:512]
    mb4 = a32[:, 512:1024]
    senc_sb = sbuf("senc_sb", (128, 1), F32)
    ffn_c = ckvlat_t[:, 0:512]
    ffn_r = ckvlat_t[:, 512:1024]
    ffn_r2 = a_bf[:, 3072:3584]
    out_c = ckv32[:, 512:1024]
    out_c2 = proB[:, 2048:2560]
    tt1 = ckv32[:, 1024:1536]
    tt2 = ckv32[:, 1536:2048]

    def slot(i, j=0):
        return rowpool[0:1, 512 * i:512 * (i + 1)]

    ring = psum("ring", (128, 3072))
    pctx = psum("pctx", (128, 512))
    paux = psum("paux", (128, 512))
    rbank = [ring[:, 512 * i:512 * (i + 1)] for i in range(6)]
    RB = [("rng", i) for i in range(6)]
    PCTX = ("pctx",)
    PAUX = ("paux",)



    for c in range(NCH):
        B.dma("pool", wcat_sb[:, 17 * c:17 * (c + 1)], wcat[128 * c:128 * (c + 1), :],
              grp="cpool", writes=[("wcat_sb",)])
    B.dma("pool", wqk_sb[:], wqk_ext[:], grp="cpool", writes=[("wqk_sb",)])
    B.dma("pool", wqka_sb[:], wqk_aug[:], grp="cpool", writes=[("wqka_sb",)])
    B.dma("pool", e4_sb[:], e4_d[:], grp="cpool", writes=[("e4_sb",)])
    B.dma("syn", e4t_sb[:], e4t_d[:], grp="csyn", writes=[("e4t_sb",)])
    B.dma("syn", b17_sb[:], b17_d[:], grp="csyn", writes=[("b17_sb",)])
    B.dma("syn", ident_sb[:], ident_d[:], grp="csyn", writes=[("ident_sb",)])
    B.dma("syn", ones17[:], ones_dr[0:1, 0:32], grp="csyn", writes=[("ones17",)])
    B.dma("pool", ones_bf[:], ones_col[:], grp="cpool", writes=[("ones_bf",)])
    B.dma("syn", ones_f32[:], ones_col[:], grp="csyn", writes=[("ones_f32",)])
    B.dma("syn", onesr_f32[:], ones_dr[0:1, 0:128], grp="csyn", writes=[("onesr_f32",)])
    B.dma("syn", bo_sb[:], bo_d[:], grp="csyn", writes=[("bo_sb",)])
    B.dma("syn", bb_sb[:], bb_d[:], grp="csyn", writes=[("bb_sb",)])
    B.dma("pool", bvu_sb[:], bvu_d[:], grp="cpool", writes=[("bvu_sb",)])
    B.dve(lambda v: v.memset(wvall[:], 0.0), writes=[("wstr", 0)])
    for c in range(NCH):
        B.dma("pool", wvall[:, 128 * c + 4 * c:128 * c + 4 * c + 4],
              wvuT[128 * c:128 * (c + 1), :], grp="cpool", writes=[("wstr", 0)])
    B.dve(lambda v: v.memset(ckvlat[:], 1.0), writes=[("ckvlat",)])
    B.dve(lambda v: v.memset(ckv1[:], 1.0), writes=[("ckv1",)])
    B.dve(lambda v: v.memset(cqm2[0][:], 0.0), writes=[("cqm", 0)])
    B.dve(lambda v: v.memset(cqm2[1][:], 0.0), writes=[("cqm", 1)])
    B.dve(lambda v: v.memset(cqb[:], 0.0), writes=[("cqb",)])
    B.dma("pool", cqb[12:13, :], ones_dr[0:1, 0:TOK], grp="cpool", writes=[("cqb",)])
    for bf_ in range(2):
        for g in range(4):
            B.dma("pool", cqm2[bf_][32 * g + 12:32 * g + 13, :],
                  ones_dr[0:1, 0:TOK], grp="cpool", writes=[("cqm", bf_)])

    for half, (co_, mk_, gall, wdt) in enumerate(
            [(co0, mk0, g0all, 32), (co1, mk1, g1all, 8)]):
        W = NCH * wdt
        for e in range(3):
            B.dma("syn", cfw[:, 0:W].rearrange("p (cc w) -> p cc w", w=wdt),
                  co_[e, :, :].rearrange("(cc p) w -> p cc w", p=128),
                  grp=f"gld{half}", writes=[("cfe",)])
            B.dma("syn", mkw8[:, 0:W].rearrange("p (cc w) -> p cc w", w=wdt),
                  mk_[e, :, :].rearrange("(cc p) w -> p cc w", p=128),
                  grp=f"gld{half}", writes=[("mku",)])
            B.dve(lambda v, W=W: v.tensor_copy(mkwf[:, 0:W], mkw8[:, 0:W]),
                  reads=[("mku",)], writes=[("mkf",)], self_dep=True)
            if e == 0:
                B.dve(lambda v, W=W: v.tensor_mul(acc0[:, 0:W], cfw[:, 0:W],
                                                  mkwf[:, 0:W]),
                      reads=[("cfe",), ("mkf",)], writes=[("ga",)], self_dep=True)
            else:
                B.dve(lambda v, W=W: v.tensor_mul(gtmpw[:, 0:W], cfw[:, 0:W],
                                                  mkwf[:, 0:W]),
                      reads=[("cfe",), ("mkf",)], writes=[("gtmp",)], self_dep=True)
                B.dve(lambda v, W=W: v.tensor_add(acc0[:, 0:W], acc0[:, 0:W],
                                                  gtmpw[:, 0:W]),
                      reads=[("gtmp",)], writes=[("ga",)], self_dep=True)
        B.dve(lambda v, gl=gall, W=W: v.tensor_copy(gl[:, :], acc0[:, 0:W]),
              reads=[("ga",)], writes=[("gall", half)], self_dep=True)

    for c in range(NCH):
        B.pe(lambda t, c=c: t.matmul(paux[0:17, 0:1], wcat_sb[:, 17 * c:17 * (c + 1)],
                                     ones_bf[:], start=(c == 0), stop=(c == NCH - 1)),
             reads=[("wcat_sb",), ("ones_bf",)], writes=[PAUX])
    B.dve(lambda v: v.tensor_scalar_mul(sneg[:], paux[0:17, 0:1], -1.0),
          reads=[PAUX], writes=[("sneg",)], self_dep=True)

    for blk in range(NBLK):
        for q4 in range(8):
            par = q4 % 2
            xb, xq, xr = xblk[par], xsq[par], xraw[par]
            B.dma("syn", xr[:, :],
                  xT.rearrange("(cc p) t -> p cc t", p=128)
                  [:, 4 * q4:4 * q4 + 4, 512 * blk:512 * (blk + 1)],
                  grp=f"xb{par}", writes=[("xraw", par)])
            B.act(lambda s, xb=xb, xr=xr: s.activation(xb[:, :], xr[:, :], AF.Copy),
                  reads=[("xraw", par)], writes=[("xblk", par)])
            B.act(lambda s, xq=xq, xr=xr: s.activation(xq[:, :], xr[:, :], AF.Square),
                  reads=[("xraw", par)], writes=[("xsq", par)])
            for i in range(4):
                c = 4 * q4 + i
                B.pe(lambda t, xb=xb, i=i, c=c:
                     t.matmul(paux[0:17, :], wcat_sb[:, 17 * c:17 * (c + 1)],
                              xb[:, 512 * i:512 * (i + 1)],
                              start=(c == 0), stop=(c == NCH - 1)),
                     reads=[("xblk", par), ("wcat_sb",)], writes=[PAUX])
                B.pe(lambda t, xq=xq, i=i, c=c:
                     t.matmul(pctx[0:1, :], ones_bf[:],
                              xq[:, 512 * i:512 * (i + 1)],
                              start=(c == 0), stop=(c == NCH - 1),
                              skip_group_check=True),
                     reads=[("xsq", par), ("ones_bf",)], writes=[PCTX])
                B.pe(lambda t, xb=xb, i=i, c=c:
                     t.matmul(pctx[32:33, :], ones_bf[:],
                              xb[:, 512 * i:512 * (i + 1)],
                              start=(c == 0), stop=(c == NCH - 1),
                              tile_position=(0, 32), skip_group_check=True),
                     reads=[("xblk", par), ("ones_bf",)], writes=[PCTX])
        B.dve(lambda v: v.tensor_copy(blk_f[0:17, :], paux[0:17, :]),
              reads=[PAUX], writes=[("blk_f",)])
        B.dve(lambda v: v.tensor_scalar_mul(slot(0), pctx[32:33, :], 1.0 / D),
              reads=[PCTX], writes=[("rp", 0)])
        B.dve(lambda v: v.tensor_scalar_mul(slot(1), pctx[0:1, :], 1.0 / D),
              reads=[PCTX], writes=[("rp", 1)])
        B.dve(lambda v: v.tensor_mul(slot(2), slot(0), slot(0)),
              reads=[("rp", 0)], writes=[("rp", 2)], self_dep=True)
        B.dve(lambda v: v.tensor_sub(slot(1), slot(1), slot(2)),
              reads=[("rp", 2), ("rp", 1)], writes=[("rp", 1)], self_dep=True)
        B.dve(lambda v: v.tensor_scalar_add(slot(1), slot(1), EPS),
              reads=[("rp", 1)], writes=[("rp", 1)], self_dep=True)
        B.act(lambda s: s.activation(slot(3), slot(1), AF.Ln),
              reads=[("rp", 1)], writes=[("rp", 3)])
        B.act(lambda s: s.activation(slot(3), slot(3), AF.Exp, scale=-0.5),
              reads=[("rp", 3)], writes=[("rp", 3)], self_dep=True)
        B.dve(lambda v: v.tensor_mul(slot(4), slot(0), slot(3)),
              reads=[("rp", 0), ("rp", 3)], writes=[("rp", 4)], self_dep=True)
        B.pe(lambda t: t.matmul(rbank[0][0:17, :], ones17[0:1, 0:17], slot(3),
                                start=True, stop=True),
             reads=[("rp", 3), ("ones17",)], writes=[RB[0]])
        B.pe(lambda t: t.matmul(rbank[1][0:17, :], ones17[0:1, 0:17], slot(4),
                                start=True, stop=True),
             reads=[("rp", 4), ("ones17",)], writes=[RB[1]])
        B.dve(lambda v: v.tensor_mul(blk_fix[:], blk_f[0:17, :], rbank[0][0:17, :]),
              reads=[("blk_f",), RB[0]], writes=[("blk_fix",)])
        B.dve(lambda v: v.scalar_tensor_tensor(blk_fix[:], rbank[1][0:17, :], sneg[:],
                                               blk_fix[:], ALU.mult, ALU.add),
              reads=[RB[1], ("sneg",), ("blk_fix",)], writes=[("blk_fix",)],
              self_dep=True)
        B.dve(lambda v: v.tensor_scalar_add(blk_fix[:], blk_fix[:], b17_sb[:]),
              reads=[("b17_sb",), ("blk_fix",)], writes=[("blk_fix",)])
        B.dma("pool", ckvlat[0:4, 512 * blk:512 * (blk + 1)], blk_fix[12:16, :],
              grp="ckv", reads=[("blk_fix",)], writes=[("ckvlat", blk)])
        for i in range(4):
            kt = 4 * blk + i
            B.pe(lambda t, i=i: t.transpose(rbank[2][:, 0:16],
                                            blk_fix[0:16, 128 * i:128 * (i + 1)],
                                            ident_sb[:]),
                 reads=[("blk_fix",), ("ident_sb",)], writes=[RB[2]])
            B.dve(lambda v, kt=kt: v.tensor_copy(ckv1[:, 8 * kt:8 * kt + 4],
                                                 rbank[2][:, 12:16]),
                  reads=[RB[2]], writes=[("ckv1", kt)])
        if DEBUG:
            B.dma("syn", cqkv_dbg[:, 512 * blk:512 * (blk + 1)], blk_fix[:],
                  grp="dbg", reads=[("blk_fix",)], writes=[("cqkv_dbg",)])
        if blk < 2:
            B.dve(lambda v, blk=blk:
                  v.tensor_copy(cqb[0:12, 512 * blk:512 * (blk + 1)], blk_fix[0:12, :]),
                  reads=[("blk_fix",)], writes=[("cqb", blk)])
            for g in range(4):
                for bf_ in range(2):
                    B.dma("pool", cqm2[bf_][32 * g:32 * g + 12,
                                            512 * blk:512 * (blk + 1)],
                          blk_fix[0:12, :],
                          grp="cqc", reads=[("blk_fix",)], writes=[("cqm", bf_)])

    for h2 in range(2):
        B.pe(lambda t, h2=h2: t.matmul(rbank[0][:, :], wqka_sb[:],
                                       cqb[0:13, 512 * h2:512 * (h2 + 1)],
                                       start=True, stop=True),
             reads=[("cqb", 0), ("cqb", 1), ("wqka_sb",)], writes=[RB[0]])
        B.dve(lambda v, h2=h2: v.tensor_copy(qsb[:, 512 * h2:512 * (h2 + 1)],
                                             rbank[0][:, :]),
              reads=[RB[0]], writes=[("qsb", h2)])
        if DEBUG:
            B.dma("pool", q_dbg[:, 512 * h2:512 * (h2 + 1)],
                  qsb[:, 512 * h2:512 * (h2 + 1)],
                  grp="dbg", reads=[("qsb", h2)], writes=[("q_dbg",)])
        B.dve(lambda v, h2=h2: v.tensor_mul(sq_c[:],
                                            qsb[:, 512 * h2:512 * (h2 + 1)],
                                            qsb[:, 512 * h2:512 * (h2 + 1)]),
              reads=[("qsb", h2)], writes=[("sq_c",)])
        B.pe(lambda t, h2=h2: t.matmul(rbank[1][0:32, :], e4_sb[:], sq_c[:],
                                       start=True, stop=True),
             reads=[("sq_c",), ("e4_sb",)], writes=[RB[1]])
        B.dve(lambda v, h2=h2: v.tensor_scalar(c_sb[0:32, 512 * h2:512 * (h2 + 1)],
                                               rbank[1][0:32, :], 0.5, CSHIFT,
                                               ALU.mult, ALU.add),
              reads=[RB[1]], writes=[("c_sb", h2)])
    B.dma("syn", c_stage[:], c_sb[0:32, :], grp="cst",
          reads=[("c_sb", 0), ("c_sb", 1)], writes=[("c_stage",)])


    if TRUNC < 2:
        return finish(nc, B, stack)
    for seg in range(8):
        for cq in range(8):
            par = cq % 2
            B.dma("pool", wos2[par][:, :].rearrange("p (cc o) -> p cc o", o=512),
                  wo_d[512 * cq:512 * (cq + 1), 512 * seg:512 * (seg + 1)]
                  .rearrange("(cc p) o -> p cc o", p=128),
                  grp=f"wos{par}", writes=[("wos", par)])
            B.dma("syn", ms2[par][:, :].rearrange("p (cc o) -> p cc o", o=512),
                  mmat[512 * cq:512 * (cq + 1), 512 * seg:512 * (seg + 1)]
                  .rearrange("(cc p) o -> p cc o", p=128),
                  grp=f"ms{par}", writes=[("ms", par)])
            for i in range(4):
                c = 4 * cq + i
                B.pe(lambda t, c=c, par=par, i=i: t.matmul(
                    rbank[3][:, :], wvall[:, 128 * c:128 * (c + 1)],
                    wos2[par][:, 512 * i:512 * (i + 1)],
                    start=(c == 0), stop=(c == NCH - 1)),
                    reads=[("wos", par), ("wstr", 0)], writes=[RB[3]])
                B.pe(lambda t, c=c, par=par, i=i: t.matmul(
                    paux[0:1, :], bvu_sb[:, c:c + 1],
                    wos2[par][:, 512 * i:512 * (i + 1)],
                    start=(c == 0), stop=(c == NCH - 1), skip_group_check=True),
                    reads=[("wos", par), ("bvu_sb",)], writes=[PAUX])
            for i in range(4):
                c = 4 * cq + i
                B.pe(lambda t, c=c, par=par, i=i: t.matmul(
                    pctx[0:32, :], g0all[:, 32 * c:32 * (c + 1)],
                    ms2[par][:, 512 * i:512 * (i + 1)],
                    start=(c == 0), stop=(c == NCH - 1), skip_group_check=True),
                    reads=[("ms", par), ("gall", 0)], writes=[PCTX])
                B.pe(lambda t, c=c, par=par, i=i: t.matmul(
                    pctx[32:40, :], g1all[:, 8 * c:8 * (c + 1)],
                    ms2[par][:, 512 * i:512 * (i + 1)],
                    start=(c == 0), stop=(c == NCH - 1), tile_position=(0, 32),
                    skip_group_check=True),
                    reads=[("ms", par), ("gall", 1)], writes=[PCTX])
        B.dve(lambda v, og=seg: v.tensor_copy(a_bf[:, 512 * og:512 * (og + 1)],
                                              rbank[3][:, :]),
              reads=[RB[3]], writes=[("a_bf", seg)])
        B.dve(lambda v, og=seg: v.tensor_copy(bcrow[0:1, 512 * og:512 * (og + 1)],
                                              paux[0:1, :]),
              reads=[PAUX], writes=[("bcrow",)])
        B.dve(lambda v, jg=seg: v.tensor_scalar_mul(hcat[0:32, 512 * jg:512 * (jg + 1)],
                                                    pctx[0:32, :], 0.5),
              reads=[PCTX], writes=[("hcat",)])
        B.dve(lambda v, jg=seg: v.tensor_scalar_mul(hcat[32:40, 512 * jg:512 * (jg + 1)],
                                                    pctx[32:40, :], 0.5),
              reads=[PCTX], writes=[("hcat",)])
    for k in range(32):
        B.pe(lambda t, k=k: t.transpose(paux[:, k:k + 1],
                                        bcrow[0:1, 128 * k:128 * (k + 1)],
                                        ones_f32[0:1, 0:1]),
             reads=[("bcrow",), ("ones_f32",)], writes=[PAUX])
    B.dve(lambda v: v.tensor_copy(bc_sb[:, :], paux[:, 0:32]),
          reads=[PAUX], writes=[("bc_sb",)])
    B.dve(lambda v: v.tensor_add(bc_sb[:], bc_sb[:], bo_sb[:]),
          reads=[("bo_sb",), ("bc_sb",)], writes=[("bc_sb",)], self_dep=True)

    if DEBUG:
        B.dma("pool", a_dbg[:], a_bf[:], grp="dbg", reads=[("a_bf", og) for og in range(8)],
              writes=[("a_dbg",)])
        B.dma("syn", bc_dbg[:], bc_sb[:], grp="dbg", reads=[("bc_sb",)], writes=[("bc_dbg",)])

    if DEBUG:
        B.dma("pool", h_dbg[:], hcat[0:40, :], grp="dbg", reads=[("hcat",)], writes=[("h_dbg",)])
        B.dma("pool", g_dbg[:, 0:NCH * 32], g0all[:], grp="dbg", reads=[("gall", 0)],
              writes=[("g_dbg",)])
        B.dma("pool", g_dbg[:, NCH * 32:NCH * 32 + NCH * 8], g1all[:],
              grp="dbg", reads=[("gall", 1)], writes=[("g_dbg",)])
    if TRUNC < 3:
        return finish(nc, B, stack)
    B.dve(lambda v: v.memset(paux[:, :], 0.0), writes=[PAUX])
    B.dve(lambda v: v.memset(pctx[:, :], 0.0), writes=[PCTX])
    for s_ in range(NSLOT):
        cb = s_ % 2
        for g in range(4):
            B.dma("pool", cqm2[cb][32 * g + 13:32 * g + 14, :],
                  c_stage[4 * s_ + g:4 * s_ + g + 1, :],
                  grp=f"cq{cb}", reads=[("c_stage",)], writes=[("cqm", cb)])
        for ktg in range(8):
            for g in range(4):
                B.pe(lambda t, g=g, ktg=ktg, s_=s_: t.matmul(
                    paux[32 * g:32 * g + 16, :],
                    wqk_sb[0:5, 16 * (4 * s_ + g):16 * (4 * s_ + g) + 16],
                    ckvlat[0:5, 512 * ktg:512 * (ktg + 1)],
                    start=True, stop=True, tile_position=(0, 32 * g)),
                    reads=[("wqk_sb",), ("ckvlat", ktg)],
                    writes=[PAUX])
            B.dve(lambda v, ktg=ktg: v.tensor_scalar_mul(
                kh2[:, 512 * ktg:512 * (ktg + 1)], paux[:, :], 0.5),
                reads=[PAUX], writes=[("kh2", ktg)])
        for q in range(NQ):
            prev = None
            for kt in range(NKT):
                pa = (2 * kt) % 3
                pb = (2 * kt + 1) % 3
                slot3 = kt % 3
                for g in range(4):
                    pair = pa if g < 2 else pb
                    bank = 2 * pair + (g & 1)
                    B.pe(lambda t, g=g, kt=kt, cb=cb, q=q, bank=bank: t.matmul(
                        rbank[bank][:, :],
                        kh2[32 * g:32 * g + 16, 128 * kt:128 * (kt + 1)],
                        cqm2[cb][32 * g:32 * g + 16, 512 * q:512 * (q + 1)],
                        start=True, stop=True, tile_position=(32 * g, 0)),
                        reads=[("kh2", kt // 4), ("cqm", cb)], writes=[RB[bank]])
                rv = ring.rearrange("p (c n) -> p c n", n=1024)
                dst = expsc[:, 2048 * slot3:2048 * (slot3 + 1)]
                if pb == pa + 1:
                    src = ring[:, 1024 * pa:1024 * (pb + 1)]
                    dst2 = dst
                else:
                    src = rv[:, ::-2, :]
                    dst2 = dst.rearrange("p (c n) -> p c n", n=1024)
                B.act(lambda s, src=src, dst2=dst2: s.activation(dst2, src, AF.Exp),
                      reads=[RB[2 * pa], RB[2 * pa + 1], RB[2 * pb], RB[2 * pb + 1]],
                      writes=[("expsc", slot3)])
                if prev is not None:
                    pkt, pslot3 = prev
                    for g in range(4):
                        pair_off = 0 if g < 2 else 1024
                        off = 2048 * pslot3 + pair_off + 512 * (g & 1)
                        B.pe(lambda t, g=g, pkt=pkt, off=off: t.matmul(
                            pctx[32 * g:32 * g + 5, :],
                            ckv1[:, 8 * pkt:8 * pkt + 5],
                            expsc[:, off:off + 512],
                            start=(pkt == 0), stop=(pkt == NKT - 1),
                            tile_position=(0, 32 * g), skip_group_check=True),
                            reads=[("expsc", pslot3), ("ckv1", pkt)],
                            writes=[PCTX])
                prev = (kt, slot3)
            pkt, pslot3 = prev
            for g in range(4):
                pair_off = 0 if g < 2 else 1024
                off = 2048 * pslot3 + pair_off + 512 * (g & 1)
                B.pe(lambda t, g=g, pkt=pkt, off=off: t.matmul(
                    pctx[32 * g:32 * g + 5, :],
                    ckv1[:, 8 * pkt:8 * pkt + 5],
                    expsc[:, off:off + 512],
                    start=(pkt == 0), stop=(pkt == NKT - 1),
                    tile_position=(0, 32 * g), skip_group_check=True),
                    reads=[("expsc", pslot3), ("ckv1", pkt)], writes=[PCTX])
            B.dve(lambda v: v.tensor_copy(ctx_sp[:], pctx[:]),
                  reads=[PCTX], writes=[("ctx_sp",)])
            for g in range(4):
                B.dma("syn", ctx_stage[(q * NSLOT + s_) * 20 + 5 * g:
                                       (q * NSLOT + s_) * 20 + 5 * (g + 1), :],
                      ctx_sp[32 * g:32 * g + 5, :],
                      grp="ctw", reads=[("ctx_sp",)], writes=[("ctx_stage", q, s_)])

    def stat_finish(pbank, pkey, ri, mi):
        B.dve(lambda v: v.tensor_scalar_mul(slot(0), pbank[0:1, :], 1.0 / D),
              reads=[pkey], writes=[("rp", 0)])
        B.dve(lambda v: v.tensor_scalar_mul(slot(1), pbank[32:33, :], 1.0 / D),
              reads=[pkey], writes=[("rp", 1)])
        B.dve(lambda v: v.tensor_mul(slot(2), slot(0), slot(0)),
              reads=[("rp", 0)], writes=[("rp", 2)], self_dep=True)
        B.dve(lambda v: v.tensor_sub(slot(1), slot(1), slot(2)),
              reads=[("rp", 2), ("rp", 1)], writes=[("rp", 1)], self_dep=True)
        B.dve(lambda v: v.tensor_scalar_add(slot(1), slot(1), EPS),
              reads=[("rp", 1)], writes=[("rp", 1)], self_dep=True)
        B.act(lambda s: s.activation(slot(ri), slot(1), AF.Ln),
              reads=[("rp", 1)], writes=[("rp", ri)])
        B.act(lambda s: s.activation(slot(ri), slot(ri), AF.Exp, scale=-0.5),
              reads=[("rp", ri)], writes=[("rp", ri)], self_dep=True)
        B.dve(lambda v: v.tensor_mul(slot(mi), slot(0), slot(ri)),
              reads=[("rp", 0), ("rp", ri)], writes=[("rp", mi)], self_dep=True)

    if TRUNC < 4:
        return finish(nc, B, stack)
    B.dve(lambda v: v.engine_nop(),
          writes=[("ckvlat", b) for b in range(NBLK)] +
                 [("c_sb", 0), ("c_sb", 1), ("ckvlat",)])
    for q in range(NQ):
        for s_ in range(NSLOT):
            for g in range(4):
                base = (q * NSLOT + s_) * 20 + 5 * g
                B.dma("syn", ctx_u[16 * s_ + 4 * g:16 * s_ + 4 * (g + 1), :],
                      ctx_stage[base:base + 4, :],
                      grp="cxr", reads=[("ctx_stage", q, s_)], writes=[("ctx_u", q)])
                B.dma("syn", dnm[4 * s_ + g:4 * s_ + g + 1, :],
                      ctx_stage[base + 4:base + 5, :],
                      grp="cxr", reads=[("ctx_stage", q, s_)], writes=[("dnm", q)])
        B.dve(lambda v: v.reciprocal(dtmp[:], dnm[:]),
              reads=[("dnm", q)], writes=[("dtmp",)])
        B.dve(lambda v: v.tensor_mul(dtmp2[:], dnm[:], dtmp[:]),
              reads=[("dnm", q), ("dtmp",)], writes=[("dtmp2",)], self_dep=True)
        B.dve(lambda v: v.tensor_scalar(dtmp2[:], dtmp2[:], -1.0, 2.0, ALU.mult, ALU.add),
              reads=[("dtmp2",)], writes=[("dtmp2",)], self_dep=True)
        B.dve(lambda v: v.tensor_mul(dtmp[:], dtmp[:], dtmp2[:]),
              reads=[("dtmp",), ("dtmp2",)], writes=[("dtmp",)], self_dep=True)
        B.pe(lambda t: t.matmul(rbank[2][:, :], e4t_sb[:], dtmp[:],
                                start=True, stop=True),
             reads=[("dtmp",), ("e4t_sb",)], writes=[RB[2]])
        B.dve(lambda v: v.tensor_mul(ctx_n[:], ctx_u[:], rbank[2][:, :]),
              reads=[("ctx_u", q), RB[2]], writes=[("ctx_n", q)])
        if DEBUG:
            B.dma("pool", ctxn_dbg[:, 512 * q:512 * (q + 1)], ctx_n[:],
                  grp="dbg", reads=[("ctx_n", q)], writes=[("ctxn_dbg",)])
        atb, sqb = [attn_c, attn_c2], [sq_c, sq_c2]

        def attn_stats(ot, q):
            po = ot % 2
            B.pe(lambda t, ot=ot, po=po: t.matmul(
                rbank[2][0:1, :], ones_bf[:], atb[po][:, :],
                start=(ot == 0), stop=(ot == NCH - 1), skip_group_check=True),
                reads=[("attn_c", po), ("ones_bf",)], writes=[RB[2]])
            B.pe(lambda t, ot=ot, po=po: t.matmul(
                rbank[2][32:33, :], ones_bf[:], sqb[po][:, :],
                start=(ot == 0), stop=(ot == NCH - 1), tile_position=(0, 32),
                skip_group_check=True),
                reads=[("sq_a", po), ("ones_bf",)], writes=[RB[2]])
            B.dma("syn", attn_sp[128 * ot:128 * (ot + 1), 512 * q:512 * (q + 1)],
                  atb[po][:, :], grp=f"aw{po}", reads=[("attn_c", po)],
                  writes=[("attn_spl", ot, q)])
            if DEBUG:
                B.dma("pool", attn_dbg[128 * ot:128 * (ot + 1), 512 * q:512 * (q + 1)],
                      atb[po][:, :], grp="dbg", reads=[("attn_c", po)],
                      writes=[("attn_dbg",)])

        pend = None
        for ot in range(NCH):
            pbk = ot % 2
            B.pe(lambda t, ot=ot, pbk=pbk: t.matmul(
                rbank[pbk][:, :], a_bf[:, 128 * ot:128 * (ot + 1)], ctx_n[:],
                start=True, stop=True),
                reads=[("ctx_n", q), ("a_bf", ot // 4)], writes=[RB[pbk]])
            B.dve(lambda v, ot=ot, pbk=pbk: v.tensor_scalar(
                atb[pbk][:, :], rbank[pbk][:, :],
                bc_sb[:, ot:ot + 1], None, ALU.add),
                reads=[RB[pbk], ("bc_sb",)], writes=[("attn_c", pbk)])
            B.dve(lambda v, pbk=pbk: v.tensor_mul(sqb[pbk][:, :], atb[pbk][:, :],
                                                  atb[pbk][:, :]),
                  reads=[("attn_c", pbk)], writes=[("sq_a", pbk)])
            if pend is not None:
                attn_stats(pend, q)
            pend = ot
        attn_stats(pend, q)
        stat_finish(rbank[2], RB[2], 3, 4)
        B.pe(lambda t: t.matmul(rbank[3][:, :], onesr_f32[:], slot(3),
                                start=True, stop=True),
             reads=[("rp", 3), ("onesr_f32",)], writes=[RB[3]])
        B.pe(lambda t: t.matmul(rbank[4][:, :], onesr_f32[:], slot(4),
                                start=True, stop=True),
             reads=[("rp", 4), ("onesr_f32",)], writes=[RB[4]])
        B.dve(lambda v: v.tensor_copy(rb_sb[:], rbank[3][:, :]),
              reads=[RB[3]], writes=[("rb_sb",)])
        B.dve(lambda v: v.tensor_copy(mb_sb[:], rbank[4][:, :]),
              reads=[RB[4]], writes=[("mb_sb",)])
        arb, x1b = [attn_r, attn_rb], [x1f_c, x1f_c2]
        for c in range(NCH):
            par = c % 2
            B.dma("syn", xf_c[par][:, :],
                  xT[128 * c:128 * (c + 1), 512 * q:512 * (q + 1)],
                  grp=f"xf{par}", writes=[("xf_c", par)])
            B.dma("act", arb[par][:, :],
                  attn_sp[128 * c:128 * (c + 1), 512 * q:512 * (q + 1)],
                  grp=f"ar{par}", reads=[("attn_spl", c, q)], writes=[("attn_r", par)])
            B.dve(lambda v, par=par: v.tensor_mul(tt1[:], arb[par][:, :], rb_sb[:]),
                  reads=[("attn_r", par), ("rb_sb",)], writes=[("tt1",)])
            B.dve(lambda v: v.tensor_sub(tt2[:], tt1[:], mb_sb[:]),
                  reads=[("tt1",), ("mb_sb",)], writes=[("tt2",)])
            B.dve(lambda v, par=par: v.tensor_add(x1b[par][:, :], tt2[:],
                                                  xf_c[par][:, :]),
                  reads=[("tt2",), ("xf_c", par)], writes=[("x1f_c", par)])
            B.act(lambda s, c=c, q=q, par=par: s.activation(
                x1_bf[:, TOK * c + 512 * q:TOK * c + 512 * (q + 1)],
                x1b[par][:, :], AF.Copy),
                reads=[("x1f_c", par)], writes=[("x1_bf", c, q)])
            B.dma("syn", x1_spill[128 * c:128 * (c + 1), 512 * q:512 * (q + 1)],
                  x1b[par][:, :], grp=f"x1w{par}", reads=[("x1f_c", par)],
                  writes=[("x1_spill", c, q)])

    if TRUNC < 5:
        return finish(nc, B, stack)
    B.dve(lambda v: v.engine_nop(),
          writes=[("xblk", 0, qq) for qq in range(8)] +
                 [("xblk", 1, qq) for qq in range(8)] +
                 [("cqb", 0), ("cqb", 1), ("qsb", 0), ("qsb", 1),
                  ("c_sb", 0), ("c_sb", 1)])
    for c in range(NCH):
        B.dma("pool", enc_sb[:, 40 * c:40 * (c + 1)], encT_d[128 * c:128 * (c + 1), :],
              grp="enc", writes=[("enc_sb",), ("cqb", 0), ("cqb", 1),
                                 ("qsb", 0), ("qsb", 1)])
    sqln = [sq_c, sq_c2]
    for q in range(NQ):
        for c in range(NCH):
            par = c % 2
            B.act(lambda s, c=c, q=q, par=par: s.activation(
                sqln[par][:, :], x1_bf[:, TOK * c + 512 * q:TOK * c + 512 * (q + 1)],
                AF.Square),
                reads=[("x1_bf", c, q)], writes=[("sq_a", par)])
            B.pe(lambda t, c=c, q=q: t.matmul(
                rbank[2][0:1, :], ones_bf[:],
                x1_bf[:, TOK * c + 512 * q:TOK * c + 512 * (q + 1)],
                start=(c == 0), stop=(c == NCH - 1), skip_group_check=True),
                reads=[("x1_bf", c, q), ("ones_bf",)], writes=[RB[2]])
            B.pe(lambda t, c=c, par=par: t.matmul(
                rbank[2][32:33, :], ones_bf[:], sqln[par][:, :],
                start=(c == 0), stop=(c == NCH - 1), tile_position=(0, 32),
                skip_group_check=True),
                reads=[("sq_a", par), ("ones_bf",)], writes=[RB[2]])
        stat_finish(rbank[2], RB[2], 3, 4)
        B.pe(lambda t: t.matmul(rbank[3][:, :], onesr_f32[:], slot(4),
                                start=True, stop=True),
             reads=[("rp", 4), ("onesr_f32",)], writes=[RB[3]])
        B.dve(lambda v, q=q: v.tensor_copy(mb3[:, 512 * q:512 * (q + 1)],
                                           rbank[3][:, :]),
              reads=[RB[3]], writes=[("mb3", q)])
        B.pe(lambda t: t.matmul(rbank[3][:, :], onesr_f32[:], slot(3),
                                start=True, stop=True),
             reads=[("rp", 3), ("onesr_f32",)], writes=[RB[3]])
        B.dve(lambda v, q=q: v.tensor_scalar_mul(rb3[:, 512 * q:512 * (q + 1)],
                                                 rbank[3][:, :], -1.0),
              reads=[RB[3]], writes=[("rb3", q)])

    fcb, sqf = [ffn_c, attn_c2], [sq_c, sq_c2]

    def ffn_stats(ot, q):
        sbk, skey = (rbank[2], RB[2]) if q == 0 else (pctx, PCTX)
        B.pe(lambda t, ot=ot, sbk=sbk, q=q: t.matmul(
            sbk[0:1, :], ones_bf[:], fcb[q][:, :],
            start=(ot == 1), stop=(ot == NCH), skip_group_check=True),
            reads=[("ffn_cq", q), ("ones_bf",)], writes=[skey])
        B.pe(lambda t, ot=ot, sbk=sbk, q=q: t.matmul(
            sbk[32:33, :], ones_bf[:], sqf[q][:, :],
            start=(ot == 1), stop=(ot == NCH), tile_position=(0, 32),
            skip_group_check=True),
            reads=[("sq_f", q), ("ones_bf",)], writes=[skey])
        B.dma("syn", ffn_spill[128 * (ot - 1):128 * ot, 512 * q:512 * (q + 1)],
              fcb[q][:, :], grp=f"fw{q}", reads=[("ffn_cq", q)],
              writes=[("ffn_sp", ot, q)])

    pend = None
    for ot in range(NCH + 1):
        par = ot % 2
        if ot > 0:
            for sub in range(4):
                B.dma("pool", wstr[par][:, 1024 * sub:1024 * (sub + 1)],
                      wbT_d.rearrange("(cc p) o -> p cc o", p=128)
                      [:, 8 * sub:8 * sub + 8, 128 * (ot - 1):128 * ot],
                      grp=f"wb{par}", writes=[("wstr", par)])
        for q in range(NQ):
            ww = 40 if ot == 0 else 128
            wkey = ("enc_sb",) if ot == 0 else ("wstr", par)
            for c in range(NCH):
                if ot == 0:
                    lt = enc_sb[:, 40 * c:40 * (c + 1)]
                else:
                    lt = wstr[par].rearrange("p (cc o) -> p cc o", o=128)[:, c, :]
                if q == 0:
                    B.pe(lambda t, c=c, lt=lt, ww=ww: t.matmul(
                        paux[0:ww, 0:1], lt, ones_bf[:],
                        start=(c == 0), stop=(c == NCH - 1)),
                        reads=[wkey, ("ones_bf",)], writes=[PAUX])
                B.pe(lambda t, c=c, q=q, lt=lt, ww=ww, par=par: t.matmul(
                    rbank[par][0:ww, :], lt,
                    x1_bf[:, TOK * c + 512 * q:TOK * c + 512 * (q + 1)],
                    start=(c == 0), stop=(c == NCH - 1)),
                    reads=[wkey, ("x1_bf", c, q)], writes=[RB[par]])
            if q == 0:
                B.dve(lambda v, ww=ww: v.tensor_copy(senc_sb[0:ww, :], paux[0:ww, 0:1]),
                      reads=[PAUX], writes=[("senc_sb",)], self_dep=True)
            if ot == 0:
                B.dve(lambda v, q=q, par=par: v.scalar_tensor_tensor(
                    tt1[0:40, :], mb3[0:40, 512 * q:512 * (q + 1)], senc_sb[0:40, :],
                    rbank[par][0:40, :], ALU.mult, ALU.subtract),
                    reads=[("mb3", q), ("senc_sb",), RB[par]], writes=[("tt1",)],
                    self_dep=True)
                B.dve(lambda v, q=q: v.tensor_mul(tt2[0:40, :], tt1[0:40, :],
                                                  rb3[0:40, 512 * q:512 * (q + 1)]),
                      reads=[("tt1",), ("rb3", q)], writes=[("tt2",)])
                B.dve(lambda v, q=q: v.tensor_copy(u_sb[0:40, 512 * q:512 * (q + 1)],
                                                   tt2[0:40, :]),
                      reads=[("tt2",)], writes=[("u_sb", q)])
                if DEBUG:
                    B.dma("pool", u_dbg[:, 512 * q:512 * (q + 1)],
                          u_sb[0:40, 512 * q:512 * (q + 1)],
                          grp="dbg", reads=[("u_sb", q)], writes=[("u_dbg",)])
            else:
                ub = 4 + (1 - par)
                B.pe(lambda t, ot=ot, q=q, ub=ub: t.matmul(
                    rbank[ub][:, :], hcat[0:40, 128 * (ot - 1):128 * ot],
                    u_sb[0:40, 512 * q:512 * (q + 1)], start=True, stop=True),
                    reads=[("u_sb", q), ("hcat",)], writes=[RB[ub]])
                B.dve(lambda v, q=q, par=par: v.scalar_tensor_tensor(
                    tt1[:], mb3[:, 512 * q:512 * (q + 1)], senc_sb[:],
                    rbank[par][:, :], ALU.mult, ALU.subtract),
                    reads=[("mb3", q), ("senc_sb",), RB[par]], writes=[("tt1",)],
                    self_dep=True)
                B.dve(lambda v, q=q: v.tensor_mul(tt2[:], tt1[:],
                                                  rb3[:, 512 * q:512 * (q + 1)]),
                      reads=[("tt1",), ("rb3", q)], writes=[("tt2",)])
                B.dve(lambda v, ot=ot, ub=ub, q=q: v.scalar_tensor_tensor(
                    fcb[q][:, :], tt2[:], bb_sb[:, ot - 1:ot], rbank[ub][:, :],
                    ALU.add, ALU.add),
                    reads=[("tt2",), ("bb_sb",), RB[ub]], writes=[("ffn_cq", q)])
                B.act(lambda s, q=q: s.activation(sqf[q][:, :], fcb[q][:, :],
                                                  AF.Square),
                      reads=[("ffn_cq", q)], writes=[("sq_f", q)])
                if pend is not None:
                    ffn_stats(*pend)
                pend = (ot, q)
    ffn_stats(*pend)

    B.dve(lambda v: v.engine_nop(),
          writes=[("a_bf", og) for og in range(8)])
    st4slots = [(5, 6), (7, 8)]
    for q in range(NQ):
        sbk, skey = (rbank[2], RB[2]) if q == 0 else (pctx, PCTX)
        stat_finish(sbk, skey, st4slots[q][0], st4slots[q][1])

    for q in range(NQ):
        ri, mi = st4slots[q]
        B.pe(lambda t, ri=ri: t.matmul(rbank[3][:, :], onesr_f32[:],
                                       slot(ri), start=True, stop=True),
             reads=[("rp", ri), ("onesr_f32",)], writes=[RB[3]])
        B.dve(lambda v: v.tensor_copy(rb4[:], rbank[3][:, :]),
              reads=[RB[3]], writes=[("rb4", q)])
        B.pe(lambda t, mi=mi: t.matmul(rbank[3][:, :], onesr_f32[:],
                                       slot(mi), start=True, stop=True),
             reads=[("rp", mi), ("onesr_f32",)], writes=[RB[3]])
        B.dve(lambda v: v.tensor_copy(mb4[:], rbank[3][:, :]),
              reads=[RB[3]], writes=[("mb4", q)])
        frb, ocb = [ffn_r, ffn_r2], [out_c, out_c2]
        for c in range(NCH):
            par = c % 2
            B.dma("syn", xf_c[par][:, :],
                  x1_spill[128 * c:128 * (c + 1), 512 * q:512 * (q + 1)],
                  grp=f"xf{par}", reads=[("x1_spill", c, q)], writes=[("xf_c", par)])
            B.dma("act", frb[par][:, :],
                  ffn_spill[128 * c:128 * (c + 1), 512 * q:512 * (q + 1)],
                  grp=f"fr{par}", reads=[("ffn_sp", c + 1, q)], writes=[("ffn_r", par)])
            B.dve(lambda v, par=par: v.tensor_mul(tt1[:], frb[par][:, :], rb4[:]),
                  reads=[("ffn_r", par), ("rb4", q)], writes=[("tt1",)])
            B.dve(lambda v: v.tensor_sub(tt2[:], tt1[:], mb4[:]),
                  reads=[("tt1",), ("mb4", q)], writes=[("tt2",)])
            B.dve(lambda v, par=par: v.tensor_add(ocb[par][:, :], tt2[:],
                                                  xf_c[par][:, :]),
                  reads=[("tt2",), ("xf_c", par)], writes=[("out_c", par)])
            B.dma("act", out[128 * c:128 * (c + 1), 512 * q:512 * (q + 1)],
                  ocb[par][:, :], grp=f"ow{par}", reads=[("out_c", par)],
                  writes=[("out", c, q)])

    return finish(nc, B, stack)


def finish(nc, B, stack):
    B.finalize()
    sems = {s: stack.enter_context(nc.semaphore(s)) for s in Builder.STREAMS}
    for g in B.grps:
        sems[g] = stack.enter_context(nc.semaphore(g))
    grp_totals = {g: 16 * len(ops) for g, ops in B.grps.items()}
    with nc.Block() as block:
        @block.tensor
        def _(t):
            B.emit_stream("pe", t, sems)

        @block.scalar
        def _(s):
            B.emit_stream("act", s, sems)

        @block.vector
        def _(v):
            B.emit_stream("dve", v, sems)

        @block.gpsimd
        def _(g):
            B.emit_stream("pool", g, sems)

        @block.sync
        def _(sy):
            B.emit_stream("syn", sy, sems)
            for g, tot in grp_totals.items():
                sy.wait_ge(sems[g], tot)

    stack.close()
    return nc


def host_constants(inputs):
    wq = np.asarray(inputs["L0_attn_Wq_d"], np.float32)
    bq = np.asarray(inputs["L0_attn_bq_d"], np.float32)
    wqk = np.asarray(inputs["L0_attn_W_qk"], np.float32)
    bqk = np.asarray(inputs["L0_attn_b_qk"], np.float32)
    wkv = np.asarray(inputs["L0_attn_Wkv_d"], np.float32)
    bkv = np.asarray(inputs["L0_attn_bkv_d"], np.float32)
    wvu = np.asarray(inputs["L0_attn_Wv_u"], np.float32)
    bvu = np.asarray(inputs["L0_attn_bv_u"], np.float32)
    wo = np.ascontiguousarray(np.asarray(inputs["L0_attn_Wo"], np.float32))
    bo = np.asarray(inputs["L0_attn_bo"], np.float32)
    wb = np.asarray(inputs["L0_ffn_W_base"], np.float32)
    bb = np.asarray(inputs["L0_ffn_b_base"], np.float32)
    enc0 = np.asarray(inputs["L0_ffn_enc0"], np.float32)
    enc1 = np.asarray(inputs["L0_ffn_enc1"], np.float32)

    d = {}
    d["wcat"] = np.ascontiguousarray(
        np.concatenate([wq, wkv, np.ones((D, 1), np.float32)], axis=1))
    d["b17_d"] = np.concatenate([bq, bkv, np.zeros(1, np.float32)])[:, None].copy()
    wqe = np.zeros((5, 16 * 32), np.float32)
    for h in range(32):
        for l in range(4):
            wqe[l, 16 * h:16 * h + 12] = wqk[:, 4 * h + l]
            wqe[l, 16 * h + 12] = bqk[4 * h + l]
        wqe[4, 16 * h + 13] = -2.0
    d["wqk_ext"] = wqe
    d["wqk_aug"] = np.ascontiguousarray(np.concatenate([wqk, bqk[None, :]], axis=0))
    e4 = np.zeros((128, 32), np.float32)
    e4t = np.zeros((32, 128), np.float32)
    for h in range(32):
        for l in range(4):
            e4[4 * h + l, h] = 1.0
            e4t[h, 4 * h + l] = 1.0
    d["e4_d"] = e4
    d["e4t_d"] = e4t
    d["ones_dr"] = np.ones((1, NSLOT * TOK), np.float32)
    d["ones_col"] = np.ones((128, 1), np.float32)
    d["ident_d"] = np.eye(16, dtype=np.float32)
    d["wvuT"] = np.ascontiguousarray(wvu.T)
    d["bvu_d"] = np.ascontiguousarray(bvu.reshape(32, 128).T)
    d["wo_d"] = wo
    d["bo_d"] = np.ascontiguousarray(bo.reshape(32, 128).T)
    d["wbT_d"] = np.ascontiguousarray(wb.T)
    d["encT_d"] = np.ascontiguousarray(np.concatenate([enc0.T, enc1.T], axis=1))
    d["bb_d"] = np.ascontiguousarray(bb.reshape(32, 128).T)
    d["co0"] = np.ascontiguousarray(np.asarray(inputs["L0_ffn_coefs0"], np.float32))
    d["co1"] = np.ascontiguousarray(np.asarray(inputs["L0_ffn_coefs1"], np.float32))
    d["mk0"] = np.ascontiguousarray(np.asarray(inputs["L0_ffn_mask0"]).astype(np.uint8))
    d["mk1"] = np.ascontiguousarray(np.asarray(inputs["L0_ffn_mask1"]).astype(np.uint8))

    n = D
    j = np.arange(n, dtype=np.float64)
    M = np.empty((n, n), np.float32)
    M[0] = np.float32(1.0 / math.sqrt(n))
    f = np.arange(1, n // 2, dtype=np.float64)
    ang = (2.0 * np.pi / n) * np.outer(f, j)
    M[1:n - 1:2] = (math.sqrt(2.0 / n) * np.cos(ang)).astype(np.float32)
    M[2:n - 1:2] = (math.sqrt(2.0 / n) * np.sin(ang)).astype(np.float32)
    M[n - 1] = (((-1.0) ** j) / math.sqrt(n)).astype(np.float32)
    d["mmat"] = M.astype(ml_dtypes.bfloat16)
    return d


def kernel(**inputs):
    x = np.asarray(inputs["L0_attn_x"], np.float32)
    const = host_constants(inputs)
    nc = build_program()
    in_maps = []
    for core in range(8):
        b, r = core // 4, core % 4
        order = np.concatenate([np.arange(r * TOK, (r + 1) * TOK),
                                np.arange(0, r * TOK),
                                np.arange((r + 1) * TOK, SB)])
        xTc = np.ascontiguousarray(x[b][order].T)
        m = dict(const)
        m["xT"] = xTc
        in_maps.append(m)
    res = run_bass_kernel_spmd(nc, in_maps, core_ids=list(range(8)))
    outf = np.empty((2, SB, D), np.float32)
    for core in range(8):
        b, r = core // 4, core % 4
        outf[b, r * TOK:(r + 1) * TOK, :] = res.results[core]["out"].T
    return outf


# version 06
# speedup vs baseline: 1.7150x; 1.1424x over previous
import contextlib
import math
import numpy as np
import ml_dtypes
import concourse.bass as bass
import concourse.mybir as mybir
from concourse.bass_utils import run_bass_kernel_spmd

F32 = mybir.dt.float32
BF16 = mybir.dt.bfloat16
U8 = mybir.dt.uint8
AF = mybir.ActivationFunctionType
ALU = mybir.AluOpType

D = 4096
SB = 4096
TOK = 1024
NCH = 32
NKT = 32
NBLK = 8
NSLOT = 8
NQ = 2
EPS = 1e-5
CSHIFT = 8.0

DEBUG = False
TRUNC = 5


class Op:
    __slots__ = ("stream", "idx", "fn", "deps", "inc", "num", "grp", "gseq")

    def __init__(self, stream, idx, fn):
        self.stream = stream
        self.idx = idx
        self.fn = fn
        self.deps = []
        self.inc = False
        self.num = None
        self.grp = None
        self.gseq = 0


class Builder:
    STREAMS = ("pe", "act", "dve", "pool", "syn")

    def __init__(self):
        self.ops = {s: [] for s in self.STREAMS}
        self.last_write = {}
        self.readers = {}
        self.grps = {}
        self.gseq = 0

    def add(self, stream, fn, reads=(), writes=(), self_dep=False):
        op = Op(stream, len(self.ops[stream]), fn)
        self.gseq += 1
        op.gseq = self.gseq
        deps = []
        for b in reads:
            w = self.last_write.get(b)
            if w is not None:
                deps.append(w)
        for b in writes:
            w = self.last_write.get(b)
            if w is not None:
                deps.append(w)
            deps.extend(self.readers.get(b, ()))
        if stream == "pe" and not self_dep:
            op.deps = [d for d in deps if d.stream != stream]
        else:
            op.deps = deps
        self.ops[stream].append(op)
        for b in writes:
            self.last_write[b] = op
            self.readers[b] = []
        for b in reads:
            self.readers.setdefault(b, []).append(op)
        return op

    def pe(self, fn, reads=(), writes=()):
        return self.add("pe", fn, reads, writes)

    def act(self, fn, reads=(), writes=(), self_dep=False):
        return self.add("act", fn, reads, writes, self_dep=self_dep)

    def dve(self, fn, reads=(), writes=(), self_dep=False):
        return self.add("dve", fn, reads, writes, self_dep=self_dep)

    def dma(self, q, out_ap, in_ap, grp=None, reads=(), writes=(), **kw):
        assert grp is not None, "dma needs a sem group"
        def fn(eng):
            return eng.dma_start(out_ap, in_ap, **kw)
        op = self.add(q, fn, reads, writes)
        op.grp = grp
        self.grps.setdefault(grp, []).append(op)
        return op

    def finalize(self):
        for g, ops in self.grps.items():
            for i, op in enumerate(ops):
                op.num = 16 * (i + 1)
                op.inc = True
        for s in self.STREAMS:
            for op in self.ops[s]:
                merged = {}
                for d in op.deps:
                    key = d.grp if d.grp is not None else d.stream
                    cur = merged.get(key)
                    if cur is None:
                        merged[key] = d
                    elif d.grp is not None:
                        if d.num > cur.num:
                            merged[key] = d
                    elif d.idx > cur.idx:
                        merged[key] = d
                op.deps = list(merged.values())
                for d in op.deps:
                    d.inc = True
        for s in ("pe", "act", "dve"):
            n = 0
            for op in self.ops[s]:
                if op.grp is not None:
                    continue
                if op.inc:
                    n += 1
                op.num = n

    def emit_stream(self, s, eng, sems):
        import bisect
        gseqs = {g: [o.gseq for o in ops] for g, ops in self.grps.items()}
        wm = {}
        for op in self.ops[s]:
            for d in op.deps:
                key = d.grp if d.grp is not None else d.stream
                if d.grp is not None:
                    k = bisect.bisect_left(gseqs[d.grp], op.gseq)
                    val = 16 * max(k, (d.num + 15) // 16)
                else:
                    val = d.num
                if val > wm.get(key, 0):
                    eng.wait_ge(sems[key], val)
                    wm[key] = val
            inst = op.fn(eng)
            if op.inc:
                key = op.grp if op.grp is not None else s
                inst.then_inc(sems[key], 16 if op.grp is not None else 1)


def build_program():
    nc = bass.Bass("TRN2", target_bir_lowering=False)
    B = Builder()
    stack = contextlib.ExitStack()

    def din(name, shape, dtype=F32):
        return nc.dram_tensor(name, list(shape), dtype, kind="ExternalInput").ap()

    xT = din("xT", (1024, 16384))
    wcat = din("wcat", (D, 17))
    b17_d = din("b17_d", (17, 1))
    wqk_ext = din("wqk_ext", (5, 16 * 32))
    wqk_aug = din("wqk_aug", (13, 128))
    e4_d = din("e4_d", (128, 32))
    e4t_d = din("e4t_d", (32, 128))
    ones_dr = din("ones_dr", (1, NSLOT * TOK))
    ones_col = din("ones_col", (128, 1))
    ident_d = din("ident_d", (16, 16))
    wvuT = din("wvuT", (D, 4))
    bvu_d = din("bvu_d", (128, 32))
    wo_d = din("wo_d", (1024, 16384))
    bo_d = din("bo_d", (128, 32))
    wbT_d = din("wbT_d", (D, D))
    encT_d = din("encT_d", (D, 40))
    bb_d = din("bb_d", (128, 32))
    co0 = din("co0", (3, D, 32))
    co1 = din("co1", (3, D, 8))
    mk0 = din("mk0", (3, D, 32), U8)
    mk1 = din("mk1", (3, D, 8), U8)
    mmat = din("mmat", (1024, 16384), BF16)

    dbg = "ExternalOutput" if DEBUG else "Internal"
    out = nc.dram_tensor("out", [D, TOK], F32, kind="ExternalOutput").ap()
    c_stage = nc.dram_tensor("c_stage", [32, TOK], F32, kind=dbg).ap()
    ctx_stage = nc.dram_tensor("ctx_stage", [NQ * NSLOT * 20, 512], F32, kind=dbg).ap()
    x1_spill = nc.dram_tensor("x1_spill", [D, TOK], F32, kind=dbg).ap()
    ffn_spill = nc.dram_tensor("ffn_spill", [D, TOK], BF16, kind=dbg).ap()
    attn_dbg = nc.dram_tensor("attn_dbg", [D, TOK], F32, kind=dbg).ap()
    attn_sp = nc.dram_tensor("attn_sp", [D, TOK], BF16, kind="Internal").ap()
    cqkv_dbg = nc.dram_tensor("cqkv_dbg", [17, SB], F32, kind=dbg).ap()
    q_dbg = nc.dram_tensor("q_dbg", [128, TOK], F32, kind=dbg).ap()
    a_dbg = nc.dram_tensor("a_dbg", [128, D], F32, kind=dbg).ap()
    bc_dbg = nc.dram_tensor("bc_dbg", [128, 32], F32, kind=dbg).ap()
    ctxn_dbg = nc.dram_tensor("ctxn_dbg", [128, TOK], F32, kind=dbg).ap()
    h_dbg = nc.dram_tensor("h_dbg", [40, D], F32, kind=dbg).ap()
    u_dbg = nc.dram_tensor("u_dbg", [40, TOK], F32, kind=dbg).ap()
    g_dbg = nc.dram_tensor("g_dbg", [128, NCH * 40], F32, kind=dbg).ap()

    def sbuf(name, shape, dtype):
        return stack.enter_context(nc.sbuf_tensor(name, list(shape), dtype))

    def psum(name, shape, dtype=F32):
        return stack.enter_context(nc.psum_tensor(name, list(shape), dtype))

    proB = sbuf("proB", (128, 4096), F32)
    proB_bf = proB.bitcast(BF16)
    xblk = [proB_bf[:, 0:2048], proB_bf[:, 2048:4096]]
    xsq_t = sbuf("xsq", (128, 2048), BF16)
    wcat_sb = sbuf("wcat_sb", (128, NCH * 17), BF16)
    b17_sb = sbuf("b17_sb", (17, 1), F32)
    sneg = sbuf("sneg", (17, 1), F32)
    blk_f = sbuf("blk_f", (17, 512), F32)
    blk_fix = sbuf("blk_fix", (17, 512), F32)
    rowpool = sbuf("rowpool", (1, 4608), F32)
    ones17 = sbuf("ones17", (1, 32), F32)
    cqb = proB_bf[:, 4096:5120]
    qsb = proB_bf[:, 5120:6144]
    c_sb = proB[:, 3072:4096]
    phA = sbuf("phA", (128, NKT * 128 + 3 * 2048 + 2 * TOK), BF16)
    kh2 = phA[:, 0:NKT * 128]
    cqm2 = [phA[:, NKT * 128 + 3 * 2048:NKT * 128 + 3 * 2048 + TOK],
            phA[:, NKT * 128 + 3 * 2048 + TOK:]]
    ckvlat_t = sbuf("ckvlat", (128, SB), BF16)
    ckvlat = ckvlat_t
    ckv32 = ckvlat_t.bitcast(F32)
    ckv1 = sbuf("ckv1", (128, NKT * 8), BF16)
    expsc = phA[:, NKT * 128:NKT * 128 + 3 * 2048]
    ctx_sp = sbuf("ctx_sp", (128, 512), F32)
    ctx_u = sbuf("ctx_u", (128, 512), F32)
    dnm = sbuf("dnm", (32, 512), F32)
    dtmp = sbuf("dtmp", (32, 512), F32)
    dtmp2 = sbuf("dtmp2", (32, 512), F32)
    ctx_n = sbuf("ctx_n", (128, 512), BF16)
    a_bf = sbuf("a_bf", (128, D), BF16)
    bc_sb = sbuf("bc_sb", (128, 32), F32)
    bo_sb = sbuf("bo_sb", (128, 32), F32)
    bb_sb = sbuf("bb_sb", (128, 32), F32)
    attn_c = sbuf("attn_c", (128, 512), BF16)
    attn_c2 = sbuf("attn_c2", (128, 512), BF16)
    attn_r = sbuf("attn_r", (128, 512), BF16)
    attn_rb = sbuf("attn_rb", (128, 512), BF16)
    sq_c = sbuf("sq_c", (128, 512), BF16)
    sq_c2 = sbuf("sq_c2", (128, 512), BF16)
    x1_bf = sbuf("x1_bf", (128, NCH * TOK), BF16)
    xf_c = [sbuf(f"xf_c{i}", (128, 512), F32) for i in range(2)]
    x1f_c = sbuf("x1f_c", (128, 512), F32)
    x1f_c2 = sbuf("x1f_c2", (128, 512), F32)
    hcat = sbuf("hcat", (40, D), BF16)
    g0all = sbuf("g0all", (128, NCH * 32), BF16)
    g1all = sbuf("g1all", (128, NCH * 8), BF16)
    u_sb = sbuf("u_sb", (40, TOK), BF16)
    wstr = [sbuf(f"wstr{i}", (128, 4096), BF16) for i in range(2)]
    wvall = wstr[0]
    x1f = x1_bf.bitcast(F32)
    x1u8 = x1_bf.bitcast(U8)
    cfw = x1f[:, 0:1024]
    acc0 = x1f[:, 1024:2048]
    gtmpw = x1f[:, 2048:3072]
    mkwf = x1_bf[:, 6144:7168]
    mkw8 = x1u8[:, 14336:15360]
    bcrow = x1f[0:1, 4096:8192]
    xraw = [x1f[:, 8192:10240], x1f[:, 10240:12288]]
    xsq = [xsq_t, x1_bf[:, 24576:26624]]
    wos2 = [phA[:, 0:2048], phA[:, 2048:4096]]
    ms2 = [phA[:, 4096:6144], phA[:, 6144:8192]]
    enc_sb = proB_bf[:, 4096:5376]
    wqk_sb = sbuf("wqk_sb", (5, 16 * 32), BF16)
    wqka_sb = sbuf("wqka_sb", (13, 128), BF16)
    e4_sb = sbuf("e4_sb", (128, 32), BF16)
    e4t_sb = sbuf("e4t_sb", (32, 128), F32)
    ident_sb = sbuf("ident_sb", (16, 16), F32)
    ones_bf = sbuf("ones_bf", (128, 1), BF16)
    ones_f32 = sbuf("ones_f32", (128, 1), F32)
    onesr_f32 = sbuf("onesr_f32", (1, 128), F32)
    bvu_sb = sbuf("bvu_sb", (128, NCH), BF16)
    rb_sb = proB[:, 3072:3584]
    mb_sb = proB[:, 3584:4096]
    rb3 = proB[:, 0:1024]
    mb3 = proB[:, 1024:2048]
    a32 = a_bf.bitcast(F32)
    rb4 = a32[:, 0:512]
    mb4 = a32[:, 512:1024]
    senc_sb = sbuf("senc_sb", (128, 1), F32)
    ffn_c = ckvlat_t[:, 0:512]
    ffn_r = ckvlat_t[:, 512:1024]
    ffn_r2 = a_bf[:, 3072:3584]
    out_c = ckv32[:, 512:1024]
    out_c2 = proB[:, 2048:2560]
    tt1 = ckv32[:, 1024:1536]
    tt2 = ckv32[:, 1536:2048]

    def slot(i, j=0):
        return rowpool[0:1, 512 * i:512 * (i + 1)]

    ring = psum("ring", (128, 3072))
    pctx = psum("pctx", (128, 512))
    paux = psum("paux", (128, 512))
    rbank = [ring[:, 512 * i:512 * (i + 1)] for i in range(6)]
    RB = [("rng", i) for i in range(6)]
    PCTX = ("pctx",)
    PAUX = ("paux",)



    for c in range(NCH):
        B.dma("pool", wcat_sb[:, 17 * c:17 * (c + 1)], wcat[128 * c:128 * (c + 1), :],
              grp="cpool", writes=[("wcat_sb",)])
    B.dma("pool", wqk_sb[:], wqk_ext[:], grp="cpool", writes=[("wqk_sb",)])
    B.dma("pool", wqka_sb[:], wqk_aug[:], grp="cpool", writes=[("wqka_sb",)])
    B.dma("pool", e4_sb[:], e4_d[:], grp="cpool", writes=[("e4_sb",)])
    B.dma("syn", e4t_sb[:], e4t_d[:], grp="csyn", writes=[("e4t_sb",)])
    B.dma("syn", b17_sb[:], b17_d[:], grp="csyn", writes=[("b17_sb",)])
    B.dma("syn", ident_sb[:], ident_d[:], grp="csyn", writes=[("ident_sb",)])
    B.dma("syn", ones17[:], ones_dr[0:1, 0:32], grp="csyn", writes=[("ones17",)])
    B.dma("pool", ones_bf[:], ones_col[:], grp="cpool", writes=[("ones_bf",)])
    B.dma("syn", ones_f32[:], ones_col[:], grp="csyn", writes=[("ones_f32",)])
    B.dma("syn", onesr_f32[:], ones_dr[0:1, 0:128], grp="csyn", writes=[("onesr_f32",)])
    B.dma("syn", bo_sb[:], bo_d[:], grp="csyn", writes=[("bo_sb",)])
    B.dma("syn", bb_sb[:], bb_d[:], grp="csyn", writes=[("bb_sb",)])
    B.dma("pool", bvu_sb[:], bvu_d[:], grp="cpool", writes=[("bvu_sb",)])
    B.dve(lambda v: v.memset(wvall[:], 0.0), writes=[("wstr", 0)])
    for c in range(NCH):
        B.dma("pool", wvall[:, 128 * c + 4 * c:128 * c + 4 * c + 4],
              wvuT[128 * c:128 * (c + 1), :], grp="cpool", writes=[("wstr", 0)])
    B.dve(lambda v: v.memset(ckvlat[:], 1.0), writes=[("ckvlat",)])
    B.dve(lambda v: v.memset(ckv1[:], 1.0), writes=[("ckv1",)])
    B.dve(lambda v: v.memset(cqm2[0][:], 0.0), writes=[("cqm", 0)])
    B.dve(lambda v: v.memset(cqm2[1][:], 0.0), writes=[("cqm", 1)])
    B.dve(lambda v: v.memset(cqb[:], 0.0), writes=[("cqb",)])
    B.dma("pool", cqb[12:13, :], ones_dr[0:1, 0:TOK], grp="cpool", writes=[("cqb",)])
    for bf_ in range(2):
        for g in range(4):
            B.dma("pool", cqm2[bf_][32 * g + 12:32 * g + 13, :],
                  ones_dr[0:1, 0:TOK], grp="cpool", writes=[("cqm", bf_)])

    for half, (co_, mk_, gall, wdt) in enumerate(
            [(co0, mk0, g0all, 32), (co1, mk1, g1all, 8)]):
        W = NCH * wdt
        for e in range(3):
            B.dma("syn", cfw[:, 0:W].rearrange("p (cc w) -> p cc w", w=wdt),
                  co_[e, :, :].rearrange("(cc p) w -> p cc w", p=128),
                  grp=f"gld{half}", writes=[("cfe",)])
            B.dma("syn", mkw8[:, 0:W].rearrange("p (cc w) -> p cc w", w=wdt),
                  mk_[e, :, :].rearrange("(cc p) w -> p cc w", p=128),
                  grp=f"gld{half}", writes=[("mku",)])
            B.dve(lambda v, W=W: v.tensor_copy(mkwf[:, 0:W], mkw8[:, 0:W]),
                  reads=[("mku",)], writes=[("mkf",)], self_dep=True)
            if e == 0:
                B.dve(lambda v, W=W: v.tensor_mul(acc0[:, 0:W], cfw[:, 0:W],
                                                  mkwf[:, 0:W]),
                      reads=[("cfe",), ("mkf",)], writes=[("ga",)], self_dep=True)
            else:
                B.dve(lambda v, W=W: v.tensor_mul(gtmpw[:, 0:W], cfw[:, 0:W],
                                                  mkwf[:, 0:W]),
                      reads=[("cfe",), ("mkf",)], writes=[("gtmp",)], self_dep=True)
                B.dve(lambda v, W=W: v.tensor_add(acc0[:, 0:W], acc0[:, 0:W],
                                                  gtmpw[:, 0:W]),
                      reads=[("gtmp",)], writes=[("ga",)], self_dep=True)
        B.dve(lambda v, gl=gall, W=W: v.tensor_copy(gl[:, :], acc0[:, 0:W]),
              reads=[("ga",)], writes=[("gall", half)], self_dep=True)

    for c in range(NCH):
        B.pe(lambda t, c=c: t.matmul(paux[0:17, 0:1], wcat_sb[:, 17 * c:17 * (c + 1)],
                                     ones_bf[:], start=(c == 0), stop=(c == NCH - 1)),
             reads=[("wcat_sb",), ("ones_bf",)], writes=[PAUX])
    B.dve(lambda v: v.tensor_scalar_mul(sneg[:], paux[0:17, 0:1], -1.0),
          reads=[PAUX], writes=[("sneg",)], self_dep=True)

    for blk in range(NBLK):
        for q4 in range(8):
            par = q4 % 2
            xb, xq, xr = xblk[par], xsq[par], xraw[par]
            B.dma("syn", xr[:, :],
                  xT[128 * q4:128 * (q4 + 1), 2048 * blk:2048 * (blk + 1)],
                  grp=f"xb{par}", writes=[("xraw", par)])
            B.act(lambda s, xb=xb, xr=xr: s.activation(xb[:, :], xr[:, :], AF.Copy),
                  reads=[("xraw", par)], writes=[("xblk", par)])
            B.act(lambda s, xq=xq, xr=xr: s.activation(xq[:, :], xr[:, :], AF.Square),
                  reads=[("xraw", par)], writes=[("xsq", par)])
            for i in range(4):
                c = 4 * q4 + i
                B.pe(lambda t, xb=xb, i=i, c=c:
                     t.matmul(paux[0:17, :], wcat_sb[:, 17 * c:17 * (c + 1)],
                              xb[:, 512 * i:512 * (i + 1)],
                              start=(c == 0), stop=(c == NCH - 1)),
                     reads=[("xblk", par), ("wcat_sb",)], writes=[PAUX])
                B.pe(lambda t, xq=xq, i=i, c=c:
                     t.matmul(pctx[0:1, :], ones_bf[:],
                              xq[:, 512 * i:512 * (i + 1)],
                              start=(c == 0), stop=(c == NCH - 1),
                              skip_group_check=True),
                     reads=[("xsq", par), ("ones_bf",)], writes=[PCTX])
                B.pe(lambda t, xb=xb, i=i, c=c:
                     t.matmul(pctx[32:33, :], ones_bf[:],
                              xb[:, 512 * i:512 * (i + 1)],
                              start=(c == 0), stop=(c == NCH - 1),
                              tile_position=(0, 32), skip_group_check=True),
                     reads=[("xblk", par), ("ones_bf",)], writes=[PCTX])
        B.dve(lambda v: v.tensor_copy(blk_f[0:17, :], paux[0:17, :]),
              reads=[PAUX], writes=[("blk_f",)])
        B.dve(lambda v: v.tensor_scalar_mul(slot(0), pctx[32:33, :], 1.0 / D),
              reads=[PCTX], writes=[("rp", 0)])
        B.dve(lambda v: v.tensor_scalar_mul(slot(1), pctx[0:1, :], 1.0 / D),
              reads=[PCTX], writes=[("rp", 1)])
        B.dve(lambda v: v.tensor_mul(slot(2), slot(0), slot(0)),
              reads=[("rp", 0)], writes=[("rp", 2)], self_dep=True)
        B.dve(lambda v: v.tensor_sub(slot(1), slot(1), slot(2)),
              reads=[("rp", 2), ("rp", 1)], writes=[("rp", 1)], self_dep=True)
        B.dve(lambda v: v.tensor_scalar_add(slot(1), slot(1), EPS),
              reads=[("rp", 1)], writes=[("rp", 1)], self_dep=True)
        B.act(lambda s: s.activation(slot(3), slot(1), AF.Ln),
              reads=[("rp", 1)], writes=[("rp", 3)])
        B.act(lambda s: s.activation(slot(3), slot(3), AF.Exp, scale=-0.5),
              reads=[("rp", 3)], writes=[("rp", 3)], self_dep=True)
        B.dve(lambda v: v.tensor_mul(slot(4), slot(0), slot(3)),
              reads=[("rp", 0), ("rp", 3)], writes=[("rp", 4)], self_dep=True)
        B.pe(lambda t: t.matmul(rbank[0][0:17, :], ones17[0:1, 0:17], slot(3),
                                start=True, stop=True),
             reads=[("rp", 3), ("ones17",)], writes=[RB[0]])
        B.pe(lambda t: t.matmul(rbank[1][0:17, :], ones17[0:1, 0:17], slot(4),
                                start=True, stop=True),
             reads=[("rp", 4), ("ones17",)], writes=[RB[1]])
        B.dve(lambda v: v.tensor_mul(blk_fix[:], blk_f[0:17, :], rbank[0][0:17, :]),
              reads=[("blk_f",), RB[0]], writes=[("blk_fix",)])
        B.dve(lambda v: v.scalar_tensor_tensor(blk_fix[:], rbank[1][0:17, :], sneg[:],
                                               blk_fix[:], ALU.mult, ALU.add),
              reads=[RB[1], ("sneg",), ("blk_fix",)], writes=[("blk_fix",)],
              self_dep=True)
        B.dve(lambda v: v.tensor_scalar_add(blk_fix[:], blk_fix[:], b17_sb[:]),
              reads=[("b17_sb",), ("blk_fix",)], writes=[("blk_fix",)])
        B.dma("pool", ckvlat[0:4, 512 * blk:512 * (blk + 1)], blk_fix[12:16, :],
              grp="ckv", reads=[("blk_fix",)], writes=[("ckvlat", blk)])
        for i in range(4):
            kt = 4 * blk + i
            B.pe(lambda t, i=i: t.transpose(rbank[2][:, 0:16],
                                            blk_fix[0:16, 128 * i:128 * (i + 1)],
                                            ident_sb[:]),
                 reads=[("blk_fix",), ("ident_sb",)], writes=[RB[2]])
            B.dve(lambda v, kt=kt: v.tensor_copy(ckv1[:, 8 * kt:8 * kt + 4],
                                                 rbank[2][:, 12:16]),
                  reads=[RB[2]], writes=[("ckv1", kt)])
        if DEBUG:
            B.dma("syn", cqkv_dbg[:, 512 * blk:512 * (blk + 1)], blk_fix[:],
                  grp="dbg", reads=[("blk_fix",)], writes=[("cqkv_dbg",)])
        if blk < 2:
            B.dve(lambda v, blk=blk:
                  v.tensor_copy(cqb[0:12, 512 * blk:512 * (blk + 1)], blk_fix[0:12, :]),
                  reads=[("blk_fix",)], writes=[("cqb", blk)])
            for g in range(4):
                for bf_ in range(2):
                    B.dma("pool", cqm2[bf_][32 * g:32 * g + 12,
                                            512 * blk:512 * (blk + 1)],
                          blk_fix[0:12, :],
                          grp="cqc", reads=[("blk_fix",)], writes=[("cqm", bf_)])

    for h2 in range(2):
        B.pe(lambda t, h2=h2: t.matmul(rbank[0][:, :], wqka_sb[:],
                                       cqb[0:13, 512 * h2:512 * (h2 + 1)],
                                       start=True, stop=True),
             reads=[("cqb", 0), ("cqb", 1), ("wqka_sb",)], writes=[RB[0]])
        B.dve(lambda v, h2=h2: v.tensor_copy(qsb[:, 512 * h2:512 * (h2 + 1)],
                                             rbank[0][:, :]),
              reads=[RB[0]], writes=[("qsb", h2)])
        if DEBUG:
            B.dma("pool", q_dbg[:, 512 * h2:512 * (h2 + 1)],
                  qsb[:, 512 * h2:512 * (h2 + 1)],
                  grp="dbg", reads=[("qsb", h2)], writes=[("q_dbg",)])
        B.dve(lambda v, h2=h2: v.tensor_mul(sq_c[:],
                                            qsb[:, 512 * h2:512 * (h2 + 1)],
                                            qsb[:, 512 * h2:512 * (h2 + 1)]),
              reads=[("qsb", h2)], writes=[("sq_c",)])
        B.pe(lambda t, h2=h2: t.matmul(rbank[1][0:32, :], e4_sb[:], sq_c[:],
                                       start=True, stop=True),
             reads=[("sq_c",), ("e4_sb",)], writes=[RB[1]])
        B.dve(lambda v, h2=h2: v.tensor_scalar(c_sb[0:32, 512 * h2:512 * (h2 + 1)],
                                               rbank[1][0:32, :], 0.5, CSHIFT,
                                               ALU.mult, ALU.add),
              reads=[RB[1]], writes=[("c_sb", h2)])
    B.dma("syn", c_stage[:], c_sb[0:32, :], grp="cst",
          reads=[("c_sb", 0), ("c_sb", 1)], writes=[("c_stage",)])


    if TRUNC < 2:
        return finish(nc, B, stack)
    for seg in range(8):
        for cq in range(8):
            par = cq % 2
            B.dma("pool", wos2[par][:, :],
                  wo_d[128 * cq:128 * (cq + 1), 2048 * seg:2048 * (seg + 1)],
                  grp=f"wos{par}", writes=[("wos", par)])
            B.dma("syn", ms2[par][:, :],
                  mmat[128 * cq:128 * (cq + 1), 2048 * seg:2048 * (seg + 1)],
                  grp=f"ms{par}", writes=[("ms", par)])
            for i in range(4):
                c = 4 * cq + i
                B.pe(lambda t, c=c, par=par, i=i: t.matmul(
                    rbank[3][:, :], wvall[:, 128 * c:128 * (c + 1)],
                    wos2[par][:, 512 * i:512 * (i + 1)],
                    start=(c == 0), stop=(c == NCH - 1)),
                    reads=[("wos", par), ("wstr", 0)], writes=[RB[3]])
                B.pe(lambda t, c=c, par=par, i=i: t.matmul(
                    paux[0:1, :], bvu_sb[:, c:c + 1],
                    wos2[par][:, 512 * i:512 * (i + 1)],
                    start=(c == 0), stop=(c == NCH - 1), skip_group_check=True),
                    reads=[("wos", par), ("bvu_sb",)], writes=[PAUX])
            for i in range(4):
                c = 4 * cq + i
                B.pe(lambda t, c=c, par=par, i=i: t.matmul(
                    pctx[0:32, :], g0all[:, 32 * c:32 * (c + 1)],
                    ms2[par][:, 512 * i:512 * (i + 1)],
                    start=(c == 0), stop=(c == NCH - 1), skip_group_check=True),
                    reads=[("ms", par), ("gall", 0)], writes=[PCTX])
                B.pe(lambda t, c=c, par=par, i=i: t.matmul(
                    pctx[32:40, :], g1all[:, 8 * c:8 * (c + 1)],
                    ms2[par][:, 512 * i:512 * (i + 1)],
                    start=(c == 0), stop=(c == NCH - 1), tile_position=(0, 32),
                    skip_group_check=True),
                    reads=[("ms", par), ("gall", 1)], writes=[PCTX])
        B.dve(lambda v, og=seg: v.tensor_copy(a_bf[:, 512 * og:512 * (og + 1)],
                                              rbank[3][:, :]),
              reads=[RB[3]], writes=[("a_bf", seg)])
        B.dve(lambda v, og=seg: v.tensor_copy(bcrow[0:1, 512 * og:512 * (og + 1)],
                                              paux[0:1, :]),
              reads=[PAUX], writes=[("bcrow",)])
        B.dve(lambda v, jg=seg: v.tensor_scalar_mul(hcat[0:32, 512 * jg:512 * (jg + 1)],
                                                    pctx[0:32, :], 0.5),
              reads=[PCTX], writes=[("hcat",)])
        B.dve(lambda v, jg=seg: v.tensor_scalar_mul(hcat[32:40, 512 * jg:512 * (jg + 1)],
                                                    pctx[32:40, :], 0.5),
              reads=[PCTX], writes=[("hcat",)])
    for k in range(32):
        B.pe(lambda t, k=k: t.transpose(paux[:, k:k + 1],
                                        bcrow[0:1, 128 * k:128 * (k + 1)],
                                        ones_f32[0:1, 0:1]),
             reads=[("bcrow",), ("ones_f32",)], writes=[PAUX])
    B.dve(lambda v: v.tensor_copy(bc_sb[:, :], paux[:, 0:32]),
          reads=[PAUX], writes=[("bc_sb",)])
    B.dve(lambda v: v.tensor_add(bc_sb[:], bc_sb[:], bo_sb[:]),
          reads=[("bo_sb",), ("bc_sb",)], writes=[("bc_sb",)], self_dep=True)

    if DEBUG:
        B.dma("pool", a_dbg[:], a_bf[:], grp="dbg", reads=[("a_bf", og) for og in range(8)],
              writes=[("a_dbg",)])
        B.dma("syn", bc_dbg[:], bc_sb[:], grp="dbg", reads=[("bc_sb",)], writes=[("bc_dbg",)])

    if DEBUG:
        B.dma("pool", h_dbg[:], hcat[0:40, :], grp="dbg", reads=[("hcat",)], writes=[("h_dbg",)])
        B.dma("pool", g_dbg[:, 0:NCH * 32], g0all[:], grp="dbg", reads=[("gall", 0)],
              writes=[("g_dbg",)])
        B.dma("pool", g_dbg[:, NCH * 32:NCH * 32 + NCH * 8], g1all[:],
              grp="dbg", reads=[("gall", 1)], writes=[("g_dbg",)])
    if TRUNC < 3:
        return finish(nc, B, stack)
    B.dve(lambda v: v.memset(paux[:, :], 0.0), writes=[PAUX])
    B.dve(lambda v: v.memset(pctx[:, :], 0.0), writes=[PCTX])
    for s_ in range(NSLOT):
        cb = s_ % 2
        for g in range(4):
            B.dma("pool", cqm2[cb][32 * g + 13:32 * g + 14, :],
                  c_stage[4 * s_ + g:4 * s_ + g + 1, :],
                  grp=f"cq{cb}", reads=[("c_stage",)], writes=[("cqm", cb)])
        for ktg in range(8):
            for g in range(4):
                B.pe(lambda t, g=g, ktg=ktg, s_=s_: t.matmul(
                    paux[32 * g:32 * g + 16, :],
                    wqk_sb[0:5, 16 * (4 * s_ + g):16 * (4 * s_ + g) + 16],
                    ckvlat[0:5, 512 * ktg:512 * (ktg + 1)],
                    start=True, stop=True, tile_position=(0, 32 * g)),
                    reads=[("wqk_sb",), ("ckvlat", ktg)],
                    writes=[PAUX])
            B.dve(lambda v, ktg=ktg: v.tensor_scalar_mul(
                kh2[:, 512 * ktg:512 * (ktg + 1)], paux[:, :], 0.5),
                reads=[PAUX], writes=[("kh2", ktg)])
        for q in range(NQ):
            prev = None
            for kt in range(NKT):
                pa = (2 * kt) % 3
                pb = (2 * kt + 1) % 3
                slot3 = kt % 3
                for g in range(4):
                    pair = pa if g < 2 else pb
                    bank = 2 * pair + (g & 1)
                    B.pe(lambda t, g=g, kt=kt, cb=cb, q=q, bank=bank: t.matmul(
                        rbank[bank][:, :],
                        kh2[32 * g:32 * g + 16, 128 * kt:128 * (kt + 1)],
                        cqm2[cb][32 * g:32 * g + 16, 512 * q:512 * (q + 1)],
                        start=True, stop=True, tile_position=(32 * g, 0)),
                        reads=[("kh2", kt // 4), ("cqm", cb)], writes=[RB[bank]])
                rv = ring.rearrange("p (c n) -> p c n", n=1024)
                dst = expsc[:, 2048 * slot3:2048 * (slot3 + 1)]
                if pb == pa + 1:
                    src = ring[:, 1024 * pa:1024 * (pb + 1)]
                    dst2 = dst
                else:
                    src = rv[:, ::-2, :]
                    dst2 = dst.rearrange("p (c n) -> p c n", n=1024)
                B.act(lambda s, src=src, dst2=dst2: s.activation(dst2, src, AF.Exp),
                      reads=[RB[2 * pa], RB[2 * pa + 1], RB[2 * pb], RB[2 * pb + 1]],
                      writes=[("expsc", slot3)])
                if prev is not None:
                    pkt, pslot3 = prev
                    for g in range(4):
                        pair_off = 0 if g < 2 else 1024
                        off = 2048 * pslot3 + pair_off + 512 * (g & 1)
                        B.pe(lambda t, g=g, pkt=pkt, off=off: t.matmul(
                            pctx[32 * g:32 * g + 5, :],
                            ckv1[:, 8 * pkt:8 * pkt + 5],
                            expsc[:, off:off + 512],
                            start=(pkt == 0), stop=(pkt == NKT - 1),
                            tile_position=(0, 32 * g), skip_group_check=True),
                            reads=[("expsc", pslot3), ("ckv1", pkt)],
                            writes=[PCTX])
                prev = (kt, slot3)
            pkt, pslot3 = prev
            for g in range(4):
                pair_off = 0 if g < 2 else 1024
                off = 2048 * pslot3 + pair_off + 512 * (g & 1)
                B.pe(lambda t, g=g, pkt=pkt, off=off: t.matmul(
                    pctx[32 * g:32 * g + 5, :],
                    ckv1[:, 8 * pkt:8 * pkt + 5],
                    expsc[:, off:off + 512],
                    start=(pkt == 0), stop=(pkt == NKT - 1),
                    tile_position=(0, 32 * g), skip_group_check=True),
                    reads=[("expsc", pslot3), ("ckv1", pkt)], writes=[PCTX])
            B.dve(lambda v: v.tensor_copy(ctx_sp[:], pctx[:]),
                  reads=[PCTX], writes=[("ctx_sp",)])
            for g in range(4):
                B.dma("syn", ctx_stage[(q * NSLOT + s_) * 20 + 5 * g:
                                       (q * NSLOT + s_) * 20 + 5 * (g + 1), :],
                      ctx_sp[32 * g:32 * g + 5, :],
                      grp="ctw", reads=[("ctx_sp",)], writes=[("ctx_stage", q, s_)])

    def stat_finish(pbank, pkey, ri, mi):
        B.dve(lambda v: v.tensor_scalar_mul(slot(0), pbank[0:1, :], 1.0 / D),
              reads=[pkey], writes=[("rp", 0)])
        B.dve(lambda v: v.tensor_scalar_mul(slot(1), pbank[32:33, :], 1.0 / D),
              reads=[pkey], writes=[("rp", 1)])
        B.dve(lambda v: v.tensor_mul(slot(2), slot(0), slot(0)),
              reads=[("rp", 0)], writes=[("rp", 2)], self_dep=True)
        B.dve(lambda v: v.tensor_sub(slot(1), slot(1), slot(2)),
              reads=[("rp", 2), ("rp", 1)], writes=[("rp", 1)], self_dep=True)
        B.dve(lambda v: v.tensor_scalar_add(slot(1), slot(1), EPS),
              reads=[("rp", 1)], writes=[("rp", 1)], self_dep=True)
        B.act(lambda s: s.activation(slot(ri), slot(1), AF.Ln),
              reads=[("rp", 1)], writes=[("rp", ri)])
        B.act(lambda s: s.activation(slot(ri), slot(ri), AF.Exp, scale=-0.5),
              reads=[("rp", ri)], writes=[("rp", ri)], self_dep=True)
        B.dve(lambda v: v.tensor_mul(slot(mi), slot(0), slot(ri)),
              reads=[("rp", 0), ("rp", ri)], writes=[("rp", mi)], self_dep=True)

    if TRUNC < 4:
        return finish(nc, B, stack)
    B.dve(lambda v: v.engine_nop(),
          writes=[("ckvlat", b) for b in range(NBLK)] +
                 [("c_sb", 0), ("c_sb", 1), ("ckvlat",)])
    for q in range(NQ):
        for s_ in range(NSLOT):
            for g in range(4):
                base = (q * NSLOT + s_) * 20 + 5 * g
                B.dma("syn", ctx_u[16 * s_ + 4 * g:16 * s_ + 4 * (g + 1), :],
                      ctx_stage[base:base + 4, :],
                      grp="cxr", reads=[("ctx_stage", q, s_)], writes=[("ctx_u", q)])
                B.dma("syn", dnm[4 * s_ + g:4 * s_ + g + 1, :],
                      ctx_stage[base + 4:base + 5, :],
                      grp="cxr", reads=[("ctx_stage", q, s_)], writes=[("dnm", q)])
        B.dve(lambda v: v.reciprocal(dtmp[:], dnm[:]),
              reads=[("dnm", q)], writes=[("dtmp",)])
        B.dve(lambda v: v.tensor_mul(dtmp2[:], dnm[:], dtmp[:]),
              reads=[("dnm", q), ("dtmp",)], writes=[("dtmp2",)], self_dep=True)
        B.dve(lambda v: v.tensor_scalar(dtmp2[:], dtmp2[:], -1.0, 2.0, ALU.mult, ALU.add),
              reads=[("dtmp2",)], writes=[("dtmp2",)], self_dep=True)
        B.dve(lambda v: v.tensor_mul(dtmp[:], dtmp[:], dtmp2[:]),
              reads=[("dtmp",), ("dtmp2",)], writes=[("dtmp",)], self_dep=True)
        B.pe(lambda t: t.matmul(rbank[2][:, :], e4t_sb[:], dtmp[:],
                                start=True, stop=True),
             reads=[("dtmp",), ("e4t_sb",)], writes=[RB[2]])
        B.dve(lambda v: v.tensor_mul(ctx_n[:], ctx_u[:], rbank[2][:, :]),
              reads=[("ctx_u", q), RB[2]], writes=[("ctx_n", q)])
        if DEBUG:
            B.dma("pool", ctxn_dbg[:, 512 * q:512 * (q + 1)], ctx_n[:],
                  grp="dbg", reads=[("ctx_n", q)], writes=[("ctxn_dbg",)])
        atb, sqb = [attn_c, attn_c2], [sq_c, sq_c2]

        def attn_stats(ot, q):
            po = ot % 2
            B.pe(lambda t, ot=ot, po=po: t.matmul(
                rbank[2][0:1, :], ones_bf[:], atb[po][:, :],
                start=(ot == 0), stop=(ot == NCH - 1), skip_group_check=True),
                reads=[("attn_c", po), ("ones_bf",)], writes=[RB[2]])
            B.pe(lambda t, ot=ot, po=po: t.matmul(
                rbank[2][32:33, :], ones_bf[:], sqb[po][:, :],
                start=(ot == 0), stop=(ot == NCH - 1), tile_position=(0, 32),
                skip_group_check=True),
                reads=[("sq_a", po), ("ones_bf",)], writes=[RB[2]])
            B.dma("syn", attn_sp[128 * ot:128 * (ot + 1), 512 * q:512 * (q + 1)],
                  atb[po][:, :], grp=f"aw{po}", reads=[("attn_c", po)],
                  writes=[("attn_spl", ot, q)])
            if DEBUG:
                B.dma("pool", attn_dbg[128 * ot:128 * (ot + 1), 512 * q:512 * (q + 1)],
                      atb[po][:, :], grp="dbg", reads=[("attn_c", po)],
                      writes=[("attn_dbg",)])

        pend = None
        for ot in range(NCH):
            pbk = ot % 2
            B.pe(lambda t, ot=ot, pbk=pbk: t.matmul(
                rbank[pbk][:, :], a_bf[:, 128 * ot:128 * (ot + 1)], ctx_n[:],
                start=True, stop=True),
                reads=[("ctx_n", q), ("a_bf", ot // 4)], writes=[RB[pbk]])
            B.dve(lambda v, ot=ot, pbk=pbk: v.tensor_scalar(
                atb[pbk][:, :], rbank[pbk][:, :],
                bc_sb[:, ot:ot + 1], None, ALU.add),
                reads=[RB[pbk], ("bc_sb",)], writes=[("attn_c", pbk)])
            B.dve(lambda v, pbk=pbk: v.tensor_mul(sqb[pbk][:, :], atb[pbk][:, :],
                                                  atb[pbk][:, :]),
                  reads=[("attn_c", pbk)], writes=[("sq_a", pbk)])
            if pend is not None:
                attn_stats(pend, q)
            pend = ot
        attn_stats(pend, q)
        stat_finish(rbank[2], RB[2], 3, 4)
        B.pe(lambda t: t.matmul(rbank[3][:, :], onesr_f32[:], slot(3),
                                start=True, stop=True),
             reads=[("rp", 3), ("onesr_f32",)], writes=[RB[3]])
        B.pe(lambda t: t.matmul(rbank[4][:, :], onesr_f32[:], slot(4),
                                start=True, stop=True),
             reads=[("rp", 4), ("onesr_f32",)], writes=[RB[4]])
        B.dve(lambda v: v.tensor_copy(rb_sb[:], rbank[3][:, :]),
              reads=[RB[3]], writes=[("rb_sb",)])
        B.dve(lambda v: v.tensor_copy(mb_sb[:], rbank[4][:, :]),
              reads=[RB[4]], writes=[("mb_sb",)])
        arb, x1b = [attn_r, attn_rb], [x1f_c, x1f_c2]
        for c in range(NCH):
            par = c % 2
            B.dma("syn", xf_c[par][:, :],
                  xT[128 * (c // 4):128 * (c // 4) + 128,
                     2048 * q + 512 * (c % 4):2048 * q + 512 * (c % 4) + 512],
                  grp=f"xf{par}", writes=[("xf_c", par)])
            B.dma("act", arb[par][:, :],
                  attn_sp[128 * c:128 * (c + 1), 512 * q:512 * (q + 1)],
                  grp=f"ar{par}", reads=[("attn_spl", c, q)], writes=[("attn_r", par)])
            B.dve(lambda v, par=par: v.tensor_mul(tt1[:], arb[par][:, :], rb_sb[:]),
                  reads=[("attn_r", par), ("rb_sb",)], writes=[("tt1",)])
            B.dve(lambda v: v.tensor_sub(tt2[:], tt1[:], mb_sb[:]),
                  reads=[("tt1",), ("mb_sb",)], writes=[("tt2",)])
            B.dve(lambda v, par=par: v.tensor_add(x1b[par][:, :], tt2[:],
                                                  xf_c[par][:, :]),
                  reads=[("tt2",), ("xf_c", par)], writes=[("x1f_c", par)])
            B.dve(lambda v, c=c, q=q, par=par: v.tensor_copy(
                x1_bf[:, TOK * c + 512 * q:TOK * c + 512 * (q + 1)],
                x1b[par][:, :]),
                reads=[("x1f_c", par)], writes=[("x1_bf", c, q)])
            B.dma("pool", x1_spill[128 * c:128 * (c + 1), 512 * q:512 * (q + 1)],
                  x1b[par][:, :], grp=f"x1w{par}", reads=[("x1f_c", par)],
                  writes=[("x1_spill", c, q)])

    if TRUNC < 5:
        return finish(nc, B, stack)
    B.dve(lambda v: v.engine_nop(),
          writes=[("xblk", 0, qq) for qq in range(8)] +
                 [("xblk", 1, qq) for qq in range(8)] +
                 [("cqb", 0), ("cqb", 1), ("qsb", 0), ("qsb", 1),
                  ("c_sb", 0), ("c_sb", 1)])
    for c in range(NCH):
        B.dma("pool", enc_sb[:, 40 * c:40 * (c + 1)], encT_d[128 * c:128 * (c + 1), :],
              grp="enc", writes=[("enc_sb",), ("cqb", 0), ("cqb", 1),
                                 ("qsb", 0), ("qsb", 1)])
    sqln = [sq_c, sq_c2]
    for q in range(NQ):
        for c in range(NCH):
            par = c % 2
            B.act(lambda s, c=c, q=q, par=par: s.activation(
                sqln[par][:, :], x1_bf[:, TOK * c + 512 * q:TOK * c + 512 * (q + 1)],
                AF.Square),
                reads=[("x1_bf", c, q)], writes=[("sq_a", par)])
            B.pe(lambda t, c=c, q=q: t.matmul(
                rbank[2][0:1, :], ones_bf[:],
                x1_bf[:, TOK * c + 512 * q:TOK * c + 512 * (q + 1)],
                start=(c == 0), stop=(c == NCH - 1), skip_group_check=True),
                reads=[("x1_bf", c, q), ("ones_bf",)], writes=[RB[2]])
            B.pe(lambda t, c=c, par=par: t.matmul(
                rbank[2][32:33, :], ones_bf[:], sqln[par][:, :],
                start=(c == 0), stop=(c == NCH - 1), tile_position=(0, 32),
                skip_group_check=True),
                reads=[("sq_a", par), ("ones_bf",)], writes=[RB[2]])
        stat_finish(rbank[2], RB[2], 3, 4)
        B.pe(lambda t: t.matmul(rbank[3][:, :], onesr_f32[:], slot(4),
                                start=True, stop=True),
             reads=[("rp", 4), ("onesr_f32",)], writes=[RB[3]])
        B.dve(lambda v, q=q: v.tensor_copy(mb3[:, 512 * q:512 * (q + 1)],
                                           rbank[3][:, :]),
              reads=[RB[3]], writes=[("mb3", q)])
        B.pe(lambda t: t.matmul(rbank[3][:, :], onesr_f32[:], slot(3),
                                start=True, stop=True),
             reads=[("rp", 3), ("onesr_f32",)], writes=[RB[3]])
        B.dve(lambda v, q=q: v.tensor_scalar_mul(rb3[:, 512 * q:512 * (q + 1)],
                                                 rbank[3][:, :], -1.0),
              reads=[RB[3]], writes=[("rb3", q)])

    fcb, sqf = [ffn_c, attn_c2], [sq_c, sq_c2]

    def ffn_stats(ot, q):
        sbk, skey = (rbank[2], RB[2]) if q == 0 else (pctx, PCTX)
        B.pe(lambda t, ot=ot, sbk=sbk, q=q: t.matmul(
            sbk[0:1, :], ones_bf[:], fcb[q][:, :],
            start=(ot == 1), stop=(ot == NCH), skip_group_check=True),
            reads=[("ffn_cq", q), ("ones_bf",)], writes=[skey])
        B.pe(lambda t, ot=ot, sbk=sbk, q=q: t.matmul(
            sbk[32:33, :], ones_bf[:], sqf[q][:, :],
            start=(ot == 1), stop=(ot == NCH), tile_position=(0, 32),
            skip_group_check=True),
            reads=[("sq_f", q), ("ones_bf",)], writes=[skey])
        B.dma("syn", ffn_spill[128 * (ot - 1):128 * ot, 512 * q:512 * (q + 1)],
              fcb[q][:, :], grp=f"fw{q}", reads=[("ffn_cq", q)],
              writes=[("ffn_sp", ot, q)])

    pend = None
    for ot in range(NCH + 1):
        par = ot % 2
        if ot > 0:
            B.dma("pool", wstr[par][:, :], wbT_d[128 * (ot - 1):128 * ot, :],
                  grp=f"wb{par}", writes=[("wstr", par)])
        for q in range(NQ):
            ww = 40 if ot == 0 else 128
            wkey = ("enc_sb",) if ot == 0 else ("wstr", par)
            for c in range(NCH):
                if ot == 0:
                    lt = enc_sb[:, 40 * c:40 * (c + 1)]
                else:
                    lt = wstr[par].rearrange("p (cc o) -> p cc o", o=128)[:, c, :]
                if q == 0:
                    B.pe(lambda t, c=c, lt=lt, ww=ww: t.matmul(
                        paux[0:ww, 0:1], lt, ones_bf[:],
                        start=(c == 0), stop=(c == NCH - 1)),
                        reads=[wkey, ("ones_bf",)], writes=[PAUX])
                B.pe(lambda t, c=c, q=q, lt=lt, ww=ww, par=par: t.matmul(
                    rbank[par][0:ww, :], lt,
                    x1_bf[:, TOK * c + 512 * q:TOK * c + 512 * (q + 1)],
                    start=(c == 0), stop=(c == NCH - 1)),
                    reads=[wkey, ("x1_bf", c, q)], writes=[RB[par]])
            if q == 0:
                B.dve(lambda v, ww=ww: v.tensor_copy(senc_sb[0:ww, :], paux[0:ww, 0:1]),
                      reads=[PAUX], writes=[("senc_sb",)], self_dep=True)
            if ot == 0:
                B.dve(lambda v, q=q, par=par: v.scalar_tensor_tensor(
                    tt1[0:40, :], mb3[0:40, 512 * q:512 * (q + 1)], senc_sb[0:40, :],
                    rbank[par][0:40, :], ALU.mult, ALU.subtract),
                    reads=[("mb3", q), ("senc_sb",), RB[par]], writes=[("tt1",)],
                    self_dep=True)
                B.dve(lambda v, q=q: v.tensor_mul(tt2[0:40, :], tt1[0:40, :],
                                                  rb3[0:40, 512 * q:512 * (q + 1)]),
                      reads=[("tt1",), ("rb3", q)], writes=[("tt2",)])
                B.dve(lambda v, q=q: v.tensor_copy(u_sb[0:40, 512 * q:512 * (q + 1)],
                                                   tt2[0:40, :]),
                      reads=[("tt2",)], writes=[("u_sb", q)])
                if DEBUG:
                    B.dma("pool", u_dbg[:, 512 * q:512 * (q + 1)],
                          u_sb[0:40, 512 * q:512 * (q + 1)],
                          grp="dbg", reads=[("u_sb", q)], writes=[("u_dbg",)])
            else:
                ub = 4 + (1 - par)
                B.pe(lambda t, ot=ot, q=q, ub=ub: t.matmul(
                    rbank[ub][:, :], hcat[0:40, 128 * (ot - 1):128 * ot],
                    u_sb[0:40, 512 * q:512 * (q + 1)], start=True, stop=True),
                    reads=[("u_sb", q), ("hcat",)], writes=[RB[ub]])
                B.dve(lambda v, q=q, par=par: v.scalar_tensor_tensor(
                    tt1[:], mb3[:, 512 * q:512 * (q + 1)], senc_sb[:],
                    rbank[par][:, :], ALU.mult, ALU.subtract),
                    reads=[("mb3", q), ("senc_sb",), RB[par]], writes=[("tt1",)],
                    self_dep=True)
                B.dve(lambda v, q=q: v.tensor_mul(tt2[:], tt1[:],
                                                  rb3[:, 512 * q:512 * (q + 1)]),
                      reads=[("tt1",), ("rb3", q)], writes=[("tt2",)])
                B.dve(lambda v, ot=ot, ub=ub, q=q: v.scalar_tensor_tensor(
                    fcb[q][:, :], tt2[:], bb_sb[:, ot - 1:ot], rbank[ub][:, :],
                    ALU.add, ALU.add),
                    reads=[("tt2",), ("bb_sb",), RB[ub]], writes=[("ffn_cq", q)])
                B.act(lambda s, q=q: s.activation(sqf[q][:, :], fcb[q][:, :],
                                                  AF.Square),
                      reads=[("ffn_cq", q)], writes=[("sq_f", q)])
                if pend is not None:
                    ffn_stats(*pend)
                pend = (ot, q)
    ffn_stats(*pend)

    B.dve(lambda v: v.engine_nop(),
          writes=[("a_bf", og) for og in range(8)])
    st4slots = [(5, 6), (7, 8)]
    for q in range(NQ):
        sbk, skey = (rbank[2], RB[2]) if q == 0 else (pctx, PCTX)
        stat_finish(sbk, skey, st4slots[q][0], st4slots[q][1])

    for q in range(NQ):
        ri, mi = st4slots[q]
        B.pe(lambda t, ri=ri: t.matmul(rbank[3][:, :], onesr_f32[:],
                                       slot(ri), start=True, stop=True),
             reads=[("rp", ri), ("onesr_f32",)], writes=[RB[3]])
        B.dve(lambda v: v.tensor_copy(rb4[:], rbank[3][:, :]),
              reads=[RB[3]], writes=[("rb4", q)])
        B.pe(lambda t, mi=mi: t.matmul(rbank[3][:, :], onesr_f32[:],
                                       slot(mi), start=True, stop=True),
             reads=[("rp", mi), ("onesr_f32",)], writes=[RB[3]])
        B.dve(lambda v: v.tensor_copy(mb4[:], rbank[3][:, :]),
              reads=[RB[3]], writes=[("mb4", q)])
        frb, ocb = [ffn_r, ffn_r2], [out_c, out_c2]
        for c in range(NCH):
            par = c % 2
            B.dma("syn", xf_c[par][:, :],
                  x1_spill[128 * c:128 * (c + 1), 512 * q:512 * (q + 1)],
                  grp=f"xf{par}", reads=[("x1_spill", c, q)], writes=[("xf_c", par)])
            B.dma("act", frb[par][:, :],
                  ffn_spill[128 * c:128 * (c + 1), 512 * q:512 * (q + 1)],
                  grp=f"fr{par}", reads=[("ffn_sp", c + 1, q)], writes=[("ffn_r", par)])
            B.dve(lambda v, par=par: v.tensor_mul(tt1[:], frb[par][:, :], rb4[:]),
                  reads=[("ffn_r", par), ("rb4", q)], writes=[("tt1",)])
            B.dve(lambda v: v.tensor_sub(tt2[:], tt1[:], mb4[:]),
                  reads=[("tt1",), ("mb4", q)], writes=[("tt2",)])
            B.dve(lambda v, par=par: v.tensor_add(ocb[par][:, :], tt2[:],
                                                  xf_c[par][:, :]),
                  reads=[("tt2",), ("xf_c", par)], writes=[("out_c", par)])
            B.dma("pool", out[128 * c:128 * (c + 1), 512 * q:512 * (q + 1)],
                  ocb[par][:, :], grp=f"ow{par}", reads=[("out_c", par)],
                  writes=[("out", c, q)])

    return finish(nc, B, stack)


def finish(nc, B, stack):
    B.finalize()
    sems = {s: stack.enter_context(nc.semaphore(s)) for s in Builder.STREAMS}
    for g in B.grps:
        sems[g] = stack.enter_context(nc.semaphore(g))
    grp_totals = {g: 16 * len(ops) for g, ops in B.grps.items()}
    with nc.Block() as block:
        @block.tensor
        def _(t):
            B.emit_stream("pe", t, sems)

        @block.scalar
        def _(s):
            B.emit_stream("act", s, sems)

        @block.vector
        def _(v):
            B.emit_stream("dve", v, sems)

        @block.gpsimd
        def _(g):
            B.emit_stream("pool", g, sems)

        @block.sync
        def _(sy):
            B.emit_stream("syn", sy, sems)
            for g, tot in grp_totals.items():
                sy.wait_ge(sems[g], tot)

    stack.close()
    return nc


def host_constants(inputs):
    wq = np.asarray(inputs["L0_attn_Wq_d"], np.float32)
    bq = np.asarray(inputs["L0_attn_bq_d"], np.float32)
    wqk = np.asarray(inputs["L0_attn_W_qk"], np.float32)
    bqk = np.asarray(inputs["L0_attn_b_qk"], np.float32)
    wkv = np.asarray(inputs["L0_attn_Wkv_d"], np.float32)
    bkv = np.asarray(inputs["L0_attn_bkv_d"], np.float32)
    wvu = np.asarray(inputs["L0_attn_Wv_u"], np.float32)
    bvu = np.asarray(inputs["L0_attn_bv_u"], np.float32)
    wo = np.ascontiguousarray(np.asarray(inputs["L0_attn_Wo"], np.float32))
    bo = np.asarray(inputs["L0_attn_bo"], np.float32)
    wb = np.asarray(inputs["L0_ffn_W_base"], np.float32)
    bb = np.asarray(inputs["L0_ffn_b_base"], np.float32)
    enc0 = np.asarray(inputs["L0_ffn_enc0"], np.float32)
    enc1 = np.asarray(inputs["L0_ffn_enc1"], np.float32)

    d = {}
    d["wcat"] = np.ascontiguousarray(
        np.concatenate([wq, wkv, np.ones((D, 1), np.float32)], axis=1))
    d["b17_d"] = np.concatenate([bq, bkv, np.zeros(1, np.float32)])[:, None].copy()
    wqe = np.zeros((5, 16 * 32), np.float32)
    for h in range(32):
        for l in range(4):
            wqe[l, 16 * h:16 * h + 12] = wqk[:, 4 * h + l]
            wqe[l, 16 * h + 12] = bqk[4 * h + l]
        wqe[4, 16 * h + 13] = -2.0
    d["wqk_ext"] = wqe
    d["wqk_aug"] = np.ascontiguousarray(np.concatenate([wqk, bqk[None, :]], axis=0))
    e4 = np.zeros((128, 32), np.float32)
    e4t = np.zeros((32, 128), np.float32)
    for h in range(32):
        for l in range(4):
            e4[4 * h + l, h] = 1.0
            e4t[h, 4 * h + l] = 1.0
    d["e4_d"] = e4
    d["e4t_d"] = e4t
    d["ones_dr"] = np.ones((1, NSLOT * TOK), np.float32)
    d["ones_col"] = np.ones((128, 1), np.float32)
    d["ident_d"] = np.eye(16, dtype=np.float32)
    d["wvuT"] = np.ascontiguousarray(wvu.T)
    d["bvu_d"] = np.ascontiguousarray(bvu.reshape(32, 128).T)
    d["wo_d"] = np.ascontiguousarray(
        wo.reshape(8, 4, 128, 8, 512).transpose(0, 2, 3, 1, 4).reshape(1024, 16384))
    d["bo_d"] = np.ascontiguousarray(bo.reshape(32, 128).T)
    d["wbT_d"] = np.ascontiguousarray(
        wb.reshape(32, 128, 32, 128).transpose(0, 3, 2, 1).reshape(4096, 4096))
    d["encT_d"] = np.ascontiguousarray(np.concatenate([enc0.T, enc1.T], axis=1))
    d["bb_d"] = np.ascontiguousarray(bb.reshape(32, 128).T)
    d["co0"] = np.ascontiguousarray(np.asarray(inputs["L0_ffn_coefs0"], np.float32))
    d["co1"] = np.ascontiguousarray(np.asarray(inputs["L0_ffn_coefs1"], np.float32))
    d["mk0"] = np.ascontiguousarray(np.asarray(inputs["L0_ffn_mask0"]).astype(np.uint8))
    d["mk1"] = np.ascontiguousarray(np.asarray(inputs["L0_ffn_mask1"]).astype(np.uint8))

    n = D
    j = np.arange(n, dtype=np.float64)
    M = np.empty((n, n), np.float32)
    M[0] = np.float32(1.0 / math.sqrt(n))
    f = np.arange(1, n // 2, dtype=np.float64)
    ang = (2.0 * np.pi / n) * np.outer(f, j)
    M[1:n - 1:2] = (math.sqrt(2.0 / n) * np.cos(ang)).astype(np.float32)
    M[2:n - 1:2] = (math.sqrt(2.0 / n) * np.sin(ang)).astype(np.float32)
    M[n - 1] = (((-1.0) ** j) / math.sqrt(n)).astype(np.float32)
    d["mmat"] = np.ascontiguousarray(
        M.astype(ml_dtypes.bfloat16)
        .reshape(8, 4, 128, 8, 512).transpose(0, 2, 3, 1, 4).reshape(1024, 16384))
    return d


def kernel(**inputs):
    x = np.asarray(inputs["L0_attn_x"], np.float32)
    const = host_constants(inputs)
    nc = build_program()
    in_maps = []
    for core in range(8):
        b, r = core // 4, core % 4
        order = np.concatenate([np.arange(r * TOK, (r + 1) * TOK),
                                np.arange(0, r * TOK),
                                np.arange((r + 1) * TOK, SB)])
        xTc = x[b][order].T
        m = dict(const)
        m["xT"] = np.ascontiguousarray(
            xTc.reshape(8, 4, 128, 8, 512).transpose(0, 2, 3, 1, 4)
            .reshape(1024, 16384))
        in_maps.append(m)
    res = run_bass_kernel_spmd(nc, in_maps, core_ids=list(range(8)))
    outf = np.empty((2, SB, D), np.float32)
    for core in range(8):
        b, r = core // 4, core % 4
        outf[b, r * TOK:(r + 1) * TOK, :] = res.results[core]["out"].T
    return outf


# version 07 (final)
# speedup vs baseline: 1.8672x; 1.0887x over previous
import contextlib
import math
import numpy as np
import ml_dtypes
import concourse.bass as bass
import concourse.mybir as mybir
from concourse.bass_utils import run_bass_kernel_spmd

F32 = mybir.dt.float32
BF16 = mybir.dt.bfloat16
U8 = mybir.dt.uint8
AF = mybir.ActivationFunctionType
ALU = mybir.AluOpType

D = 4096
SB = 4096
TOK = 1024
NCH = 32
NKT = 32
NBLK = 8
NSLOT = 8
NQ = 2
EPS = 1e-5
CSHIFT = 8.0

DEBUG = False
TRUNC = 5


class Op:
    __slots__ = ("stream", "idx", "fn", "deps", "inc", "num", "grp", "gseq")

    def __init__(self, stream, idx, fn):
        self.stream = stream
        self.idx = idx
        self.fn = fn
        self.deps = []
        self.inc = False
        self.num = None
        self.grp = None
        self.gseq = 0


class Builder:
    STREAMS = ("pe", "act", "dve", "pool", "syn")

    def __init__(self):
        self.ops = {s: [] for s in self.STREAMS}
        self.last_write = {}
        self.readers = {}
        self.grps = {}
        self.gseq = 0

    def add(self, stream, fn, reads=(), writes=(), self_dep=False):
        op = Op(stream, len(self.ops[stream]), fn)
        self.gseq += 1
        op.gseq = self.gseq
        deps = []
        for b in reads:
            w = self.last_write.get(b)
            if w is not None:
                deps.append(w)
        for b in writes:
            w = self.last_write.get(b)
            if w is not None:
                deps.append(w)
            deps.extend(self.readers.get(b, ()))
        if stream == "pe" and not self_dep:
            op.deps = [d for d in deps if d.stream != stream]
        else:
            op.deps = deps
        self.ops[stream].append(op)
        for b in writes:
            self.last_write[b] = op
            self.readers[b] = []
        for b in reads:
            self.readers.setdefault(b, []).append(op)
        return op

    def pe(self, fn, reads=(), writes=()):
        return self.add("pe", fn, reads, writes)

    def act(self, fn, reads=(), writes=(), self_dep=False):
        return self.add("act", fn, reads, writes, self_dep=self_dep)

    def dve(self, fn, reads=(), writes=(), self_dep=False):
        return self.add("dve", fn, reads, writes, self_dep=self_dep)

    def dma(self, q, out_ap, in_ap, grp=None, reads=(), writes=(), **kw):
        assert grp is not None, "dma needs a sem group"
        def fn(eng):
            return eng.dma_start(out_ap, in_ap, **kw)
        op = self.add(q, fn, reads, writes)
        op.grp = grp
        self.grps.setdefault(grp, []).append(op)
        return op

    def finalize(self):
        for g, ops in self.grps.items():
            for i, op in enumerate(ops):
                op.num = 16 * (i + 1)
                op.inc = True
        for s in self.STREAMS:
            for op in self.ops[s]:
                merged = {}
                for d in op.deps:
                    key = d.grp if d.grp is not None else d.stream
                    cur = merged.get(key)
                    if cur is None:
                        merged[key] = d
                    elif d.grp is not None:
                        if d.num > cur.num:
                            merged[key] = d
                    elif d.idx > cur.idx:
                        merged[key] = d
                op.deps = list(merged.values())
                for d in op.deps:
                    d.inc = True
        for s in ("pe", "act", "dve"):
            n = 0
            for op in self.ops[s]:
                if op.grp is not None:
                    continue
                if op.inc:
                    n += 1
                op.num = n

    def emit_stream(self, s, eng, sems):
        import bisect
        gseqs = {g: [o.gseq for o in ops] for g, ops in self.grps.items()}
        wm = {}
        for op in self.ops[s]:
            for d in op.deps:
                key = d.grp if d.grp is not None else d.stream
                if d.grp is not None:
                    k = bisect.bisect_left(gseqs[d.grp], op.gseq)
                    val = 16 * max(k, (d.num + 15) // 16)
                else:
                    val = d.num
                if val > wm.get(key, 0):
                    eng.wait_ge(sems[key], val)
                    wm[key] = val
            inst = op.fn(eng)
            if op.inc:
                key = op.grp if op.grp is not None else s
                inst.then_inc(sems[key], 16 if op.grp is not None else 1)


def build_program():
    nc = bass.Bass("TRN2", target_bir_lowering=False)
    B = Builder()
    stack = contextlib.ExitStack()

    def din(name, shape, dtype=F32):
        return nc.dram_tensor(name, list(shape), dtype, kind="ExternalInput").ap()

    xT = din("xT", (1024, 16384))
    wcat = din("wcat", (128, NCH * 17))
    b17_d = din("b17_d", (17, 1))
    wqk_ext = din("wqk_ext", (5, 16 * 32))
    wqk_aug = din("wqk_aug", (13, 128))
    e4_d = din("e4_d", (128, 32))
    e4t_d = din("e4t_d", (32, 128))
    ones_dr = din("ones_dr", (1, NSLOT * TOK))
    ones_col = din("ones_col", (128, 1))
    ident_d = din("ident_d", (16, 16))
    wvall_d = din("wvall_d", (128, D))
    bvu_d = din("bvu_d", (128, 32))
    wo_d = din("wo_d", (1024, 16384))
    bo_d = din("bo_d", (128, 32))
    wbT_d = din("wbT_d", (D, D))
    encT_d = din("encT_d", (D, 40))
    bb_d = din("bb_d", (128, 32))
    co0 = din("co0", (3, D, 32))
    co1 = din("co1", (3, D, 8))
    mk0 = din("mk0", (3, D, 32), U8)
    mk1 = din("mk1", (3, D, 8), U8)
    mmat = din("mmat", (1024, 16384), BF16)

    dbg = "ExternalOutput" if DEBUG else "Internal"
    out = nc.dram_tensor("out", [D, TOK], F32, kind="ExternalOutput").ap()
    c_stage = nc.dram_tensor("c_stage", [32, TOK], F32, kind=dbg).ap()
    ctxu_stage = nc.dram_tensor("ctxu_stage", [NQ * NSLOT * 16, 512], F32, kind=dbg).ap()
    dnm_stage = nc.dram_tensor("dnm_stage", [NQ * NSLOT * 4, 512], F32, kind=dbg).ap()
    x1_spill = nc.dram_tensor("x1_spill", [D, TOK], F32, kind=dbg).ap()
    ffn_spill = nc.dram_tensor("ffn_spill", [D, TOK], BF16, kind=dbg).ap()
    attn_dbg = nc.dram_tensor("attn_dbg", [D, TOK], F32, kind=dbg).ap()
    attn_sp = nc.dram_tensor("attn_sp", [D, TOK], BF16, kind="Internal").ap()
    cqkv_dbg = nc.dram_tensor("cqkv_dbg", [17, SB], F32, kind=dbg).ap()
    q_dbg = nc.dram_tensor("q_dbg", [128, TOK], F32, kind=dbg).ap()
    a_dbg = nc.dram_tensor("a_dbg", [128, D], F32, kind=dbg).ap()
    bc_dbg = nc.dram_tensor("bc_dbg", [128, 32], F32, kind=dbg).ap()
    ctxn_dbg = nc.dram_tensor("ctxn_dbg", [128, TOK], F32, kind=dbg).ap()
    h_dbg = nc.dram_tensor("h_dbg", [40, D], F32, kind=dbg).ap()
    u_dbg = nc.dram_tensor("u_dbg", [40, TOK], F32, kind=dbg).ap()
    g_dbg = nc.dram_tensor("g_dbg", [128, NCH * 40], F32, kind=dbg).ap()

    def sbuf(name, shape, dtype):
        return stack.enter_context(nc.sbuf_tensor(name, list(shape), dtype))

    def psum(name, shape, dtype=F32):
        return stack.enter_context(nc.psum_tensor(name, list(shape), dtype))

    proB = sbuf("proB", (128, 4096), F32)
    proB_bf = proB.bitcast(BF16)
    xblk = [proB_bf[:, 0:2048], proB_bf[:, 2048:4096]]
    xsq_t = sbuf("xsq", (128, 2048), BF16)
    wcat_sb = sbuf("wcat_sb", (128, NCH * 17), BF16)
    b17_sb = sbuf("b17_sb", (17, 1), F32)
    sneg = sbuf("sneg", (17, 1), F32)
    blk_f = sbuf("blk_f", (17, 512), F32)
    blk_fix = sbuf("blk_fix", (17, 512), F32)
    rowpool = sbuf("rowpool", (1, 4608), F32)
    ones17 = sbuf("ones17", (1, 32), F32)
    cqb = proB_bf[:, 4096:5120]
    qsb = proB_bf[:, 5120:6144]
    c_sb = proB[:, 3072:4096]
    phA = sbuf("phA", (128, NKT * 128 + 3 * 2048 + 2 * TOK), BF16)
    kh2 = phA[:, 0:NKT * 128]
    cqm2 = [phA[:, NKT * 128 + 3 * 2048:NKT * 128 + 3 * 2048 + TOK],
            phA[:, NKT * 128 + 3 * 2048 + TOK:]]
    ckvlat_t = sbuf("ckvlat", (128, SB), BF16)
    ckvlat = ckvlat_t
    ckv32 = ckvlat_t.bitcast(F32)
    ckv1 = sbuf("ckv1", (128, NKT * 8), BF16)
    expsc = phA[:, NKT * 128:NKT * 128 + 3 * 2048]
    ctx_sp = sbuf("ctx_sp", (128, 512), F32)
    ctx_u = sbuf("ctx_u", (128, 512), F32)
    dnm = sbuf("dnm", (32, 512), F32)
    dtmp = sbuf("dtmp", (32, 512), F32)
    dtmp2 = sbuf("dtmp2", (32, 512), F32)
    ctx_n = sbuf("ctx_n", (128, 512), BF16)
    a_bf = sbuf("a_bf", (128, D), BF16)
    bc_sb = sbuf("bc_sb", (128, 32), F32)
    bo_sb = sbuf("bo_sb", (128, 32), F32)
    bb_sb = sbuf("bb_sb", (128, 32), F32)
    attn_c = sbuf("attn_c", (128, 512), BF16)
    attn_c2 = sbuf("attn_c2", (128, 512), BF16)
    attn_r = sbuf("attn_r", (128, 512), BF16)
    attn_rb = sbuf("attn_rb", (128, 512), BF16)
    sq_c = sbuf("sq_c", (128, 512), BF16)
    sq_c2 = sbuf("sq_c2", (128, 512), BF16)
    x1_bf = sbuf("x1_bf", (128, NCH * TOK), BF16)
    xf_c = [sbuf(f"xf_c{i}", (128, 512), F32) for i in range(2)]
    x1f_c = sbuf("x1f_c", (128, 512), F32)
    x1f_c2 = sbuf("x1f_c2", (128, 512), F32)
    hcat = sbuf("hcat", (40, D), BF16)
    g0all = sbuf("g0all", (128, NCH * 32), BF16)
    g1all = sbuf("g1all", (128, NCH * 8), BF16)
    u_sb = sbuf("u_sb", (40, TOK), BF16)
    wstr = [sbuf(f"wstr{i}", (128, 4096), BF16) for i in range(2)]
    wvall = wstr[0]
    x1f = x1_bf.bitcast(F32)
    x1u8 = x1_bf.bitcast(U8)
    cfw = x1f[:, 0:1024]
    acc0 = x1f[:, 1024:2048]
    gtmpw = x1f[:, 2048:3072]
    mkwf = x1_bf[:, 6144:7168]
    mkw8 = x1u8[:, 14336:15360]
    bcrow = x1f[0:1, 4096:8192]
    xraw = [x1f[:, 8192:10240], x1f[:, 10240:12288]]
    xsq = [xsq_t, x1_bf[:, 24576:26624]]
    wos2 = [phA[:, 0:2048], phA[:, 2048:4096]]
    ms2 = [phA[:, 4096:6144], phA[:, 6144:8192]]
    enc_sb = proB_bf[:, 4096:5376]
    wqk_sb = sbuf("wqk_sb", (5, 16 * 32), BF16)
    wqka_sb = sbuf("wqka_sb", (13, 128), BF16)
    e4_sb = sbuf("e4_sb", (128, 32), BF16)
    e4t_sb = sbuf("e4t_sb", (32, 128), F32)
    ident_sb = sbuf("ident_sb", (16, 16), F32)
    ones_bf = sbuf("ones_bf", (128, 1), BF16)
    ones_f32 = sbuf("ones_f32", (128, 1), F32)
    onesr_f32 = sbuf("onesr_f32", (1, 128), F32)
    bvu_sb = sbuf("bvu_sb", (128, NCH), BF16)
    rb_sb = proB[:, 3072:3584]
    mb_sb = proB[:, 3584:4096]
    rb3 = proB[:, 0:1024]
    mb3 = proB[:, 1024:2048]
    a32 = a_bf.bitcast(F32)
    rb4 = a32[:, 0:512]
    mb4 = a32[:, 512:1024]
    senc_sb = sbuf("senc_sb", (128, 1), F32)
    ffn_c = ckvlat_t[:, 0:512]
    ffn_r = ckvlat_t[:, 512:1024]
    ffn_r2 = a_bf[:, 3072:3584]
    out_c = ckv32[:, 512:1024]
    out_c2 = proB[:, 2048:2560]
    tt1 = ckv32[:, 1024:1536]
    tt2 = ckv32[:, 1536:2048]

    def slot(i, j=0):
        return rowpool[0:1, 512 * i:512 * (i + 1)]

    ring = psum("ring", (128, 3072))
    pctx = psum("pctx", (128, 512))
    paux = psum("paux", (128, 512))
    rbank = [ring[:, 512 * i:512 * (i + 1)] for i in range(6)]
    RB = [("rng", i) for i in range(6)]
    PCTX = ("pctx",)
    PAUX = ("paux",)



    B.dma("pool", wcat_sb[:, :], wcat[:, :], grp="cpool", writes=[("wcat_sb",)])
    B.dma("pool", wqk_sb[:], wqk_ext[:], grp="cpool", writes=[("wqk_sb",)])
    B.dma("pool", wqka_sb[:], wqk_aug[:], grp="cpool", writes=[("wqka_sb",)])
    B.dma("pool", e4_sb[:], e4_d[:], grp="cpool", writes=[("e4_sb",)])
    B.dma("syn", e4t_sb[:], e4t_d[:], grp="csyn", writes=[("e4t_sb",)])
    B.dma("syn", b17_sb[:], b17_d[:], grp="csyn", writes=[("b17_sb",)])
    B.dma("syn", ident_sb[:], ident_d[:], grp="csyn", writes=[("ident_sb",)])
    B.dma("syn", ones17[:], ones_dr[0:1, 0:32], grp="csyn", writes=[("ones17",)])
    B.dma("pool", ones_bf[:], ones_col[:], grp="cpool", writes=[("ones_bf",)])
    B.dma("syn", ones_f32[:], ones_col[:], grp="csyn", writes=[("ones_f32",)])
    B.dma("syn", onesr_f32[:], ones_dr[0:1, 0:128], grp="csyn", writes=[("onesr_f32",)])
    B.dma("syn", bo_sb[:], bo_d[:], grp="csyn", writes=[("bo_sb",)])
    B.dma("syn", bb_sb[:], bb_d[:], grp="csyn", writes=[("bb_sb",)])
    B.dma("pool", bvu_sb[:], bvu_d[:], grp="cpool", writes=[("bvu_sb",)])
    B.dma("pool", wvall[:, :], wvall_d[:, :], grp="cpool", writes=[("wstr", 0)])
    B.dve(lambda v: v.memset(ckvlat[:], 1.0), writes=[("ckvlat",)])
    B.dve(lambda v: v.memset(ckv1[:], 1.0), writes=[("ckv1",)])
    B.dve(lambda v: v.memset(cqm2[0][:], 0.0), writes=[("cqm", 0)])
    B.dve(lambda v: v.memset(cqm2[1][:], 0.0), writes=[("cqm", 1)])
    B.dve(lambda v: v.memset(cqb[:], 0.0), writes=[("cqb",)])
    B.dma("pool", cqb[12:13, :], ones_dr[0:1, 0:TOK], grp="cpool", writes=[("cqb",)])
    for bf_ in range(2):
        for g in range(4):
            B.dma("pool", cqm2[bf_][32 * g + 12:32 * g + 13, :],
                  ones_dr[0:1, 0:TOK], grp="cpool", writes=[("cqm", bf_)])

    for half, (co_, mk_, gall, wdt) in enumerate(
            [(co0, mk0, g0all, 32), (co1, mk1, g1all, 8)]):
        W = NCH * wdt
        for e in range(3):
            B.dma("syn", cfw[:, 0:W].rearrange("p (cc w) -> p cc w", w=wdt),
                  co_[e, :, :].rearrange("(cc p) w -> p cc w", p=128),
                  grp=f"gld{half}", writes=[("cfe",)])
            B.dma("syn", mkw8[:, 0:W].rearrange("p (cc w) -> p cc w", w=wdt),
                  mk_[e, :, :].rearrange("(cc p) w -> p cc w", p=128),
                  grp=f"gld{half}", writes=[("mku",)])
            B.dve(lambda v, W=W: v.tensor_copy(mkwf[:, 0:W], mkw8[:, 0:W]),
                  reads=[("mku",)], writes=[("mkf",)], self_dep=True)
            if e == 0:
                B.dve(lambda v, W=W: v.tensor_mul(acc0[:, 0:W], cfw[:, 0:W],
                                                  mkwf[:, 0:W]),
                      reads=[("cfe",), ("mkf",)], writes=[("ga",)], self_dep=True)
            else:
                B.dve(lambda v, W=W: v.tensor_mul(gtmpw[:, 0:W], cfw[:, 0:W],
                                                  mkwf[:, 0:W]),
                      reads=[("cfe",), ("mkf",)], writes=[("gtmp",)], self_dep=True)
                B.dve(lambda v, W=W: v.tensor_add(acc0[:, 0:W], acc0[:, 0:W],
                                                  gtmpw[:, 0:W]),
                      reads=[("gtmp",)], writes=[("ga",)], self_dep=True)
        B.dve(lambda v, gl=gall, W=W: v.tensor_copy(gl[:, :], acc0[:, 0:W]),
              reads=[("ga",)], writes=[("gall", half)], self_dep=True)

    for c in range(NCH):
        B.pe(lambda t, c=c: t.matmul(paux[0:17, 0:1], wcat_sb[:, 17 * c:17 * (c + 1)],
                                     ones_bf[:], start=(c == 0), stop=(c == NCH - 1)),
             reads=[("wcat_sb",), ("ones_bf",)], writes=[PAUX])
    B.dve(lambda v: v.tensor_scalar_mul(sneg[:], paux[0:17, 0:1], -1.0),
          reads=[PAUX], writes=[("sneg",)], self_dep=True)

    for blk in range(NBLK):
        for q4 in range(8):
            par = q4 % 2
            xb, xq, xr = xblk[par], xsq[par], xraw[par]
            B.dma("syn" if par == 0 else "act", xr[:, :],
                  xT[128 * q4:128 * (q4 + 1), 2048 * blk:2048 * (blk + 1)],
                  grp=f"xb{par}", writes=[("xraw", par)])
            B.act(lambda s, xb=xb, xr=xr: s.activation(xb[:, :], xr[:, :], AF.Copy),
                  reads=[("xraw", par)], writes=[("xblk", par)])
            B.act(lambda s, xq=xq, xr=xr: s.activation(xq[:, :], xr[:, :], AF.Square),
                  reads=[("xraw", par)], writes=[("xsq", par)])
            for i in range(4):
                c = 4 * q4 + i
                B.pe(lambda t, xb=xb, i=i, c=c:
                     t.matmul(paux[0:17, :], wcat_sb[:, 17 * c:17 * (c + 1)],
                              xb[:, 512 * i:512 * (i + 1)],
                              start=(c == 0), stop=(c == NCH - 1)),
                     reads=[("xblk", par), ("wcat_sb",)], writes=[PAUX])
                B.pe(lambda t, xq=xq, i=i, c=c:
                     t.matmul(pctx[0:1, :], ones_bf[:],
                              xq[:, 512 * i:512 * (i + 1)],
                              start=(c == 0), stop=(c == NCH - 1),
                              skip_group_check=True),
                     reads=[("xsq", par), ("ones_bf",)], writes=[PCTX])
                B.pe(lambda t, xb=xb, i=i, c=c:
                     t.matmul(pctx[32:33, :], ones_bf[:],
                              xb[:, 512 * i:512 * (i + 1)],
                              start=(c == 0), stop=(c == NCH - 1),
                              tile_position=(0, 32), skip_group_check=True),
                     reads=[("xblk", par), ("ones_bf",)], writes=[PCTX])
        B.dve(lambda v: v.tensor_copy(blk_f[0:17, :], paux[0:17, :]),
              reads=[PAUX], writes=[("blk_f",)])
        B.dve(lambda v: v.tensor_scalar_mul(slot(0), pctx[32:33, :], 1.0 / D),
              reads=[PCTX], writes=[("rp", 0)])
        B.dve(lambda v: v.tensor_scalar_mul(slot(1), pctx[0:1, :], 1.0 / D),
              reads=[PCTX], writes=[("rp", 1)])
        B.dve(lambda v: v.tensor_mul(slot(2), slot(0), slot(0)),
              reads=[("rp", 0)], writes=[("rp", 2)], self_dep=True)
        B.dve(lambda v: v.tensor_sub(slot(1), slot(1), slot(2)),
              reads=[("rp", 2), ("rp", 1)], writes=[("rp", 1)], self_dep=True)
        B.dve(lambda v: v.tensor_scalar_add(slot(1), slot(1), EPS),
              reads=[("rp", 1)], writes=[("rp", 1)], self_dep=True)
        B.act(lambda s: s.activation(slot(3), slot(1), AF.Ln),
              reads=[("rp", 1)], writes=[("rp", 3)])
        B.act(lambda s: s.activation(slot(3), slot(3), AF.Exp, scale=-0.5),
              reads=[("rp", 3)], writes=[("rp", 3)], self_dep=True)
        B.dve(lambda v: v.tensor_mul(slot(4), slot(0), slot(3)),
              reads=[("rp", 0), ("rp", 3)], writes=[("rp", 4)], self_dep=True)
        B.pe(lambda t: t.matmul(rbank[0][0:17, :], ones17[0:1, 0:17], slot(3),
                                start=True, stop=True),
             reads=[("rp", 3), ("ones17",)], writes=[RB[0]])
        B.pe(lambda t: t.matmul(rbank[1][0:17, :], ones17[0:1, 0:17], slot(4),
                                start=True, stop=True),
             reads=[("rp", 4), ("ones17",)], writes=[RB[1]])
        B.dve(lambda v: v.tensor_mul(blk_fix[:], blk_f[0:17, :], rbank[0][0:17, :]),
              reads=[("blk_f",), RB[0]], writes=[("blk_fix",)])
        B.dve(lambda v: v.scalar_tensor_tensor(blk_fix[:], rbank[1][0:17, :], sneg[:],
                                               blk_fix[:], ALU.mult, ALU.add),
              reads=[RB[1], ("sneg",), ("blk_fix",)], writes=[("blk_fix",)],
              self_dep=True)
        B.dve(lambda v: v.tensor_scalar_add(blk_fix[:], blk_fix[:], b17_sb[:]),
              reads=[("b17_sb",), ("blk_fix",)], writes=[("blk_fix",)])
        B.dma("pool", ckvlat[0:4, 512 * blk:512 * (blk + 1)], blk_fix[12:16, :],
              grp="ckv", reads=[("blk_fix",)], writes=[("ckvlat", blk)])
        for i in range(4):
            kt = 4 * blk + i
            B.pe(lambda t, i=i: t.transpose(rbank[2][:, 0:16],
                                            blk_fix[0:16, 128 * i:128 * (i + 1)],
                                            ident_sb[:]),
                 reads=[("blk_fix",), ("ident_sb",)], writes=[RB[2]])
            B.dve(lambda v, kt=kt: v.tensor_copy(ckv1[:, 8 * kt:8 * kt + 4],
                                                 rbank[2][:, 12:16]),
                  reads=[RB[2]], writes=[("ckv1", kt)])
        if DEBUG:
            B.dma("syn", cqkv_dbg[:, 512 * blk:512 * (blk + 1)], blk_fix[:],
                  grp="dbg", reads=[("blk_fix",)], writes=[("cqkv_dbg",)])
        if blk < 2:
            B.dve(lambda v, blk=blk:
                  v.tensor_copy(cqb[0:12, 512 * blk:512 * (blk + 1)], blk_fix[0:12, :]),
                  reads=[("blk_fix",)], writes=[("cqb", blk)])
            for g in range(4):
                for bf_ in range(2):
                    B.dma("pool", cqm2[bf_][32 * g:32 * g + 12,
                                            512 * blk:512 * (blk + 1)],
                          blk_fix[0:12, :],
                          grp="cqc", reads=[("blk_fix",)], writes=[("cqm", bf_)])

    for h2 in range(2):
        B.pe(lambda t, h2=h2: t.matmul(rbank[0][:, :], wqka_sb[:],
                                       cqb[0:13, 512 * h2:512 * (h2 + 1)],
                                       start=True, stop=True),
             reads=[("cqb", 0), ("cqb", 1), ("wqka_sb",)], writes=[RB[0]])
        B.dve(lambda v, h2=h2: v.tensor_copy(qsb[:, 512 * h2:512 * (h2 + 1)],
                                             rbank[0][:, :]),
              reads=[RB[0]], writes=[("qsb", h2)])
        if DEBUG:
            B.dma("pool", q_dbg[:, 512 * h2:512 * (h2 + 1)],
                  qsb[:, 512 * h2:512 * (h2 + 1)],
                  grp="dbg", reads=[("qsb", h2)], writes=[("q_dbg",)])
        B.dve(lambda v, h2=h2: v.tensor_mul(sq_c[:],
                                            qsb[:, 512 * h2:512 * (h2 + 1)],
                                            qsb[:, 512 * h2:512 * (h2 + 1)]),
              reads=[("qsb", h2)], writes=[("sq_c",)])
        B.pe(lambda t, h2=h2: t.matmul(rbank[1][0:32, :], e4_sb[:], sq_c[:],
                                       start=True, stop=True),
             reads=[("sq_c",), ("e4_sb",)], writes=[RB[1]])
        B.dve(lambda v, h2=h2: v.tensor_scalar(c_sb[0:32, 512 * h2:512 * (h2 + 1)],
                                               rbank[1][0:32, :], 0.5, CSHIFT,
                                               ALU.mult, ALU.add),
              reads=[RB[1]], writes=[("c_sb", h2)])
    B.dma("syn", c_stage[:], c_sb[0:32, :], grp="cst",
          reads=[("c_sb", 0), ("c_sb", 1)], writes=[("c_stage",)])


    if TRUNC < 2:
        return finish(nc, B, stack)
    for seg in range(8):
        for cq in range(8):
            par = cq % 2
            B.dma("pool", wos2[par][:, :],
                  wo_d[128 * cq:128 * (cq + 1), 2048 * seg:2048 * (seg + 1)],
                  grp=f"wos{par}", writes=[("wos", par)])
            B.dma("syn", ms2[par][:, :],
                  mmat[128 * cq:128 * (cq + 1), 2048 * seg:2048 * (seg + 1)],
                  grp=f"ms{par}", writes=[("ms", par)])
            for i in range(4):
                c = 4 * cq + i
                B.pe(lambda t, c=c, par=par, i=i: t.matmul(
                    rbank[3][:, :], wvall[:, 128 * c:128 * (c + 1)],
                    wos2[par][:, 512 * i:512 * (i + 1)],
                    start=(c == 0), stop=(c == NCH - 1)),
                    reads=[("wos", par), ("wstr", 0)], writes=[RB[3]])
                B.pe(lambda t, c=c, par=par, i=i: t.matmul(
                    paux[0:1, :], bvu_sb[:, c:c + 1],
                    wos2[par][:, 512 * i:512 * (i + 1)],
                    start=(c == 0), stop=(c == NCH - 1), skip_group_check=True),
                    reads=[("wos", par), ("bvu_sb",)], writes=[PAUX])
            for i in range(4):
                c = 4 * cq + i
                B.pe(lambda t, c=c, par=par, i=i: t.matmul(
                    pctx[0:32, :], g0all[:, 32 * c:32 * (c + 1)],
                    ms2[par][:, 512 * i:512 * (i + 1)],
                    start=(c == 0), stop=(c == NCH - 1), skip_group_check=True),
                    reads=[("ms", par), ("gall", 0)], writes=[PCTX])
                B.pe(lambda t, c=c, par=par, i=i: t.matmul(
                    pctx[32:40, :], g1all[:, 8 * c:8 * (c + 1)],
                    ms2[par][:, 512 * i:512 * (i + 1)],
                    start=(c == 0), stop=(c == NCH - 1), tile_position=(0, 32),
                    skip_group_check=True),
                    reads=[("ms", par), ("gall", 1)], writes=[PCTX])
        B.dve(lambda v, og=seg: v.tensor_copy(a_bf[:, 512 * og:512 * (og + 1)],
                                              rbank[3][:, :]),
              reads=[RB[3]], writes=[("a_bf", seg)])
        B.dve(lambda v, og=seg: v.tensor_copy(bcrow[0:1, 512 * og:512 * (og + 1)],
                                              paux[0:1, :]),
              reads=[PAUX], writes=[("bcrow",)])
        B.dve(lambda v, jg=seg: v.tensor_scalar_mul(hcat[0:32, 512 * jg:512 * (jg + 1)],
                                                    pctx[0:32, :], 0.5),
              reads=[PCTX], writes=[("hcat",)])
        B.dve(lambda v, jg=seg: v.tensor_scalar_mul(hcat[32:40, 512 * jg:512 * (jg + 1)],
                                                    pctx[32:40, :], 0.5),
              reads=[PCTX], writes=[("hcat",)])
    for k in range(32):
        B.pe(lambda t, k=k: t.transpose(paux[:, k:k + 1],
                                        bcrow[0:1, 128 * k:128 * (k + 1)],
                                        ones_f32[0:1, 0:1]),
             reads=[("bcrow",), ("ones_f32",)], writes=[PAUX])
    B.dve(lambda v: v.tensor_copy(bc_sb[:, :], paux[:, 0:32]),
          reads=[PAUX], writes=[("bc_sb",)])
    B.dve(lambda v: v.tensor_add(bc_sb[:], bc_sb[:], bo_sb[:]),
          reads=[("bo_sb",), ("bc_sb",)], writes=[("bc_sb",)], self_dep=True)

    if DEBUG:
        B.dma("pool", a_dbg[:], a_bf[:], grp="dbg", reads=[("a_bf", og) for og in range(8)],
              writes=[("a_dbg",)])
        B.dma("syn", bc_dbg[:], bc_sb[:], grp="dbg", reads=[("bc_sb",)], writes=[("bc_dbg",)])

    if DEBUG:
        B.dma("pool", h_dbg[:], hcat[0:40, :], grp="dbg", reads=[("hcat",)], writes=[("h_dbg",)])
        B.dma("pool", g_dbg[:, 0:NCH * 32], g0all[:], grp="dbg", reads=[("gall", 0)],
              writes=[("g_dbg",)])
        B.dma("pool", g_dbg[:, NCH * 32:NCH * 32 + NCH * 8], g1all[:],
              grp="dbg", reads=[("gall", 1)], writes=[("g_dbg",)])
    if TRUNC < 3:
        return finish(nc, B, stack)
    B.dve(lambda v: v.memset(paux[:, :], 0.0), writes=[PAUX])
    B.dve(lambda v: v.memset(pctx[:, :], 0.0), writes=[PCTX])
    for s_ in range(NSLOT):
        cb = s_ % 2
        for g in range(4):
            B.dma("pool", cqm2[cb][32 * g + 13:32 * g + 14, :],
                  c_stage[4 * s_ + g:4 * s_ + g + 1, :],
                  grp=f"cq{cb}", reads=[("c_stage",)], writes=[("cqm", cb)])
        for ktg in range(8):
            for g in range(4):
                B.pe(lambda t, g=g, ktg=ktg, s_=s_: t.matmul(
                    paux[32 * g:32 * g + 16, :],
                    wqk_sb[0:5, 16 * (4 * s_ + g):16 * (4 * s_ + g) + 16],
                    ckvlat[0:5, 512 * ktg:512 * (ktg + 1)],
                    start=True, stop=True, tile_position=(0, 32 * g)),
                    reads=[("wqk_sb",), ("ckvlat", ktg)],
                    writes=[PAUX])
            B.dve(lambda v, ktg=ktg: v.tensor_scalar_mul(
                kh2[:, 512 * ktg:512 * (ktg + 1)], paux[:, :], 0.5),
                reads=[PAUX], writes=[("kh2", ktg)])
        for q in range(NQ):
            prev = None
            for kt in range(NKT):
                pa = (2 * kt) % 3
                pb = (2 * kt + 1) % 3
                slot3 = kt % 3
                for g in range(4):
                    pair = pa if g < 2 else pb
                    bank = 2 * pair + (g & 1)
                    B.pe(lambda t, g=g, kt=kt, cb=cb, q=q, bank=bank: t.matmul(
                        rbank[bank][:, :],
                        kh2[32 * g:32 * g + 16, 128 * kt:128 * (kt + 1)],
                        cqm2[cb][32 * g:32 * g + 16, 512 * q:512 * (q + 1)],
                        start=True, stop=True, tile_position=(32 * g, 0)),
                        reads=[("kh2", kt // 4), ("cqm", cb)], writes=[RB[bank]])
                rv = ring.rearrange("p (c n) -> p c n", n=1024)
                dst = expsc[:, 2048 * slot3:2048 * (slot3 + 1)]
                if pb == pa + 1:
                    src = ring[:, 1024 * pa:1024 * (pb + 1)]
                    dst2 = dst
                else:
                    src = rv[:, ::-2, :]
                    dst2 = dst.rearrange("p (c n) -> p c n", n=1024)
                B.act(lambda s, src=src, dst2=dst2: s.activation(dst2, src, AF.Exp),
                      reads=[RB[2 * pa], RB[2 * pa + 1], RB[2 * pb], RB[2 * pb + 1]],
                      writes=[("expsc", slot3)])
                if prev is not None:
                    pkt, pslot3 = prev
                    for g in range(4):
                        pair_off = 0 if g < 2 else 1024
                        off = 2048 * pslot3 + pair_off + 512 * (g & 1)
                        B.pe(lambda t, g=g, pkt=pkt, off=off: t.matmul(
                            pctx[32 * g:32 * g + 5, :],
                            ckv1[:, 8 * pkt:8 * pkt + 5],
                            expsc[:, off:off + 512],
                            start=(pkt == 0), stop=(pkt == NKT - 1),
                            tile_position=(0, 32 * g), skip_group_check=True),
                            reads=[("expsc", pslot3), ("ckv1", pkt)],
                            writes=[PCTX])
                prev = (kt, slot3)
            pkt, pslot3 = prev
            for g in range(4):
                pair_off = 0 if g < 2 else 1024
                off = 2048 * pslot3 + pair_off + 512 * (g & 1)
                B.pe(lambda t, g=g, pkt=pkt, off=off: t.matmul(
                    pctx[32 * g:32 * g + 5, :],
                    ckv1[:, 8 * pkt:8 * pkt + 5],
                    expsc[:, off:off + 512],
                    start=(pkt == 0), stop=(pkt == NKT - 1),
                    tile_position=(0, 32 * g), skip_group_check=True),
                    reads=[("expsc", pslot3), ("ckv1", pkt)], writes=[PCTX])
            B.dve(lambda v: v.tensor_copy(ctx_sp[:], pctx[:]),
                  reads=[PCTX], writes=[("ctx_sp",)])
            bu = (q * NSLOT + s_) * 16
            bd = (q * NSLOT + s_) * 4
            for g in range(4):
                B.dma("syn", ctxu_stage[bu + 4 * g:bu + 4 * (g + 1), :],
                      ctx_sp[32 * g:32 * g + 4, :],
                      grp="ctw", reads=[("ctx_sp",)], writes=[("ctx_stage", q, s_)])
                B.dma("syn", dnm_stage[bd + g:bd + g + 1, :],
                      ctx_sp[32 * g + 4:32 * g + 5, :],
                      grp="ctw", reads=[("ctx_sp",)], writes=[("ctx_stage", q, s_)])

    def stat_finish(pbank, pkey, ri, mi):
        B.dve(lambda v: v.tensor_scalar_mul(slot(0), pbank[0:1, :], 1.0 / D),
              reads=[pkey], writes=[("rp", 0)])
        B.dve(lambda v: v.tensor_scalar_mul(slot(1), pbank[32:33, :], 1.0 / D),
              reads=[pkey], writes=[("rp", 1)])
        B.dve(lambda v: v.tensor_mul(slot(2), slot(0), slot(0)),
              reads=[("rp", 0)], writes=[("rp", 2)], self_dep=True)
        B.dve(lambda v: v.tensor_sub(slot(1), slot(1), slot(2)),
              reads=[("rp", 2), ("rp", 1)], writes=[("rp", 1)], self_dep=True)
        B.dve(lambda v: v.tensor_scalar_add(slot(1), slot(1), EPS),
              reads=[("rp", 1)], writes=[("rp", 1)], self_dep=True)
        B.act(lambda s: s.activation(slot(ri), slot(1), AF.Ln),
              reads=[("rp", 1)], writes=[("rp", ri)])
        B.act(lambda s: s.activation(slot(ri), slot(ri), AF.Exp, scale=-0.5),
              reads=[("rp", ri)], writes=[("rp", ri)], self_dep=True)
        B.dve(lambda v: v.tensor_mul(slot(mi), slot(0), slot(ri)),
              reads=[("rp", 0), ("rp", ri)], writes=[("rp", mi)], self_dep=True)

    if TRUNC < 4:
        return finish(nc, B, stack)
    B.dve(lambda v: v.engine_nop(),
          writes=[("ckvlat", b) for b in range(NBLK)] +
                 [("c_sb", 0), ("c_sb", 1), ("ckvlat",)])
    for q in range(NQ):
        B.dma("syn", ctx_u[:, :], ctxu_stage[q * 128:(q + 1) * 128, :],
              grp=f"cxr{q}", reads=[("ctx_stage", q, s_) for s_ in range(NSLOT)],
              writes=[("ctx_u", q)])
        B.dma("syn", dnm[:, :], dnm_stage[q * 32:(q + 1) * 32, :],
              grp=f"cxr{q}", reads=[("ctx_stage", q, s_) for s_ in range(NSLOT)],
              writes=[("dnm", q)])
        B.dve(lambda v: v.reciprocal(dtmp[:], dnm[:]),
              reads=[("dnm", q)], writes=[("dtmp",)])
        B.dve(lambda v: v.tensor_mul(dtmp2[:], dnm[:], dtmp[:]),
              reads=[("dnm", q), ("dtmp",)], writes=[("dtmp2",)], self_dep=True)
        B.dve(lambda v: v.tensor_scalar(dtmp2[:], dtmp2[:], -1.0, 2.0, ALU.mult, ALU.add),
              reads=[("dtmp2",)], writes=[("dtmp2",)], self_dep=True)
        B.dve(lambda v: v.tensor_mul(dtmp[:], dtmp[:], dtmp2[:]),
              reads=[("dtmp",), ("dtmp2",)], writes=[("dtmp",)], self_dep=True)
        B.pe(lambda t: t.matmul(rbank[2][:, :], e4t_sb[:], dtmp[:],
                                start=True, stop=True),
             reads=[("dtmp",), ("e4t_sb",)], writes=[RB[2]])
        B.dve(lambda v: v.tensor_mul(ctx_n[:], ctx_u[:], rbank[2][:, :]),
              reads=[("ctx_u", q), RB[2]], writes=[("ctx_n", q)])
        if DEBUG:
            B.dma("pool", ctxn_dbg[:, 512 * q:512 * (q + 1)], ctx_n[:],
                  grp="dbg", reads=[("ctx_n", q)], writes=[("ctxn_dbg",)])
        atb, sqb = [attn_c, attn_c2], [sq_c, sq_c2]

        def attn_stats(ot, q):
            po = ot % 2
            B.pe(lambda t, ot=ot, po=po: t.matmul(
                rbank[2][0:1, :], ones_bf[:], atb[po][:, :],
                start=(ot == 0), stop=(ot == NCH - 1), skip_group_check=True),
                reads=[("attn_c", po), ("ones_bf",)], writes=[RB[2]])
            B.pe(lambda t, ot=ot, po=po: t.matmul(
                rbank[2][32:33, :], ones_bf[:], sqb[po][:, :],
                start=(ot == 0), stop=(ot == NCH - 1), tile_position=(0, 32),
                skip_group_check=True),
                reads=[("sq_a", po), ("ones_bf",)], writes=[RB[2]])
            B.dma("syn", attn_sp[128 * ot:128 * (ot + 1), 512 * q:512 * (q + 1)],
                  atb[po][:, :], grp=f"aw{po}", reads=[("attn_c", po)],
                  writes=[("attn_spl", ot, q)])
            if DEBUG:
                B.dma("pool", attn_dbg[128 * ot:128 * (ot + 1), 512 * q:512 * (q + 1)],
                      atb[po][:, :], grp="dbg", reads=[("attn_c", po)],
                      writes=[("attn_dbg",)])

        pend = None
        for ot in range(NCH):
            pbk = ot % 2
            B.pe(lambda t, ot=ot, pbk=pbk: t.matmul(
                rbank[pbk][:, :], a_bf[:, 128 * ot:128 * (ot + 1)], ctx_n[:],
                start=True, stop=True),
                reads=[("ctx_n", q), ("a_bf", ot // 4)], writes=[RB[pbk]])
            B.dve(lambda v, ot=ot, pbk=pbk: v.tensor_scalar(
                atb[pbk][:, :], rbank[pbk][:, :],
                bc_sb[:, ot:ot + 1], None, ALU.add),
                reads=[RB[pbk], ("bc_sb",)], writes=[("attn_c", pbk)])
            B.dve(lambda v, pbk=pbk: v.tensor_mul(sqb[pbk][:, :], atb[pbk][:, :],
                                                  atb[pbk][:, :]),
                  reads=[("attn_c", pbk)], writes=[("sq_a", pbk)])
            if pend is not None:
                attn_stats(pend, q)
            pend = ot
        attn_stats(pend, q)
        stat_finish(rbank[2], RB[2], 3, 4)
        B.pe(lambda t: t.matmul(rbank[3][:, :], onesr_f32[:], slot(3),
                                start=True, stop=True),
             reads=[("rp", 3), ("onesr_f32",)], writes=[RB[3]])
        B.pe(lambda t: t.matmul(rbank[4][:, :], onesr_f32[:], slot(4),
                                start=True, stop=True),
             reads=[("rp", 4), ("onesr_f32",)], writes=[RB[4]])
        B.dve(lambda v: v.tensor_copy(rb_sb[:], rbank[3][:, :]),
              reads=[RB[3]], writes=[("rb_sb",)])
        B.dve(lambda v: v.tensor_copy(mb_sb[:], rbank[4][:, :]),
              reads=[RB[4]], writes=[("mb_sb",)])
        arb, x1b = [attn_r, attn_rb], [x1f_c, x1f_c2]
        for c in range(NCH):
            par = c % 2
            B.dma("syn", xf_c[par][:, :],
                  xT[128 * (c // 4):128 * (c // 4) + 128,
                     2048 * q + 512 * (c % 4):2048 * q + 512 * (c % 4) + 512],
                  grp=f"xf{par}", writes=[("xf_c", par)])
            B.dma("act", arb[par][:, :],
                  attn_sp[128 * c:128 * (c + 1), 512 * q:512 * (q + 1)],
                  grp=f"ar{par}", reads=[("attn_spl", c, q)], writes=[("attn_r", par)])
            B.dve(lambda v, par=par: v.tensor_mul(tt1[:], arb[par][:, :], rb_sb[:]),
                  reads=[("attn_r", par), ("rb_sb",)], writes=[("tt1",)])
            B.dve(lambda v: v.tensor_sub(tt2[:], tt1[:], mb_sb[:]),
                  reads=[("tt1",), ("mb_sb",)], writes=[("tt2",)])
            B.dve(lambda v, par=par: v.tensor_add(x1b[par][:, :], tt2[:],
                                                  xf_c[par][:, :]),
                  reads=[("tt2",), ("xf_c", par)], writes=[("x1f_c", par)])
            B.dve(lambda v, c=c, q=q, par=par: v.tensor_copy(
                x1_bf[:, TOK * c + 512 * q:TOK * c + 512 * (q + 1)],
                x1b[par][:, :]),
                reads=[("x1f_c", par)], writes=[("x1_bf", c, q)])
            B.dma("pool", x1_spill[128 * c:128 * (c + 1), 512 * q:512 * (q + 1)],
                  x1b[par][:, :], grp=f"x1w{par}", reads=[("x1f_c", par)],
                  writes=[("x1_spill", c, q)])

    if TRUNC < 5:
        return finish(nc, B, stack)
    B.dve(lambda v: v.engine_nop(),
          writes=[("xblk", 0, qq) for qq in range(8)] +
                 [("xblk", 1, qq) for qq in range(8)] +
                 [("cqb", 0), ("cqb", 1), ("qsb", 0), ("qsb", 1),
                  ("c_sb", 0), ("c_sb", 1)])
    for c in range(NCH):
        B.dma("pool", enc_sb[:, 40 * c:40 * (c + 1)], encT_d[128 * c:128 * (c + 1), :],
              grp="enc", writes=[("enc_sb",), ("cqb", 0), ("cqb", 1),
                                 ("qsb", 0), ("qsb", 1)])
    sqln = [sq_c, sq_c2]
    for q in range(NQ):
        for c in range(NCH):
            par = c % 2
            B.act(lambda s, c=c, q=q, par=par: s.activation(
                sqln[par][:, :], x1_bf[:, TOK * c + 512 * q:TOK * c + 512 * (q + 1)],
                AF.Square),
                reads=[("x1_bf", c, q)], writes=[("sq_a", par)])
            B.pe(lambda t, c=c, q=q: t.matmul(
                rbank[2][0:1, :], ones_bf[:],
                x1_bf[:, TOK * c + 512 * q:TOK * c + 512 * (q + 1)],
                start=(c == 0), stop=(c == NCH - 1), skip_group_check=True),
                reads=[("x1_bf", c, q), ("ones_bf",)], writes=[RB[2]])
            B.pe(lambda t, c=c, par=par: t.matmul(
                rbank[2][32:33, :], ones_bf[:], sqln[par][:, :],
                start=(c == 0), stop=(c == NCH - 1), tile_position=(0, 32),
                skip_group_check=True),
                reads=[("sq_a", par), ("ones_bf",)], writes=[RB[2]])
        stat_finish(rbank[2], RB[2], 3, 4)
        B.pe(lambda t: t.matmul(rbank[3][:, :], onesr_f32[:], slot(4),
                                start=True, stop=True),
             reads=[("rp", 4), ("onesr_f32",)], writes=[RB[3]])
        B.dve(lambda v, q=q: v.tensor_copy(mb3[:, 512 * q:512 * (q + 1)],
                                           rbank[3][:, :]),
              reads=[RB[3]], writes=[("mb3", q)])
        B.pe(lambda t: t.matmul(rbank[3][:, :], onesr_f32[:], slot(3),
                                start=True, stop=True),
             reads=[("rp", 3), ("onesr_f32",)], writes=[RB[3]])
        B.dve(lambda v, q=q: v.tensor_scalar_mul(rb3[:, 512 * q:512 * (q + 1)],
                                                 rbank[3][:, :], -1.0),
              reads=[RB[3]], writes=[("rb3", q)])

    fcb, sqf = [ffn_c, attn_c2], [sq_c, sq_c2]

    def ffn_stats(ot, q):
        sbk, skey = (rbank[2], RB[2]) if q == 0 else (pctx, PCTX)
        B.pe(lambda t, ot=ot, sbk=sbk, q=q: t.matmul(
            sbk[0:1, :], ones_bf[:], fcb[q][:, :],
            start=(ot == 1), stop=(ot == NCH), skip_group_check=True),
            reads=[("ffn_cq", q), ("ones_bf",)], writes=[skey])
        B.pe(lambda t, ot=ot, sbk=sbk, q=q: t.matmul(
            sbk[32:33, :], ones_bf[:], sqf[q][:, :],
            start=(ot == 1), stop=(ot == NCH), tile_position=(0, 32),
            skip_group_check=True),
            reads=[("sq_f", q), ("ones_bf",)], writes=[skey])
        B.dma("syn", ffn_spill[128 * (ot - 1):128 * ot, 512 * q:512 * (q + 1)],
              fcb[q][:, :], grp=f"fw{q}", reads=[("ffn_cq", q)],
              writes=[("ffn_sp", ot, q)])

    pend = None
    for ot in range(NCH + 1):
        par = ot % 2
        if ot > 0:
            B.dma("pool", wstr[par][:, :], wbT_d[128 * (ot - 1):128 * ot, :],
                  grp=f"wb{par}", writes=[("wstr", par)])
        for q in range(NQ):
            ww = 40 if ot == 0 else 128
            wkey = ("enc_sb",) if ot == 0 else ("wstr", par)
            for c in range(NCH):
                if ot == 0:
                    lt = enc_sb[:, 40 * c:40 * (c + 1)]
                else:
                    lt = wstr[par].rearrange("p (cc o) -> p cc o", o=128)[:, c, :]
                if q == 0:
                    B.pe(lambda t, c=c, lt=lt, ww=ww: t.matmul(
                        paux[0:ww, 0:1], lt, ones_bf[:],
                        start=(c == 0), stop=(c == NCH - 1)),
                        reads=[wkey, ("ones_bf",)], writes=[PAUX])
                B.pe(lambda t, c=c, q=q, lt=lt, ww=ww, par=par: t.matmul(
                    rbank[par][0:ww, :], lt,
                    x1_bf[:, TOK * c + 512 * q:TOK * c + 512 * (q + 1)],
                    start=(c == 0), stop=(c == NCH - 1)),
                    reads=[wkey, ("x1_bf", c, q)], writes=[RB[par]])
            if q == 0:
                B.dve(lambda v, ww=ww: v.tensor_copy(senc_sb[0:ww, :], paux[0:ww, 0:1]),
                      reads=[PAUX], writes=[("senc_sb",)], self_dep=True)
            if ot == 0:
                B.dve(lambda v, q=q, par=par: v.scalar_tensor_tensor(
                    tt1[0:40, :], mb3[0:40, 512 * q:512 * (q + 1)], senc_sb[0:40, :],
                    rbank[par][0:40, :], ALU.mult, ALU.subtract),
                    reads=[("mb3", q), ("senc_sb",), RB[par]], writes=[("tt1",)],
                    self_dep=True)
                B.dve(lambda v, q=q: v.tensor_mul(tt2[0:40, :], tt1[0:40, :],
                                                  rb3[0:40, 512 * q:512 * (q + 1)]),
                      reads=[("tt1",), ("rb3", q)], writes=[("tt2",)])
                B.dve(lambda v, q=q: v.tensor_copy(u_sb[0:40, 512 * q:512 * (q + 1)],
                                                   tt2[0:40, :]),
                      reads=[("tt2",)], writes=[("u_sb", q)])
                if DEBUG:
                    B.dma("pool", u_dbg[:, 512 * q:512 * (q + 1)],
                          u_sb[0:40, 512 * q:512 * (q + 1)],
                          grp="dbg", reads=[("u_sb", q)], writes=[("u_dbg",)])
            else:
                ub = 4 + (1 - par)
                B.pe(lambda t, ot=ot, q=q, ub=ub: t.matmul(
                    rbank[ub][:, :], hcat[0:40, 128 * (ot - 1):128 * ot],
                    u_sb[0:40, 512 * q:512 * (q + 1)], start=True, stop=True),
                    reads=[("u_sb", q), ("hcat",)], writes=[RB[ub]])
                B.dve(lambda v, q=q, par=par: v.scalar_tensor_tensor(
                    tt1[:], mb3[:, 512 * q:512 * (q + 1)], senc_sb[:],
                    rbank[par][:, :], ALU.mult, ALU.subtract),
                    reads=[("mb3", q), ("senc_sb",), RB[par]], writes=[("tt1",)],
                    self_dep=True)
                B.dve(lambda v, q=q: v.tensor_mul(tt2[:], tt1[:],
                                                  rb3[:, 512 * q:512 * (q + 1)]),
                      reads=[("tt1",), ("rb3", q)], writes=[("tt2",)])
                B.dve(lambda v, ot=ot, ub=ub, q=q: v.scalar_tensor_tensor(
                    fcb[q][:, :], tt2[:], bb_sb[:, ot - 1:ot], rbank[ub][:, :],
                    ALU.add, ALU.add),
                    reads=[("tt2",), ("bb_sb",), RB[ub]], writes=[("ffn_cq", q)])
                B.act(lambda s, q=q: s.activation(sqf[q][:, :], fcb[q][:, :],
                                                  AF.Square),
                      reads=[("ffn_cq", q)], writes=[("sq_f", q)])
                if pend is not None:
                    ffn_stats(*pend)
                pend = (ot, q)
    ffn_stats(*pend)

    B.dve(lambda v: v.engine_nop(),
          writes=[("a_bf", og) for og in range(8)])
    st4slots = [(5, 6), (7, 8)]
    for q in range(NQ):
        sbk, skey = (rbank[2], RB[2]) if q == 0 else (pctx, PCTX)
        stat_finish(sbk, skey, st4slots[q][0], st4slots[q][1])

    for q in range(NQ):
        ri, mi = st4slots[q]
        B.pe(lambda t, ri=ri: t.matmul(rbank[3][:, :], onesr_f32[:],
                                       slot(ri), start=True, stop=True),
             reads=[("rp", ri), ("onesr_f32",)], writes=[RB[3]])
        B.dve(lambda v: v.tensor_copy(rb4[:], rbank[3][:, :]),
              reads=[RB[3]], writes=[("rb4", q)])
        B.pe(lambda t, mi=mi: t.matmul(rbank[3][:, :], onesr_f32[:],
                                       slot(mi), start=True, stop=True),
             reads=[("rp", mi), ("onesr_f32",)], writes=[RB[3]])
        B.dve(lambda v: v.tensor_copy(mb4[:], rbank[3][:, :]),
              reads=[RB[3]], writes=[("mb4", q)])
        frb, ocb = [ffn_r, ffn_r2], [out_c, out_c2]
        for c in range(NCH):
            par = c % 2
            B.dma("syn", xf_c[par][:, :],
                  x1_spill[128 * c:128 * (c + 1), 512 * q:512 * (q + 1)],
                  grp=f"xf{par}", reads=[("x1_spill", c, q)], writes=[("xf_c", par)])
            B.dma("act", frb[par][:, :],
                  ffn_spill[128 * c:128 * (c + 1), 512 * q:512 * (q + 1)],
                  grp=f"fr{par}", reads=[("ffn_sp", c + 1, q)], writes=[("ffn_r", par)])
            B.dve(lambda v, par=par: v.tensor_mul(tt1[:], frb[par][:, :], rb4[:]),
                  reads=[("ffn_r", par), ("rb4", q)], writes=[("tt1",)])
            B.dve(lambda v: v.tensor_sub(tt2[:], tt1[:], mb4[:]),
                  reads=[("tt1",), ("mb4", q)], writes=[("tt2",)])
            B.dve(lambda v, par=par: v.tensor_add(ocb[par][:, :], tt2[:],
                                                  xf_c[par][:, :]),
                  reads=[("tt2",), ("xf_c", par)], writes=[("out_c", par)])
            B.dma("pool", out[128 * c:128 * (c + 1), 512 * q:512 * (q + 1)],
                  ocb[par][:, :], grp=f"ow{par}", reads=[("out_c", par)],
                  writes=[("out", c, q)])

    return finish(nc, B, stack)


def finish(nc, B, stack):
    B.finalize()
    sems = {s: stack.enter_context(nc.semaphore(s)) for s in Builder.STREAMS}
    for g in B.grps:
        sems[g] = stack.enter_context(nc.semaphore(g))
    grp_totals = {g: 16 * len(ops) for g, ops in B.grps.items()}
    with nc.Block() as block:
        @block.tensor
        def _(t):
            B.emit_stream("pe", t, sems)

        @block.scalar
        def _(s):
            B.emit_stream("act", s, sems)

        @block.vector
        def _(v):
            B.emit_stream("dve", v, sems)

        @block.gpsimd
        def _(g):
            B.emit_stream("pool", g, sems)

        @block.sync
        def _(sy):
            B.emit_stream("syn", sy, sems)
            for g, tot in grp_totals.items():
                sy.wait_ge(sems[g], tot)

    stack.close()
    return nc


def host_constants(inputs):
    wq = np.asarray(inputs["L0_attn_Wq_d"], np.float32)
    bq = np.asarray(inputs["L0_attn_bq_d"], np.float32)
    wqk = np.asarray(inputs["L0_attn_W_qk"], np.float32)
    bqk = np.asarray(inputs["L0_attn_b_qk"], np.float32)
    wkv = np.asarray(inputs["L0_attn_Wkv_d"], np.float32)
    bkv = np.asarray(inputs["L0_attn_bkv_d"], np.float32)
    wvu = np.asarray(inputs["L0_attn_Wv_u"], np.float32)
    bvu = np.asarray(inputs["L0_attn_bv_u"], np.float32)
    wo = np.ascontiguousarray(np.asarray(inputs["L0_attn_Wo"], np.float32))
    bo = np.asarray(inputs["L0_attn_bo"], np.float32)
    wb = np.asarray(inputs["L0_ffn_W_base"], np.float32)
    bb = np.asarray(inputs["L0_ffn_b_base"], np.float32)
    enc0 = np.asarray(inputs["L0_ffn_enc0"], np.float32)
    enc1 = np.asarray(inputs["L0_ffn_enc1"], np.float32)

    d = {}
    wcat_full = np.concatenate([wq, wkv, np.ones((D, 1), np.float32)], axis=1)
    d["wcat"] = np.ascontiguousarray(
        wcat_full.reshape(32, 128, 17).transpose(1, 0, 2).reshape(128, 544))
    d["b17_d"] = np.concatenate([bq, bkv, np.zeros(1, np.float32)])[:, None].copy()
    wqe = np.zeros((5, 16 * 32), np.float32)
    for h in range(32):
        for l in range(4):
            wqe[l, 16 * h:16 * h + 12] = wqk[:, 4 * h + l]
            wqe[l, 16 * h + 12] = bqk[4 * h + l]
        wqe[4, 16 * h + 13] = -2.0
    d["wqk_ext"] = wqe
    d["wqk_aug"] = np.ascontiguousarray(np.concatenate([wqk, bqk[None, :]], axis=0))
    e4 = np.zeros((128, 32), np.float32)
    e4t = np.zeros((32, 128), np.float32)
    for h in range(32):
        for l in range(4):
            e4[4 * h + l, h] = 1.0
            e4t[h, 4 * h + l] = 1.0
    d["e4_d"] = e4
    d["e4t_d"] = e4t
    d["ones_dr"] = np.ones((1, NSLOT * TOK), np.float32)
    d["ones_col"] = np.ones((128, 1), np.float32)
    d["ident_d"] = np.eye(16, dtype=np.float32)
    wva = np.zeros((128, D), np.float32)
    for c in range(32):
        wva[:, 132 * c:132 * c + 4] = wvu[:, 128 * c:128 * (c + 1)].T
    d["wvall_d"] = wva
    d["bvu_d"] = np.ascontiguousarray(bvu.reshape(32, 128).T)
    d["wo_d"] = np.ascontiguousarray(
        wo.reshape(8, 4, 128, 8, 512).transpose(0, 2, 3, 1, 4).reshape(1024, 16384))
    d["bo_d"] = np.ascontiguousarray(bo.reshape(32, 128).T)
    d["wbT_d"] = np.ascontiguousarray(
        wb.reshape(32, 128, 32, 128).transpose(0, 3, 2, 1).reshape(4096, 4096))
    d["encT_d"] = np.ascontiguousarray(np.concatenate([enc0.T, enc1.T], axis=1))
    d["bb_d"] = np.ascontiguousarray(bb.reshape(32, 128).T)
    d["co0"] = np.ascontiguousarray(np.asarray(inputs["L0_ffn_coefs0"], np.float32))
    d["co1"] = np.ascontiguousarray(np.asarray(inputs["L0_ffn_coefs1"], np.float32))
    d["mk0"] = np.ascontiguousarray(np.asarray(inputs["L0_ffn_mask0"]).astype(np.uint8))
    d["mk1"] = np.ascontiguousarray(np.asarray(inputs["L0_ffn_mask1"]).astype(np.uint8))

    n = D
    j = np.arange(n, dtype=np.float64)
    M = np.empty((n, n), np.float32)
    M[0] = np.float32(1.0 / math.sqrt(n))
    f = np.arange(1, n // 2, dtype=np.float64)
    ang = (2.0 * np.pi / n) * np.outer(f, j)
    M[1:n - 1:2] = (math.sqrt(2.0 / n) * np.cos(ang)).astype(np.float32)
    M[2:n - 1:2] = (math.sqrt(2.0 / n) * np.sin(ang)).astype(np.float32)
    M[n - 1] = (((-1.0) ** j) / math.sqrt(n)).astype(np.float32)
    d["mmat"] = np.ascontiguousarray(
        M.astype(ml_dtypes.bfloat16)
        .reshape(8, 4, 128, 8, 512).transpose(0, 2, 3, 1, 4).reshape(1024, 16384))
    return d


def kernel(**inputs):
    x = np.asarray(inputs["L0_attn_x"], np.float32)
    const = host_constants(inputs)
    nc = build_program()
    in_maps = []
    for core in range(8):
        b, r = core // 4, core % 4
        order = np.concatenate([np.arange(r * TOK, (r + 1) * TOK),
                                np.arange(0, r * TOK),
                                np.arange((r + 1) * TOK, SB)])
        xTc = x[b][order].T
        m = dict(const)
        m["xT"] = np.ascontiguousarray(
            xTc.reshape(8, 4, 128, 8, 512).transpose(0, 2, 3, 1, 4)
            .reshape(1024, 16384))
        in_maps.append(m)
    res = run_bass_kernel_spmd(nc, in_maps, core_ids=list(range(8)))
    outf = np.empty((2, SB, D), np.float32)
    for core in range(8):
        b, r = core // 4, core % 4
        outf[b, r * TOK:(r + 1) * TOK, :] = res.results[core]["out"].T
    return outf
